# version 03
# baseline (speedup 1.0000x reference)
import numpy as np
import ml_dtypes
from contextlib import ExitStack

import concourse.bass as bass
import concourse.mybir as mybir
from concourse.bass_utils import run_bass_kernel_spmd

BF16 = ml_dtypes.bfloat16
F32 = mybir.dt.float32
F16 = mybir.dt.float16
BF = mybir.dt.bfloat16
AF = mybir.ActivationFunctionType
ALU = mybir.AluOpType

ENGINES = ("sync", "gpsimd", "scalar", "vector", "tensor")
ARENA_NOFREE = [False]
DMA_ENGINES = ("sync", "gpsimd", "scalar", "vector")


class Sched:

    MAXOUT = 1
    NVQ = 4

    def __init__(self, nc, ctx):
        self.nc = nc
        self.keys = list(ENGINES) + [f"{e}_d{i}" for e in DMA_ENGINES
                                     for i in range(self.NVQ)]
        self.sem = {k: ctx.enter_context(nc.semaphore("S_" + k)) for k in self.keys}
        self.cnt = {k: 0 for k in self.keys}
        self.vq = {e: 0 for e in DMA_ENGINES}
        self.ops = {e: [] for e in ENGINES}

    def op(self, eng, emit, after=(), inc=1, fafter=()):
        waits = []
        for t in fafter:
            if t is None:
                continue
            k, c = t
            if self.isdma(k):
                c = self.cnt[k]
            waits.append((k, c))
        for t in after:
            if t is None:
                continue
            k, c = t
            if k == eng == "tensor":
                continue
            if self.isdma(k):
                c = self.cnt[k]
            waits.append((k, c))
        self.cnt[eng] += inc
        self.ops[eng].append((emit, eng, inc, waits))
        return (eng, self.cnt[eng])

    def dma(self, eng, out, in_, after=(), **kw):
        key = f"{eng}_d{self.vq[eng]}"
        self.vq[eng] = (self.vq[eng] + 1) % self.NVQ
        waits = []
        for t in after:
            if t is None:
                continue
            k, c = t
            if self.isdma(k) and k != key:
                c = self.cnt[k]
            waits.append((k, c))
        waits = [(k, c) for (k, c) in waits if k != eng and k != key]
        n = self.cnt[key] // 16 + 1
        if n > self.MAXOUT:
            waits.append((key, 16 * (n - self.MAXOUT)))
        self.cnt[key] += 16

        def emit(e, out=out, in_=in_, kw=kw):
            return e.dma_start(out=out, in_=in_, **kw)

        self.ops[eng].append((emit, key, 16, waits))
        return (key, self.cnt[key])

    @staticmethod
    def isdma(k):
        return "_d" in k

    def wait(self, eng, tok):
        k, c = tok
        self.ops[eng].append((None, eng, 0, [(k, c)]))

    def barrier(self):
        snap = [(k, self.cnt[k]) for k in self.keys if self.cnt[k] > 0]
        for e in ENGINES:
            waits = [(k, c) for (k, c) in snap if k != e and not k.startswith(e + "_d")]
            self.ops[e].append((None, e, 0, waits))

    def build(self, block):
        for e in ENGINES:
            ops = self.ops[e]

            def body(eng, ops=ops):
                wm = {k: 0 for k in self.keys}
                for (emit, key, inc, waits) in ops:
                    for (wk, wc) in waits:
                        if wc > wm[wk]:
                            eng.wait_ge(self.sem[wk], wc)
                            wm[wk] = wc
                    if emit is not None:
                        inst = emit(eng)
                        if inc:
                            inst.then_inc(self.sem[key], inc)
            getattr(block, e)(body)


class Arena:

    def __init__(self, nc, nbytes):
        self.nc = nc
        nb = nbytes * (8 if ARENA_NOFREE[0] else 1)
        self.h = nc.alloc_sbuf_tensor("ARENA", [128, nb], mybir.dt.uint8)
        self.base = nc.lookup_mloc(self.h).addr
        self.nbytes = nb
        self.flist = [(0, nb)]
        self.sizes = {}
        self.n = 0
        self.hwm = 0

    def _take(self, per):
        for i, (off, sz) in enumerate(self.flist):
            if sz >= per:
                if sz == per:
                    self.flist.pop(i)
                else:
                    self.flist[i] = (off + per, sz - per)
                self.hwm = max(self.hwm, off + per)
                return off
        raise AssertionError(
            f"SBUF arena overflow: need {per}B, free={self.flist}")

    def buf(self, shape, dtype, name=None):
        per = int(np.prod(shape[1:])) * mybir.dt.size(dtype)
        per = (per + 31) // 32 * 32
        off = self._take(per)
        self.n += 1
        t = self.nc.alloc_sbuf_tensor_at(
            name or f"ab{self.n}", shape, dtype, offset=self.base + off)
        self.sizes[t.name] = (off, per)
        return t

    def bufs(self, n, shape, dtype, name=None):
        return [self.buf(shape, dtype, f"{name}{i}" if name else None)
                for i in range(n)]

    def free(self, *handles):
        if ARENA_NOFREE[0]:
            return
        for h in handles:
            for hh in (h if isinstance(h, (list, tuple)) else [h]):
                off, per = self.sizes.pop(hh.name)
                self.flist.append((off, per))
        self.flist.sort()
        merged = []
        for off, sz in self.flist:
            if merged and merged[-1][0] + merged[-1][1] == off:
                merged[-1] = (merged[-1][0], merged[-1][1] + sz)
            else:
                merged.append((off, sz))
        self.flist = [tuple(x) for x in merged]


class Track:

    def __init__(self):
        self.w = None
        self.r = []

    def write(self, tok):
        self.w = tok
        self.r = []

    def read(self, tok):
        self.r.append(tok)

    @property
    def war(self):
        return list(self.r) + ([self.w] if self.w else [])

    @property
    def raw(self):
        return [self.w] if self.w else []


def np2bir(a):
    d = str(a.dtype)
    return {"float32": F32, "float16": F16, "bfloat16": BF}[d]


def hadamard(n):
    H = np.array([[1.0]], dtype=np.float32)
    while H.shape[0] < n:
        H = np.block([[H, H], [H, -H]])
    return H.astype(np.float32)


def fwht_pairs(nc_, stage):
    s = 1 << stage
    return [(c, c + s) for c in range(nc_) if (c & s) == 0]



def build_A(debug=False):
    nc = bass.Bass("TRN2", target_bir_lowering=False)
    ctx = ExitStack()

    def din(name, shape, dt):
        return nc.dram_tensor(name, shape, dt, kind="ExternalInput").ap()

    x = din("x", [2048, 2048], F16)
    wq = din("wq", [2048, 2048], F16)
    wk = din("wk", [2048, 512], F16)
    wv = din("wv", [2048, 512], F16)
    wo = din("wo", [2048, 2048], F16)
    wupT = din("wupT", [2048, 8192], F32)
    wgT = din("wgT", [2048, 8192], F32)
    wdT = din("wdT", [8192, 2048], F32)
    su_ug = din("su_ug", [128, 16], F32)
    su_dp = din("su_dp", [128, 64], F32)
    wsc_u = din("wsc_u", [128, 64], F32)
    wsc_g = din("wsc_g", [128, 64], F32)
    wsc_dp = din("wsc_dp", [128, 16], F32)
    h128 = din("h128", [128, 128], BF)
    ones_c16 = din("ones_c16", [128, 1], F16)
    ones_r32 = din("ones_r32", [1, 128], F32)
    ones64 = din("ones64", [1, 64], F16)
    cpool = din("cpool", [128, 8], F32)
    x1out = nc.dram_tensor("x1out", [2048, 1024], F32, kind="ExternalOutput").ap()
    had = nc.dram_tensor("had", [64, 128, 1024], BF,
                         kind="ExternalOutput" if debug else "Internal").ap()
    if debug:
        s0out = nc.dram_tensor("s0out", [2048, 1024], F16, kind="ExternalOutput").ap()
        zout = nc.dram_tensor("zout", [16, 128, 1024], BF, kind="ExternalOutput").ap()
        xnout = nc.dram_tensor("xnout", [128, 2048], F16, kind="ExternalOutput").ap()
        h2out = nc.dram_tensor("h2out", [64, 128, 1024], BF, kind="ExternalOutput").ap()

    A = Arena(nc, 202 * 1024)
    S = Sched(nc, ctx)

    ps = {k: ctx.enter_context(nc.psum_tensor("ps" + k, [128, 512], F32))
          for k in "ABCDE"}
    pt = {k: Track() for k in ps}

    def do(eng, emit, reads=(), writes=(), inc=1, sreads=()):
        after = []
        fafter = []
        for r in reads:
            after += r.raw
        for r in sreads:
            fafter += r.raw
        for w in writes:
            after += w.war
        tok = S.op(eng, emit, after=after, inc=inc, fafter=fafter)
        for r in reads:
            r.read(tok)
        for r in sreads:
            r.read(tok)
        for w in writes:
            w.write(tok)
        return tok

    def dodma(eng, out_ap, in_ap, reads=(), writes=(), **kw):
        after = []
        for r in reads:
            after += r.raw
        for w in writes:
            after += w.war
        tok = S.dma(eng, out_ap, in_ap, after=after, **kw)
        for r in reads:
            r.read(tok)
        for w in writes:
            w.write(tok)
        return tok

    def sbuf(shape, dt, name=None):
        return A.buf(shape, dt, name), Track()

    def sbufs(n, shape, dt, name):
        hs = A.bufs(n, shape, dt, name)
        return hs, [Track() for _ in hs]

    cp_t, cp_k = sbuf([128, 8], F32, "cp_t")
    dodma("sync", cp_t[:], cpool, writes=[cp_k])

    def cp(j, P=128):
        return cp_t[0:P, j:j + 1]
    su_ug_t, su_ug_k = sbuf([128, 16], F32, "su_ug_t")
    su_dp_t, su_dp_k = sbuf([128, 64], F32, "su_dp_t")
    wsc_u_t, wsc_u_k = sbuf([128, 64], F32, "wsc_u_t")
    wsc_g_t, wsc_g_k = sbuf([128, 64], F32, "wsc_g_t")
    wsc_dp_t, wsc_dp_k = sbuf([128, 16], F32, "wsc_dp_t")
    colB_t, colB_k = sbuf([128, 64], F32, "colB")
    colG_t, colG_k = sbuf([128, 64], F32, "colG")
    colD_t, colD_k = sbuf([128, 16], F32, "colD")
    h128_t, h128_k = sbuf([128, 128], BF, "h128")
    oc16_t, oc16_k = sbuf([128, 1], F16, "oc16")
    or32_t, or32_k = sbuf([1, 128], F32, "or32")
    o64_t, o64_k = sbuf([1, 64], F16, "o64")
    for (dst, k_, src) in [(su_ug_t, su_ug_k, su_ug), (su_dp_t, su_dp_k, su_dp),
                           (wsc_u_t, wsc_u_k, wsc_u), (wsc_g_t, wsc_g_k, wsc_g),
                           (wsc_dp_t, wsc_dp_k, wsc_dp), (h128_t, h128_k, h128),
                           (oc16_t, oc16_k, ones_c16), (or32_t, or32_k, ones_r32),
                           (o64_t, o64_k, ones64)]:
        dodma("sync", dst[:], src, writes=[k_])
    K1 = float(2048.0 ** -0.5)
    K2 = float(8192.0 ** -0.5)
    do("vector", lambda e: e.scalar_tensor_tensor(
        colB_t[:], wsc_u_t[:], K1, su_dp_t[:], ALU.mult, ALU.mult),
        reads=[wsc_u_k, su_dp_k], writes=[colB_k])
    do("vector", lambda e: e.tensor_scalar_mul(colG_t[:], wsc_g_t[:], K1),
        reads=[wsc_g_k], writes=[colG_k])
    do("vector", lambda e: e.tensor_scalar_mul(colD_t[:], wsc_dp_t[:], K2),
        reads=[wsc_dp_k], writes=[colD_k])

    sum_row, sum_k = sbuf([1, 2048], F32, "sum_row")
    ss_row, ss_k = sbuf([1, 2048], F32, "ss_row")
    r_row, r_k = sbuf([1, 2048], F32, "r_row")
    M_bc, M_k = sbuf([128, 2048], F32, "M_bc")
    R_bc, R_k = sbuf([128, 2048], F32, "R_bc")
    xb, xb_k = sbufs(3, [128, 512], F16, "xb")
    sq, sq_k = sbufs(2, [128, 512], F16, "sq")
    for nt in range(4):
        for fc in range(16):
            r3 = fc % 3
            r2 = fc % 2
            dodma("sync", xb[r3][:],
                  x[nt * 512:(nt + 1) * 512, fc * 128:(fc + 1) * 128],
                  writes=[xb_k[r3]], transpose=True)
            do("scalar", lambda e, a=sq[r2], b=xb[r3]: e.activation(
                a[:], b[:], AF.Square),
                reads=[xb_k[r3]], writes=[sq_k[r2]])
            do("tensor", lambda e, b=xb[r3], st=(fc == 0), sp=(fc == 15): e.matmul(
                ps["A"][0:1, :], oc16_t[:], b[:], start=st, stop=sp),
                reads=[xb_k[r3], oc16_k], writes=[pt["A"]])
            do("tensor", lambda e, b=sq[r2], st=(fc == 0), sp=(fc == 15): e.matmul(
                ps["B"][0:1, :], oc16_t[:], b[:], start=st, stop=sp),
                reads=[sq_k[r2], oc16_k], writes=[pt["B"]])
        do("vector", lambda e, nt=nt: e.tensor_copy(
            sum_row[0:1, nt * 512:(nt + 1) * 512], ps["A"][0:1, :]),
            reads=[pt["A"]], writes=[sum_k])
        do("vector", lambda e, nt=nt: e.tensor_copy(
            ss_row[0:1, nt * 512:(nt + 1) * 512], ps["B"][0:1, :]),
            reads=[pt["B"]], writes=[ss_k])
    m_row, m_k = sbuf([1, 2048], F32, "m_row")
    v_row, v_k = sbuf([1, 2048], F32, "v_row")
    do("vector", lambda e: e.tensor_scalar_mul(m_row[:], sum_row[:], 1.0 / 2048),
        reads=[sum_k], writes=[m_k])
    do("vector", lambda e: e.tensor_scalar_mul(v_row[:], ss_row[:], 1.0 / 2048),
        reads=[ss_k], writes=[v_k])
    do("vector", lambda e: e.scalar_tensor_tensor(
        r_row[:], m_row[:], -1.0, m_row[:], ALU.mult, ALU.mult),
        reads=[m_k], writes=[r_k])
    do("vector", lambda e: e.tensor_add(ss_row[:], v_row[:], r_row[:]),
        reads=[v_k, r_k], writes=[ss_k])
    do("scalar", lambda e: e.activation(v_row[:], ss_row[:], AF.Sqrt, bias=cp(0, 1)),
        reads=[ss_k, cp_k], writes=[v_k])
    do("vector", lambda e: e.reciprocal(r_row[:], v_row[:]),
        reads=[v_k], writes=[r_k])
    for nt in range(4):
        do("tensor", lambda e, nt=nt: e.matmul(
            ps["C"][:, :], or32_t[:], m_row[0:1, nt * 512:(nt + 1) * 512],
            start=True, stop=True),
            reads=[or32_k, m_k], writes=[pt["C"]])
        do("vector", lambda e, nt=nt: e.tensor_copy(
            M_bc[:, nt * 512:(nt + 1) * 512], ps["C"][:, :]),
            reads=[pt["C"]], writes=[M_k])
        do("tensor", lambda e, nt=nt: e.matmul(
            ps["D"][:, :], or32_t[:], r_row[0:1, nt * 512:(nt + 1) * 512],
            start=True, stop=True),
            reads=[or32_k, r_k], writes=[pt["D"]])
        do("vector", lambda e, nt=nt: e.tensor_copy(
            R_bc[:, nt * 512:(nt + 1) * 512], ps["D"][:, :]),
            reads=[pt["D"]], writes=[R_k])
    S.barrier()
    A.free(xb, sq, sum_row, ss_row, r_row, m_row, v_row)

    xn, xn_k = sbufs(16, [128, 2048], F16, "xn")
    zp, zp_k = sbufs(18, [128, 1024], BF, "z")
    xt2, xt2_k = sbufs(2, [128, 2048], F16, "xt2")
    dtmp, dtmp_k = sbufs(2, [128, 2048], F16, "dtmp")
    zmap = list(range(16))
    spares = [16, 17]
    for fc in range(16):
        r = fc % 2
        dodma("sync", xt2[r][:], x[:, fc * 128:(fc + 1) * 128],
              writes=[xt2_k[r]], transpose=True)
        do("vector", lambda e, fc=fc, r=r: e.tensor_tensor(
            dtmp[r][:], xt2[r][:], M_bc[:], ALU.subtract),
            reads=[xt2_k[r], M_k], writes=[dtmp_k[r]])
        do("vector", lambda e, fc=fc, r=r: e.tensor_tensor(
            xn[fc][:], dtmp[r][:], R_bc[:], ALU.mult),
            reads=[dtmp_k[r], R_k], writes=[xn_k[fc]])
        do("vector", lambda e, fc=fc: e.tensor_scalar_mul(
            zp[fc][:], xn[fc][:, 0:1024], su_ug_t[:, fc:fc + 1]),
            reads=[xn_k[fc]], sreads=[su_ug_k], writes=[zp_k[fc]])
    for fc in range(16):
        src = zmap[fc]
        dst = spares[0]
        for n2 in range(2):
            b = "A" if n2 == 0 else "B"
            do("tensor", lambda e, s=src, n2=n2, b=b: e.matmul(
                ps[b][:, :], h128_t[:], zp[s][:, n2 * 512:(n2 + 1) * 512],
                start=True, stop=True),
                reads=[zp_k[src], h128_k], writes=[pt[b]])
            do("vector", lambda e, d=dst, n2=n2, b=b: e.tensor_copy(
                zp[d][:, n2 * 512:(n2 + 1) * 512], ps[b][:, :]),
                reads=[pt[b]], writes=[zp_k[dst]])
        zmap[fc] = dst
        spares[0] = src
    for st_ in range(4):
        for (a, b) in fwht_pairs(16, st_):
            pa, pb = zmap[a], zmap[b]
            na, nb = spares
            do("vector", lambda e, pa=pa, pb=pb, na=na: e.tensor_add(
                zp[na][:], zp[pa][:], zp[pb][:]),
                reads=[zp_k[pa], zp_k[pb]], writes=[zp_k[na]])
            do("vector", lambda e, pa=pa, pb=pb, nb=nb: e.tensor_tensor(
                zp[nb][:], zp[pa][:], zp[pb][:], ALU.subtract),
                reads=[zp_k[pa], zp_k[pb]], writes=[zp_k[nb]])
            zmap[a], zmap[b] = na, nb
            spares = [pa, pb]
    if debug:
        for fc in range(16):
            dodma("gpsimd", zout[fc], zp[zmap[fc]][:], reads=[zp_k[zmap[fc]]])
        dodma("gpsimd", xnout, xn[0][:], reads=[xn_k[0]])
    S.barrier()
    A.free(xt2, dtmp, M_bc, R_bc)

    wv_sb, wv_k = sbufs(16, [128, 512], F16, "wv_sb")
    for fc in range(16):
        dodma("sync", wv_sb[fc][:], wv[fc * 128:(fc + 1) * 128, :],
              writes=[wv_k[fc]])
    v_tm, v_k = sbufs(16, [128, 520], F16, "v_tm")
    for tt in range(16):
        for fc in range(16):
            do("tensor", lambda e, tt=tt, fc=fc: e.matmul(
                ps["A"][:, :], xn[fc][:, tt * 128:(tt + 1) * 128], wv_sb[fc][:],
                start=(fc == 0), stop=(fc == 15)),
                reads=[xn_k[fc], wv_k[fc]], writes=[pt["A"]])
        do("vector", lambda e, tt=tt: e.tensor_copy(
            v_tm[tt][:].rearrange("p (g c) -> p g c", c=65)[:, :, 0:64],
            ps["A"][:, :].rearrange("p (g c) -> p g c", c=64)),
            reads=[pt["A"]], writes=[v_k[tt]])
        do("vector", lambda e, tt=tt: e.memset(
            v_tm[tt][:].rearrange("p (g c) -> p g c", c=65)[:, :, 64:65], 1.0),
            reads=[], writes=[v_k[tt]])
    S.barrier()
    A.free(wv_sb)
    wk_sb, wk_k = sbufs(16, [128, 512], F16, "wk_sb")
    for fc in range(16):
        dodma("sync", wk_sb[fc][:], wk[fc * 128:(fc + 1) * 128, :],
              writes=[wk_k[fc]])
    k_fm, k_k = sbufs(4, [128, 2048], F16, "k_fm")
    for oc in range(4):
        for n4 in range(4):
            for fc in range(16):
                do("tensor", lambda e, oc=oc, n4=n4, fc=fc: e.matmul(
                    ps["B"][:, :], wk_sb[fc][:, oc * 128:(oc + 1) * 128],
                    xn[fc][:, n4 * 512:(n4 + 1) * 512],
                    start=(fc == 0), stop=(fc == 15)),
                    reads=[xn_k[fc], wk_k[fc]], writes=[pt["B"]])
            do("vector", lambda e, oc=oc, n4=n4: e.tensor_copy(
                k_fm[oc][:, n4 * 512:(n4 + 1) * 512], ps["B"][:, :]),
                reads=[pt["B"]], writes=[k_k[oc]])
    k_dup, kd_k = sbufs(8, [128, 2048], F16, "k_dup")
    for g in range(8):
        kc, kr = g // 2, (g % 2) * 64
        for half in range(2):
            do("vector", lambda e, g=g, kc=kc, kr=kr, half=half: e.tensor_copy(
                k_dup[g][half * 64:half * 64 + 64, :], k_fm[kc][kr:kr + 64, :]),
                reads=[k_k[kc]], writes=[kd_k[g]])
    S.barrier()
    A.free(wk_sb, k_fm)
    q_fm, q_k = sbufs(16, [128, 1024], F16, "q_fm")
    wq_t, wq_k = sbufs(2, [128, 2048], F16, "wq_t")
    for oc in range(16):
        r = oc % 2
        dodma("sync", wq_t[r][:].rearrange("p (c m) -> p c m", m=128),
              wq[:, oc * 128:(oc + 1) * 128].rearrange("(c p) m -> p c m", p=128),
              writes=[wq_k[r]])
        for n2 in range(2):
            for fc in range(16):
                do("tensor", lambda e, r=r, fc=fc, n2=n2: e.matmul(
                    ps["C"][:, :], wq_t[r][:, fc * 128:(fc + 1) * 128],
                    xn[fc][:, n2 * 512:(n2 + 1) * 512],
                    start=(fc == 0), stop=(fc == 15)),
                    reads=[xn_k[fc], wq_k[r]], writes=[pt["C"]])
            do("vector", lambda e, oc=oc, n2=n2: e.tensor_copy(
                q_fm[oc][:, n2 * 512:(n2 + 1) * 512], ps["C"][:, :]),
                reads=[pt["C"]], writes=[q_k[oc]])
    S.barrier()
    A.free(xn, wq_t)

    o_fm, o_k = sbufs(16, [128, 1024], F16, "o_fm")
    eb, eb_k = sbufs(3, [128, 512], F16, "eb")
    rd_t, rd_k = sbuf([1, 512], F16, "rd")
    rb_t, rb_k = sbuf([64, 512], F16, "rb")
    for h in range(32):
        g = h // 4
        qc, qr = h // 2, (h % 2) * 64
        for tq2 in range(2):
            for tk in range(16):
                b = "A" if tk % 2 == 0 else "B"
                r3 = tk % 3
                do("tensor", lambda e, b=b, g=g, qc=qc, qr=qr,
                   tk=tk, tq2=tq2: e.matmul(
                       ps[b][:, :],
                       k_dup[g][qr:qr + 64, tk * 128:(tk + 1) * 128],
                       q_fm[qc][qr:qr + 64, tq2 * 512:(tq2 + 1) * 512],
                       start=True, stop=True),
                    reads=[kd_k[g], q_k[qc]], writes=[pt[b]])
                do("scalar", lambda e, b=b, r3=r3: e.activation(
                    eb[r3][:], ps[b][:, :], AF.Exp, bias=cp(2), scale=cp(1)),
                    reads=[pt[b], cp_k], writes=[eb_k[r3]])
                do("tensor", lambda e, r3=r3, tk=tk, g=g: e.matmul(
                    ps["D"][0:65, :], v_tm[tk][:, g * 65:(g + 1) * 65],
                    eb[r3][:], start=(tk == 0), stop=(tk == 15)),
                    reads=[eb_k[r3], v_k[tk]], writes=[pt["D"]])
            do("vector", lambda e: e.reciprocal(rd_t[:], ps["D"][64:65, :]),
                reads=[pt["D"]], writes=[rd_k])
            do("tensor", lambda e: e.matmul(
                ps["E"][0:64, :], o64_t[:], rd_t[:], start=True, stop=True),
                reads=[rd_k, o64_k], writes=[pt["E"]])
            do("vector", lambda e: e.tensor_copy(rb_t[:], ps["E"][0:64, :]),
                reads=[pt["E"]], writes=[rb_k])
            do("vector", lambda e, qc=qc, qr=qr, tq2=tq2: e.tensor_tensor(
                o_fm[qc][qr:qr + 64, tq2 * 512:(tq2 + 1) * 512],
                ps["D"][0:64, :], rb_t[:], ALU.mult),
                reads=[pt["D"], rb_k], writes=[o_k[qc]])
    S.barrier()
    A.free(q_fm, k_dup, v_tm, eb, rd_t, rb_t)

    wo_sb, wo_k = sbufs(16, [128, 2048], F16, "wo_sb")
    for ic in range(16):
        dodma("sync", wo_sb[ic][:], wo[ic * 128:(ic + 1) * 128, :],
              writes=[wo_k[ic]])
    s0_fm, s0_k = sbufs(16, [128, 1024], F16, "s0_fm")
    for oc in range(16):
        for n2 in range(2):
            for ic in range(16):
                do("tensor", lambda e, oc=oc, n2=n2, ic=ic: e.matmul(
                    ps["A"][:, :], wo_sb[ic][:, oc * 128:(oc + 1) * 128],
                    o_fm[ic][:, n2 * 512:(n2 + 1) * 512],
                    start=(ic == 0), stop=(ic == 15)),
                    reads=[o_k[ic], wo_k[ic]], writes=[pt["A"]])
            do("vector", lambda e, oc=oc, n2=n2: e.tensor_copy(
                s0_fm[oc][:, n2 * 512:(n2 + 1) * 512], ps["A"][:, :]),
                reads=[pt["A"]], writes=[s0_k[oc]])
    if debug:
        for oc in range(16):
            dodma("gpsimd", s0out[oc * 128:(oc + 1) * 128, :], s0_fm[oc][:],
                  reads=[s0_k[oc]])
    S.barrier()
    A.free(wo_sb, o_fm)

    wu_t, wu_k = sbufs(2, [128, 2048], BF, "wu_t")
    wg_t, wg_k = sbufs(2, [128, 2048], BF, "wg_t")
    sg_b, sg_k = sbufs(2, [128, 1024], BF, "sg_b")
    hh_b, hh_k = sbufs(2, [128, 1024], BF, "hh_b")
    ha_b, ha_k = sbufs(2, [128, 1024], BF, "ha_b")
    for oc in range(64):
        r = oc % 2
        dodma("gpsimd", wu_t[r][:].rearrange("p (c m) -> p c m", m=128),
              wupT[:, oc * 128:(oc + 1) * 128].rearrange("(c p) m -> p c m", p=128),
              writes=[wu_k[r]])
        dodma("gpsimd", wg_t[r][:].rearrange("p (c m) -> p c m", m=128),
              wgT[:, oc * 128:(oc + 1) * 128].rearrange("(c p) m -> p c m", p=128),
              writes=[wg_k[r]])
        for n2 in range(2):
            for ic in range(16):
                zt = zmap[ic]
                do("tensor", lambda e, r=r, ic=ic, n2=n2, zt=zt: e.matmul(
                    ps["A"][:, :], wu_t[r][:, ic * 128:(ic + 1) * 128],
                    zp[zt][:, n2 * 512:(n2 + 1) * 512],
                    start=(ic == 0), stop=(ic == 15)),
                    reads=[zp_k[zt], wu_k[r]], writes=[pt["A"]])
                do("tensor", lambda e, r=r, ic=ic, n2=n2, zt=zt: e.matmul(
                    ps["B"][:, :], wg_t[r][:, ic * 128:(ic + 1) * 128],
                    zp[zt][:, n2 * 512:(n2 + 1) * 512],
                    start=(ic == 0), stop=(ic == 15)),
                    reads=[zp_k[zt], wg_k[r]], writes=[pt["B"]])
            do("scalar", lambda e, r=r, oc=oc, n2=n2: e.activation(
                sg_b[r][:, n2 * 512:(n2 + 1) * 512], ps["B"][:, :], AF.Silu,
                scale=colG_t[:, oc:oc + 1]),
                reads=[pt["B"], colG_k], writes=[sg_k[r]])
            do("vector", lambda e, r=r, oc=oc, n2=n2: e.scalar_tensor_tensor(
                hh_b[r][:, n2 * 512:(n2 + 1) * 512], ps["A"][:, :],
                colB_t[:, oc:oc + 1], sg_b[r][:, n2 * 512:(n2 + 1) * 512],
                ALU.mult, ALU.mult),
                reads=[pt["A"], sg_k[r]], sreads=[colB_k], writes=[hh_k[r]])
            do("tensor", lambda e, r=r, n2=n2: e.matmul(
                ps["C"][:, :], h128_t[:],
                hh_b[r][:, n2 * 512:(n2 + 1) * 512], start=True, stop=True),
                reads=[hh_k[r], h128_k], writes=[pt["C"]])
            do("vector", lambda e, r=r, n2=n2: e.tensor_copy(
                ha_b[r][:, n2 * 512:(n2 + 1) * 512], ps["C"][:, :]),
                reads=[pt["C"]], writes=[ha_k[r]])
        dodma("sync", had[oc], ha_b[r][:], reads=[ha_k[r]])
    S.barrier()
    A.free(zp, wu_t, wg_t, sg_b, hh_b, ha_b)

    h2, h2_k = sbufs(66, [128, 1024], BF, "h2")
    h2map = list(range(64))
    h2sp = [64, 65]
    for ic in range(64):
        dodma("sync", h2[ic][:], had[ic], writes=[h2_k[ic]])
    for st_ in range(6):
        for (a, b) in fwht_pairs(64, st_):
            pa, pb = h2map[a], h2map[b]
            na, nb = h2sp
            do("vector", lambda e, pa=pa, pb=pb, na=na: e.tensor_add(
                h2[na][:], h2[pa][:], h2[pb][:]),
                reads=[h2_k[pa], h2_k[pb]], writes=[h2_k[na]])
            do("vector", lambda e, pa=pa, pb=pb, nb=nb: e.tensor_tensor(
                h2[nb][:], h2[pa][:], h2[pb][:], ALU.subtract),
                reads=[h2_k[pa], h2_k[pb]], writes=[h2_k[nb]])
            h2map[a], h2map[b] = na, nb
            h2sp = [pa, pb]
    if debug:
        for c64 in range(64):
            dodma("gpsimd", h2out[c64], h2[h2map[c64]][:],
                  reads=[h2_k[h2map[c64]]])
    wd_t, wd_k = sbuf([128, 8192], BF, "wd_t")
    g0_b, g0_k = sbufs(2, [128, 512], F32, "g0_b")
    xs_b, xs_k = sbufs(2, [128, 512], F16, "xs_b")
    t_b, t_k = sbufs(2, [128, 512], F32, "t_b")
    u_b, u_k = sbufs(2, [128, 512], F32, "u_b")
    for oc in range(16):
        for q4 in range(4):
            dodma("gpsimd",
                  wd_t[:, q4 * 2048:(q4 + 1) * 2048].rearrange(
                      "p (c m) -> p c m", m=128),
                  wdT[q4 * 2048:(q4 + 1) * 2048,
                      oc * 128:(oc + 1) * 128].rearrange(
                      "(c p) m -> p c m", p=128),
                  writes=[wd_k])
        for n2 in range(2):
            b = "A" if n2 == 0 else "B"
            for ic in range(64):
                ht = h2map[ic]
                do("tensor", lambda e, b=b, ic=ic, n2=n2, ht=ht: e.matmul(
                    ps[b][:, :], wd_t[:, ic * 128:(ic + 1) * 128],
                    h2[ht][:, n2 * 512:(n2 + 1) * 512],
                    start=(ic == 0), stop=(ic == 63)),
                    reads=[h2_k[ht], wd_k], writes=[pt[b]])
            do("scalar", lambda e, b=b, oc=oc, n2=n2: e.activation(
                g0_b[n2][:], ps[b][:, :], AF.Silu, scale=colD_t[:, oc:oc + 1]),
                reads=[pt[b], colD_k], writes=[g0_k[n2]])
            dodma("sync", xs_b[n2][:],
                  x[n2 * 512:(n2 + 1) * 512, oc * 128:(oc + 1) * 128],
                  writes=[xs_k[n2]], transpose=True)
            do("vector", lambda e, oc=oc, n2=n2: e.tensor_tensor(
                t_b[n2][:], s0_fm[oc][:, n2 * 512:(n2 + 1) * 512], g0_b[n2][:],
                ALU.mult),
                reads=[s0_k[oc], g0_k[n2]], writes=[t_k[n2]])
            do("vector", lambda e, n2=n2: e.tensor_tensor(
                u_b[n2][:], t_b[n2][:], xs_b[n2][:], ALU.add),
                reads=[t_k[n2], xs_k[n2]], writes=[u_k[n2]])
            dodma("sync", x1out[oc * 128:(oc + 1) * 128,
                                  n2 * 512:(n2 + 1) * 512],
                  u_b[n2][:], reads=[u_k[n2]])
    for _i in range(Sched.NVQ):
        S.wait("gpsimd", (f"gpsimd_d{_i}", S.cnt[f"gpsimd_d{_i}"]))
        S.wait("sync", (f"sync_d{_i}", S.cnt[f"sync_d{_i}"]))
    with nc.allow_low_precision("fp16/bf16 intermediates are deliberate"):
        with nc.Block() as block:
            S.build(block)
    ctx.close()
    return nc


CPOOL = np.zeros((128, 8), np.float32)
CPOOL[:, 0] = 1e-5
CPOOL[:, 1] = 0.125
CPOOL[:, 2] = -4.0
CPOOL[:, 3] = -1.0
CPOOL[:, 4] = 1.0
CPOOL[:, 5] = 1e-6
CPOOL[:, 6] = 1.0 / 256.0
CPOOL[:, 7] = 0.0


def prep_A(d):
    x = d["L0_attn_x"]
    shared = dict(
        wq=np.ascontiguousarray(d["L0_attn_Wq"]),
        wk=np.ascontiguousarray(d["L0_attn_Wk"]),
        wv=np.ascontiguousarray(d["L0_attn_Wv"]),
        wo=np.ascontiguousarray(d["L0_attn_Wo"]),
        wupT=np.ascontiguousarray(d["L0_ffn_W_up"].T),
        wgT=np.ascontiguousarray(d["L0_ffn_W_gate"].T),
        wdT=np.ascontiguousarray(d["L0_ffn_W_down"].T),
        su_ug=np.ascontiguousarray(d["L0_ffn_SU_ug"].reshape(16, 128).T),
        su_dp=np.ascontiguousarray(d["L0_ffn_SU_dp"].reshape(64, 128).T),
        wsc_u=np.ascontiguousarray(
            d["L0_ffn_Wscale_ug"][:8192].reshape(64, 128).T),
        wsc_g=np.ascontiguousarray(
            d["L0_ffn_Wscale_ug"][8192:].reshape(64, 128).T),
        wsc_dp=np.ascontiguousarray(d["L0_ffn_Wscale_dp"].reshape(16, 128).T),
        h128=hadamard(128).astype(BF16),
        ones_c16=np.ones((128, 1), np.float16),
        ones_r32=np.ones((1, 128), np.float32),
        ones64=np.ones((1, 64), np.float16),
        cpool=CPOOL,
    )
    maps = []
    for c in range(8):
        b, half = c // 2, c % 2
        xr = np.ascontiguousarray(
            np.concatenate([x[b, half * 1024:], x[b, :half * 1024]], axis=0))
        m = dict(shared)
        m["x"] = xr
        maps.append(m)
    return maps


def run_A(d):
    nc = build_A()
    res = run_bass_kernel_spmd(nc, prep_A(d), core_ids=list(range(8)))
    x1fm = [np.concatenate([res.results[2 * b]["x1out"],
                            res.results[2 * b + 1]["x1out"]], axis=1)
            for b in range(4)]
    return x1fm



def build_B(debug=False):
    nc = bass.Bass("TRN2", target_bir_lowering=False)
    ctx = ExitStack()

    def din(name, shape, dt):
        return nc.dram_tensor(name, shape, dt, kind="ExternalInput").ap()

    x1 = din("x1", [2048, 2048], F32)
    wqT = din("wqT", [2048, 768], F32)
    wkT = din("wkT", [2048, 768], F32)
    wvT = din("wvT", [2048, 768], F32)
    wgT = din("wgT", [2048, 768], F32)
    wbaT = din("wbaT", [2048, 64], F32)
    cwq = din("cwq", [128, 24], F32)
    cwk = din("cwk", [128, 24], F32)
    cwv = din("cwv", [128, 24], F32)
    alog = din("alog", [3, 1], F32)
    dtb = din("dtb", [3, 1], F32)
    onw = din("onw", [1, 256], F32)
    utm_sc = din("utm_sc", [128, 128], F32)
    uts = din("uts", [128, 128], F32)
    lts = din("lts", [128, 128], F32)
    idn32 = din("idn32", [128, 128], F32)
    idnbf = din("idnbf", [128, 128], BF)
    ones_r32 = din("ones_r32", [1, 128], F32)
    negones_r32 = din("negones_r32", [1, 128], F32)
    ones_cbf = din("ones_cbf", [128, 1], BF)
    cpool = din("cpool", [128, 8], F32)
    oB = nc.dram_tensor("oB", [2048, 768], BF, kind="ExternalOutput").ap()
    rowd = nc.dram_tensor("rowd", [16, 2048], F32,
                          kind="ExternalOutput" if debug else "Internal").ap()
    if debug:
        dxn = nc.dram_tensor("dxn", [128, 2048], BF, kind="ExternalOutput").ap()
        dqn = nc.dram_tensor("dqn", [2, 128, 2048], BF, kind="ExternalOutput").ap()
        dkn = nc.dram_tensor("dkn", [2, 128, 2048], BF, kind="ExternalOutput").ap()
        dvc = nc.dram_tensor("dvc", [2, 128, 2048], BF, kind="ExternalOutput").ap()
        dnt = nc.dram_tensor("dnt", [2, 128, 128], BF, kind="ExternalOutput").ap()
        dpt = nc.dram_tensor("dpt", [2, 128, 128], F32, kind="ExternalOutput").ap()
        dub = nc.dram_tensor("dub", [2, 128, 256], BF, kind="ExternalOutput").ap()
        doa = nc.dram_tensor("doa", [2, 128, 256], F32, kind="ExternalOutput").ap()
        dcol = nc.dram_tensor("dcol", [6, 128, 16], F32, kind="ExternalOutput").ap()
        dktm = nc.dram_tensor("dktm", [2, 128, 256], BF, kind="ExternalOutput").ap()

    A = Arena(nc, 202 * 1024)
    S = Sched(nc, ctx)
    ps = {k: ctx.enter_context(nc.psum_tensor("ps" + k, [128, 512], F32))
          for k in "ABCDE"}
    ps["T1"] = ctx.enter_context(nc.psum_tensor("psT1", [128, 1024], BF))
    ps["T2"] = ctx.enter_context(nc.psum_tensor("psT2", [128, 1024], BF))
    pt = {k: Track() for k in ps}

    def do(eng, emit, reads=(), writes=(), inc=1, sreads=()):
        after = []
        fafter = []
        for r in reads:
            after += r.raw
        for r in sreads:
            fafter += r.raw
        for w in writes:
            after += w.war
        tok = S.op(eng, emit, after=after, inc=inc, fafter=fafter)
        for r in reads:
            r.read(tok)
        for r in sreads:
            r.read(tok)
        for w in writes:
            w.write(tok)
        return tok

    def dodma(eng, out_ap, in_ap, reads=(), writes=(), **kw):
        after = []
        for r in reads:
            after += r.raw
        for w in writes:
            after += w.war
        tok = S.dma(eng, out_ap, in_ap, after=after, **kw)
        for r in reads:
            r.read(tok)
        for w in writes:
            w.write(tok)
        return tok

    def sbuf(shape, dt, name=None):
        return A.buf(shape, dt, name), Track()

    def sbufs(n, shape, dt, name):
        hs = A.bufs(n, shape, dt, name)
        return hs, [Track() for _ in hs]

    cp_t, cp_k = sbuf([128, 8], F32, "cp_t")
    dodma("sync", cp_t[:], cpool, writes=[cp_k])

    def cp(j, P=128):
        return cp_t[0:P, j:j + 1]

    small = {}
    for nm, src, shp, dt in [
            ("cwq", cwq, [128, 24], F32), ("cwk", cwk, [128, 24], F32),
            ("cwv", cwv, [128, 24], F32), ("alog", alog, [3, 1], F32),
            ("dtb", dtb, [3, 1], F32), ("onw", onw, [1, 256], F32),
            ("utm", utm_sc, [128, 128], F32), ("uts", uts, [128, 128], F32),
            ("lts", lts, [128, 128], F32), ("idn32", idn32, [128, 128], F32),
            ("idnbf", idnbf, [128, 128], BF), ("o_r32", ones_r32, [1, 128], F32),
            ("no_r32", negones_r32, [1, 128], F32),
            ("o_cbf", ones_cbf, [128, 1], BF)]:
        t, k_ = sbuf(shp, dt, nm)
        dodma("sync", t[:], src, writes=[k_])
        small[nm] = (t, k_)

    def sm(nm):
        return small[nm][0], small[nm][1]

    x1h, x1h_k = sbufs(16, [128, 2048], F16, "x1h")
    for fc in range(16):
        dodma("gpsimd", x1h[fc][:], x1[fc * 128:(fc + 1) * 128, :],
              writes=[x1h_k[fc]])
    sum_row, sum_k = sbuf([1, 2048], F32, "sum_row")
    ss_row, ss_k = sbuf([1, 2048], F32, "ss_row")
    r_row, r_k = sbuf([1, 2048], F32, "r_row")
    oc16_t, oc16_k = sbuf([128, 1], F16, "oc16")
    do("vector", lambda e: e.memset(oc16_t[:], 1.0), writes=[oc16_k])
    sq, sq_k = sbufs(2, [128, 512], F16, "sq")
    for nt in range(4):
        for fc in range(16):
            r2 = fc % 2
            do("scalar", lambda e, fc=fc, nt=nt, r2=r2: e.activation(
                sq[r2][:], x1h[fc][:, nt * 512:(nt + 1) * 512], AF.Square),
                reads=[x1h_k[fc]], writes=[sq_k[r2]])
            do("tensor", lambda e, fc=fc, nt=nt, st=(fc == 0), sp=(fc == 15):
                e.matmul(ps["A"][0:1, :], oc16_t[:],
                         x1h[fc][:, nt * 512:(nt + 1) * 512], start=st, stop=sp),
                reads=[x1h_k[fc], oc16_k], writes=[pt["A"]])
            do("tensor", lambda e, r2=r2, st=(fc == 0), sp=(fc == 15): e.matmul(
                ps["B"][0:1, :], oc16_t[:], sq[r2][:], start=st, stop=sp),
                reads=[sq_k[r2], oc16_k], writes=[pt["B"]])
        do("vector", lambda e, nt=nt: e.tensor_copy(
            sum_row[0:1, nt * 512:(nt + 1) * 512], ps["A"][0:1, :]),
            reads=[pt["A"]], writes=[sum_k])
        do("vector", lambda e, nt=nt: e.tensor_copy(
            ss_row[0:1, nt * 512:(nt + 1) * 512], ps["B"][0:1, :]),
            reads=[pt["B"]], writes=[ss_k])
    m_row, m_k = sbuf([1, 2048], F32, "m_row")
    v_row, v_k = sbuf([1, 2048], F32, "v_row")
    do("vector", lambda e: e.tensor_scalar_mul(m_row[:], sum_row[:], 1.0 / 2048),
        reads=[sum_k], writes=[m_k])
    do("vector", lambda e: e.tensor_scalar_mul(v_row[:], ss_row[:], 1.0 / 2048),
        reads=[ss_k], writes=[v_k])
    do("vector", lambda e: e.scalar_tensor_tensor(
        r_row[:], m_row[:], -1.0, m_row[:], ALU.mult, ALU.mult),
        reads=[m_k], writes=[r_k])
    do("vector", lambda e: e.tensor_add(ss_row[:], v_row[:], r_row[:]),
        reads=[v_k, r_k], writes=[ss_k])
    do("scalar", lambda e: e.activation(v_row[:], ss_row[:], AF.Sqrt,
                                        bias=cp(0, 1)),
        reads=[ss_k, cp_k], writes=[v_k])
    do("vector", lambda e: e.reciprocal(r_row[:], v_row[:]),
        reads=[v_k], writes=[r_k])
    M_bc, M_k = sbuf([128, 2048], F32, "M_bc")
    R_bc, R_k = sbuf([128, 2048], F32, "R_bc")
    for nt in range(4):
        do("tensor", lambda e, nt=nt: e.matmul(
            ps["C"][:, :], sm("o_r32")[0][:], m_row[0:1, nt * 512:(nt + 1) * 512],
            start=True, stop=True),
            reads=[sm("o_r32")[1], m_k], writes=[pt["C"]])
        do("vector", lambda e, nt=nt: e.tensor_copy(
            M_bc[:, nt * 512:(nt + 1) * 512], ps["C"][:, :]),
            reads=[pt["C"]], writes=[M_k])
        do("tensor", lambda e, nt=nt: e.matmul(
            ps["D"][:, :], sm("o_r32")[0][:], r_row[0:1, nt * 512:(nt + 1) * 512],
            start=True, stop=True),
            reads=[sm("o_r32")[1], r_k], writes=[pt["D"]])
        do("vector", lambda e, nt=nt: e.tensor_copy(
            R_bc[:, nt * 512:(nt + 1) * 512], ps["D"][:, :]),
            reads=[pt["D"]], writes=[R_k])
    xn1, xn1_k = sbufs(16, [128, 2048], BF, "xn1")
    dtmp, dtmp_k = sbufs(2, [128, 2048], F16, "dtmpB")
    for fc in range(16):
        r = fc % 2
        do("vector", lambda e, fc=fc, r=r: e.tensor_tensor(
            dtmp[r][:], x1h[fc][:], M_bc[:], ALU.subtract),
            reads=[x1h_k[fc], M_k], writes=[dtmp_k[r]])
        do("vector", lambda e, fc=fc, r=r: e.tensor_tensor(
            xn1[fc][:], dtmp[r][:], R_bc[:], ALU.mult),
            reads=[dtmp_k[r], R_k], writes=[xn1_k[fc]])
    if debug:
        dodma("gpsimd", dxn, xn1[0][:], reads=[xn1_k[0]])
    S.barrier()
    A.free(x1h, M_bc, R_bc, sq, sum_row, ss_row, r_row, m_row, v_row, dtmp)

    wba_t, wba_k = sbuf([128, 1024], BF, "wba_t")
    dodma("gpsimd", wba_t[:].rearrange("p (c m) -> p c m", m=64),
          wbaT.rearrange("(c p) m -> p c m", p=128), writes=[wba_k])
    brow, brow_k = sbuf([32, 2048], F32, "brow")
    arow, arow_k = sbuf([32, 2048], F32, "arow")
    t3a, t3a_k = sbuf([32, 2048], F32, "t3a")
    t3b, t3b_k = sbuf([32, 2048], F32, "t3b")
    ea_t, ea_k = sbuf([32, 1], F32, "ea_t")
    dtb32, dtb32_k = sbuf([32, 1], F32, "dtb32")
    do("vector", lambda e: e.memset(dtb32[:], 0.0), writes=[dtb32_k])
    do("vector", lambda e: e.tensor_copy(dtb32[0:3, :], sm("dtb")[0][:]),
        reads=[sm("dtb")[1]], writes=[dtb32_k])
    alog32, alog32_k = sbuf([32, 1], F32, "alog32")
    do("vector", lambda e: e.memset(alog32[:], 0.0), writes=[alog32_k])
    do("vector", lambda e: e.tensor_copy(alog32[0:3, :], sm("alog")[0][:]),
        reads=[sm("alog")[1]], writes=[alog32_k])
    do("vector", lambda e: e.memset(arow[:], 0.0), writes=[arow_k])
    for nt in range(4):
        for ic in range(16):
            do("tensor", lambda e, ic=ic, nt=nt, st=(ic == 0), sp=(ic == 15):
                e.matmul(ps["A"][0:64, :], wba_t[:, ic * 64:(ic + 1) * 64],
                         xn1[ic][:, nt * 512:(nt + 1) * 512], start=st, stop=sp),
                reads=[xn1_k[ic], wba_k], writes=[pt["A"]])
        do("scalar", lambda e, nt=nt: e.activation(
            brow[0:3, nt * 512:(nt + 1) * 512], ps["A"][0:3, :], AF.Sigmoid),
            reads=[pt["A"]], writes=[brow_k])
        do("vector", lambda e, nt=nt: e.tensor_copy(
            arow[0:3, nt * 512:(nt + 1) * 512], ps["A"][32:35, :]),
            reads=[pt["A"]], writes=[arow_k])
    a2, a2_k = sbuf([32, 2048], F32, "a2")
    ea2, ea2_k = sbuf([32, 1], F32, "ea2")
    do("vector", lambda e: e.tensor_scalar_add(
        a2[:], arow[:], dtb32[:, 0:1]),
        reads=[arow_k], sreads=[dtb32_k], writes=[a2_k])
    do("scalar", lambda e: e.activation(t3a[:], a2[:], AF.Abs),
        reads=[a2_k], writes=[t3a_k])
    do("scalar", lambda e: e.activation(t3b[:], t3a[:], AF.Exp, scale=cp(3, 32)),
        reads=[t3a_k, cp_k], writes=[t3b_k])
    do("scalar", lambda e: e.activation(t3a[:], t3b[:], AF.Ln, bias=cp(4, 32)),
        reads=[t3b_k, cp_k], writes=[t3a_k])
    do("vector", lambda e: e.tensor_scalar_max(t3b[:], a2[:], 0.0),
        reads=[a2_k], writes=[t3b_k])
    do("vector", lambda e: e.tensor_add(arow[:], t3a[:], t3b[:]),
        reads=[t3a_k, t3b_k], writes=[arow_k])
    do("scalar", lambda e: e.activation(ea_t[:], alog32[:], AF.Exp),
        reads=[alog32_k], writes=[ea_k])
    do("vector", lambda e: e.tensor_scalar_mul(ea2[:], ea_t[:], -1.0),
        reads=[ea_k], writes=[ea2_k])
    do("vector", lambda e: e.tensor_scalar_mul(
        a2[:], arow[:], ea2[:, 0:1]),
        reads=[arow_k], sreads=[ea2_k], writes=[a2_k])
    do("vector", lambda e: e.tensor_tensor_scan(
        t3a[:], a2[:], a2[:], 0.0, ALU.add, ALU.bypass),
        reads=[a2_k], writes=[t3a_k])
    dodma("gpsimd", rowd[0:3, :], brow[0:3, :], reads=[brow_k])
    dodma("gpsimd", rowd[3:6, :], t3a[0:3, :], reads=[t3a_k])
    if debug:
        dodma("gpsimd", rowd[6:9, :], arow[0:3, :], reads=[arow_k])
        dodma("gpsimd", rowd[9:12, :], a2[0:3, :], reads=[a2_k])
        dodma("gpsimd", rowd[12:13, 0:32].rearrange("o p -> p o"), ea2[:, 0:1],
              reads=[ea2_k], allow_slow_non_contiguous=True)
        dodma("gpsimd", rowd[13:14, 0:32].rearrange("o p -> p o"), dtb32[:, 0:1],
              reads=[dtb32_k], allow_slow_non_contiguous=True)
    NW, NW_k = sbuf([128, 256], F32, "NW")
    do("tensor", lambda e: e.matmul(ps["B"][:, 0:256], sm("o_r32")[0][:],
                                    sm("onw")[0][:], start=True, stop=True),
        reads=[sm("onw")[1], sm("o_r32")[1]], writes=[pt["B"]])
    do("vector", lambda e: e.tensor_copy(NW[:], ps["B"][:, 0:256]),
        reads=[pt["B"]], writes=[NW_k])
    S.barrier()
    A.free(wba_t, brow, arow, t3b, ea_t, t3a, dtb32, alog32, a2, ea2)

    def do_head(hl):
        hw = {}

        def hbuf(nm, shape, dt):
            t, k_ = sbuf(shape, dt, f"h{hl}_{nm}")
            hw[nm] = t
            return t, k_

        cgr, cgr_k = hbuf("cgr", [1, 2048], F32)
        dodma("sync", cgr[:], rowd[3 + hl:4 + hl, :], writes=[cgr_k])
        btr, btr_k = hbuf("btr", [1, 2048], F32)
        dodma("sync", btr[:], rowd[hl:hl + 1, :], writes=[btr_k])
        c16, c16_k = hbuf("c16", [16, 128], F32)
        dodma("sync", c16[:], rowd[3 + hl:4 + hl, :].rearrange(
            "o (c p) -> (o c) p", p=128), writes=[c16_k])
        b16, b16_k = hbuf("b16", [16, 128], F32)
        dodma("sync", b16[:], rowd[hl:hl + 1, :].rearrange(
            "o (c p) -> (o c) p", p=128), writes=[b16_k])
        cgc, cgc_k = hbuf("cgc", [128, 16], F32)
        do("tensor", lambda e, c16=c16: e.transpose(
            ps["D"][:, 0:16], c16[:], sm("idn32")[0][0:16, 0:16]),
            reads=[c16_k, sm("idn32")[1]], writes=[pt["D"]])
        do("vector", lambda e: e.tensor_copy(cgc[:], ps["D"][:, 0:16]),
            reads=[pt["D"]], writes=[cgc_k])
        btc, btc_k = hbuf("btc", [128, 16], F32)
        do("tensor", lambda e, b16=b16: e.transpose(
            ps["D"][:, 0:16], b16[:], sm("idn32")[0][0:16, 0:16]),
            reads=[b16_k, sm("idn32")[1]], writes=[pt["D"]])
        do("vector", lambda e: e.tensor_copy(btc[:], ps["D"][:, 0:16]),
            reads=[pt["D"]], writes=[btc_k])
        nbc, nbc_k = hbuf("nbc", [128, 16], F32)
        do("vector", lambda e: e.tensor_scalar_mul(nbc[:], btc[:], -1.0),
            reads=[btc_k], writes=[nbc_k])
        endr, endr_k = hbuf("endr", [1, 16], F32)
        do("vector", lambda e: e.tensor_copy(
            endr[:], cgr[0:1, 127:2048:128]), reads=[cgr_k], writes=[endr_k])
        per, per_k = hbuf("per", [1, 16], F32)
        do("vector", lambda e: e.memset(per[:], 0.0), writes=[per_k])
        do("vector", lambda e: e.tensor_copy(
            per[0:1, 1:16], cgr[0:1, 127:1920:128]),
            reads=[cgr_k], writes=[per_k])
        gcr, gcr_k = hbuf("gcr", [1, 16], F32)
        gcx, gcx_k = hbuf("gcx", [1, 16], F32)
        do("vector", lambda e: e.tensor_tensor(
            gcx[:], endr[:], per[:], ALU.subtract),
            reads=[endr_k, per_k], writes=[gcx_k])
        do("scalar", lambda e: e.activation(gcr[:], gcx[:], AF.Exp),
            reads=[gcx_k], writes=[gcr_k])
        gCbc, gCbc_k = hbuf("gCbc", [128, 16], F32)
        do("tensor", lambda e: e.matmul(ps["D"][:, 0:16], sm("o_r32")[0][:],
                                        gcr[:], start=True, stop=True),
            reads=[gcr_k, sm("o_r32")[1]], writes=[pt["D"]])
        do("vector", lambda e: e.tensor_copy(gCbc[:], ps["D"][:, 0:16]),
            reads=[pt["D"]], writes=[gCbc_k])
        gcols, gcols_k = hbuf("gcols", [128, 16], F32)
        gtmp, gtmp_k = hbuf("gtmp", [128, 16], F32)
        do("tensor", lambda e: e.matmul(ps["D"][:, 0:16], sm("o_r32")[0][:],
                                        per[:], start=True, stop=True),
            reads=[per_k, sm("o_r32")[1]], writes=[pt["D"]])
        do("vector", lambda e: e.tensor_tensor(
            gtmp[:], cgc[:], ps["D"][:, 0:16], ALU.subtract),
            reads=[cgc_k, pt["D"]], writes=[gtmp_k])
        do("scalar", lambda e: e.activation(gcols[:], gtmp[:], AF.Exp),
            reads=[gtmp_k], writes=[gcols_k])
        gsc, gsc_k = hbuf("gsc", [128, 16], F32)
        do("vector", lambda e: e.tensor_scalar_mul(gsc[:], gcols[:], 1.0 / 16.0),
            reads=[gcols_k], writes=[gsc_k])
        nbg, nbg_k = hbuf("nbg", [128, 16], F32)
        do("vector", lambda e: e.tensor_tensor(
            nbg[:], nbc[:], gcols[:], ALU.mult),
            reads=[nbc_k, gcols_k], writes=[nbg_k])
        rend, rend_k = hbuf("rend", [128, 16], F32)
        do("tensor", lambda e: e.matmul(ps["D"][:, 0:16], sm("o_r32")[0][:],
                                        endr[:], start=True, stop=True),
            reads=[endr_k, sm("o_r32")[1]], writes=[pt["D"]])
        do("vector", lambda e: e.tensor_tensor(
            gtmp[:], ps["D"][:, 0:16], cgc[:], ALU.subtract),
            reads=[pt["D"], cgc_k], writes=[gtmp_k])
        do("scalar", lambda e: e.activation(rend[:], gtmp[:], AF.Exp),
            reads=[gtmp_k], writes=[rend_k])

        wts = {}
        for nm, src in [("wq", wqT), ("wk", wkT), ("wv", wvT), ("wg", wgT)]:
            t, k_ = hbuf(nm, [128, 4096], BF)
            dodma("gpsimd", t[:].rearrange("p (c m) -> p c m", m=256),
                  src[:, hl * 256:(hl + 1) * 256].rearrange(
                      "(c p) m -> p c m", p=128), writes=[k_])
            wts[nm] = (t, k_)

        proj = {}
        for nm in ("wq", "wk", "wv"):
            wt_, wk_ = wts[nm]
            pp, pp_k = sbufs(2, [128, 2048], BF, f"h{hl}_{nm}p")
            hw[nm + "p"] = pp
            for oc2 in range(2):
                for n4 in range(4):
                    for ic in range(16):
                        do("tensor", lambda e, wt_=wt_, ic=ic, oc2=oc2, n4=n4,
                           st=(ic == 0), sp=(ic == 15): e.matmul(
                               ps["A"][:, :],
                               wt_[:, ic * 256 + oc2 * 128:
                                   ic * 256 + oc2 * 128 + 128],
                               xn1[ic][:, n4 * 512:(n4 + 1) * 512],
                               start=st, stop=sp),
                            reads=[xn1_k[ic], wk_], writes=[pt["A"]])
                    do("vector", lambda e, pp=pp, oc2=oc2, n4=n4: e.tensor_copy(
                        pp[oc2][:, n4 * 512:(n4 + 1) * 512], ps["A"][:, :]),
                        reads=[pt["A"]], writes=[pp_k[oc2]])
            proj[nm] = (pp, pp_k)
        sg_h, sg_k = sbufs(16, [128, 256], BF, f"h{hl}_sg")
        hw["sg"] = sg_h
        wgt_, wgk_ = wts["wg"]
        for tt in range(16):
            for ic in range(16):
                do("tensor", lambda e, ic=ic, tt=tt, st=(ic == 0), sp=(ic == 15):
                    e.matmul(ps["B"][:, 0:256],
                             xn1[ic][:, tt * 128:(tt + 1) * 128],
                             wgt_[:, ic * 256:(ic + 1) * 256], start=st, stop=sp),
                    reads=[xn1_k[ic], wgk_], writes=[pt["B"]])
            do("scalar", lambda e, tt=tt: e.activation(
                sg_h[tt][:], ps["B"][:, 0:256], AF.Silu),
                reads=[pt["B"]], writes=[sg_k[tt]])
        S.barrier()
        for nm in ("wq", "wk", "wv", "wg"):
            A.free(wts[nm][0])
            hw.pop(nm)

        def conv_silu(nm, cw_nm, norm):
            pp, pp_k = proj[nm]
            cwt, cwt_k = sm(cw_nm)
            yb, yb_k = sbufs(2, [128, 2048], F32, f"h{hl}_{nm}y")
            out, out_k = sbufs(2, [128, 2048], BF, f"h{hl}_{nm}c")
            hw[nm + "c"] = out
            hw[nm + "y"] = yb
            for fc in range(2):
                cgl = hl * 2 + fc

                def cw(tap):
                    return cwt[:, cgl * 4 + tap:cgl * 4 + tap + 1]
                cur, oth = 0, 1
                do("vector", lambda e, fc=fc, cwap=cw(3): e.tensor_scalar_mul(
                    yb[0][:], pp[fc][:], cwap),
                    reads=[pp_k[fc], cwt_k], writes=[yb_k[0]])
                for i, tap in enumerate((2, 1, 0)):
                    off = i + 1
                    do("vector", lambda e, fc=fc, cwap=cw(tap), off=off, cur=cur,
                       oth=oth: e.scalar_tensor_tensor(
                           yb[oth][:, off:2048], pp[fc][:, 0:2048 - off],
                           cwap, yb[cur][:, off:2048], ALU.mult, ALU.add),
                        reads=[pp_k[fc], cwt_k, yb_k[cur]], writes=[yb_k[oth]])
                    do("vector", lambda e, off=off, cur=cur, oth=oth:
                        e.tensor_copy(yb[oth][:, 0:off], yb[cur][:, 0:off]),
                        reads=[yb_k[cur]], writes=[yb_k[oth]])
                    cur, oth = oth, cur
                do("scalar", lambda e, fc=fc, cur=cur: e.activation(
                    out[fc][:], yb[cur][:], AF.Silu),
                    reads=[yb_k[cur]], writes=[out_k[fc]])
            S.barrier()
            A.free(pp, yb)
            hw.pop(nm + "p")
            hw.pop(nm + "y")
            if norm:
                nr, nr_k = sbuf([1, 2048], F32, f"h{hl}_{nm}nr")
                nr2, nr2_k = sbuf([1, 2048], F32, f"h{hl}_{nm}nr2")
                sqb, sqb_k = sbufs(2, [128, 512], BF, f"h{hl}_{nm}sq")
                nout, nout_k = sbufs(2, [128, 2048], BF, f"h{hl}_{nm}n")
                hw[nm + "n"] = nout
                for nt in range(4):
                    for fc in range(2):
                        r2 = fc % 2
                        do("scalar", lambda e, fc=fc, nt=nt, r2=r2: e.activation(
                            sqb[r2][:], out[fc][:, nt * 512:(nt + 1) * 512],
                            AF.Square), reads=[out_k[fc]], writes=[sqb_k[r2]])
                        do("tensor", lambda e, r2=r2, st=(fc == 0), sp=(fc == 1):
                            e.matmul(ps["C"][0:1, :], sm("o_cbf")[0][:],
                                     sqb[r2][:], start=st, stop=sp),
                            reads=[sqb_k[r2], sm("o_cbf")[1]], writes=[pt["C"]])
                    do("vector", lambda e, nt=nt: e.tensor_copy(
                        nr[0:1, nt * 512:(nt + 1) * 512], ps["C"][0:1, :]),
                        reads=[pt["C"]], writes=[nr_k])
                do("scalar", lambda e: e.activation(nr2[:], nr[:], AF.Sqrt,
                                                    bias=cp(5, 1)),
                    reads=[nr_k, cp_k], writes=[nr2_k])
                do("vector", lambda e: e.reciprocal(nr[:], nr2[:]),
                    reads=[nr2_k], writes=[nr_k])
                for nt in range(4):
                    do("tensor", lambda e, nt=nt: e.matmul(
                        ps["D"][:, :], sm("o_r32")[0][:],
                        nr[0:1, nt * 512:(nt + 1) * 512], start=True, stop=True),
                        reads=[nr_k, sm("o_r32")[1]], writes=[pt["D"]])
                    for fc in range(2):
                        do("vector", lambda e, fc=fc, nt=nt: e.tensor_tensor(
                            nout[fc][:, nt * 512:(nt + 1) * 512],
                            out[fc][:, nt * 512:(nt + 1) * 512],
                            ps["D"][:, :], ALU.mult),
                            reads=[out_k[fc], pt["D"]], writes=[nout_k[fc]])
                A.free(nr, nr2, sqb, out)
                hw.pop(nm + "c")
                return nout, nout_k
            return out, out_k

        qn, qn_k = conv_silu("wq", "cwq", True)
        kn, kn_k = conv_silu("wk", "cwk", True)
        vc, vc_k = conv_silu("wv", "cwv", False)

        if debug and hl == 0:
            for fc in range(2):
                dodma("gpsimd", dqn[fc], qn[fc][:], reads=[qn_k[fc]])
                dodma("gpsimd", dkn[fc], kn[fc][:], reads=[kn_k[fc]])
                dodma("gpsimd", dvc[fc], vc[fc][:], reads=[vc_k[fc]])
            for i, (tt_, kk_) in enumerate([(cgc, cgc_k), (btc, btc_k),
                                            (gcols, gcols_k), (gCbc, gCbc_k),
                                            (rend, rend_k), (nbg, nbg_k)]):
                dodma("gpsimd", dcol[i], tt_[:], reads=[kk_])
        ktm, ktm_k = sbufs(16, [128, 256], BF, f"h{hl}_ktm")
        vtm, vtm_k = sbufs(16, [128, 256], BF, f"h{hl}_vtm")
        bv, bv_k = sbufs(16, [128, 256], BF, f"h{hl}_bv")
        kbar, kbar_k = sbufs(16, [128, 256], BF, f"h{hl}_kbar")
        hw.update(ktm=ktm, vtm=vtm, bv=bv, kbar=kbar)
        for cc in range(16):
            for fc in range(2):
                tb = "T1" if fc == 0 else "T2"
                do("tensor", lambda e, cc=cc, fc=fc, tb=tb: e.transpose(
                    ps[tb][:, 0:128], kn[fc][:, cc * 128:(cc + 1) * 128],
                    sm("idnbf")[0][:]),
                    reads=[kn_k[fc], sm("idnbf")[1]], writes=[pt[tb]])
                do("vector", lambda e, cc=cc, fc=fc, tb=tb: e.tensor_copy(
                    ktm[cc][:, fc * 128:(fc + 1) * 128], ps[tb][:, 0:128]),
                    reads=[pt[tb]], writes=[ktm_k[cc]])
                do("tensor", lambda e, cc=cc, fc=fc, tb=tb: e.transpose(
                    ps[tb][:, 128:256], vc[fc][:, cc * 128:(cc + 1) * 128],
                    sm("idnbf")[0][:]),
                    reads=[vc_k[fc], sm("idnbf")[1]], writes=[pt[tb]])
                do("vector", lambda e, cc=cc, fc=fc, tb=tb: e.tensor_copy(
                    vtm[cc][:, fc * 128:(fc + 1) * 128], ps[tb][:, 128:256]),
                    reads=[pt[tb]], writes=[vtm_k[cc]])
            do("vector", lambda e, cc=cc: e.tensor_scalar_mul(
                bv[cc][:], vtm[cc][:], btc[:, cc:cc + 1]),
                reads=[vtm_k[cc]], sreads=[btc_k], writes=[bv_k[cc]])
            do("vector", lambda e, cc=cc: e.tensor_scalar_mul(
                kbar[cc][:], ktm[cc][:], rend[:, cc:cc + 1]),
                reads=[ktm_k[cc]], sreads=[rend_k], writes=[kbar_k[cc]])

        if debug and hl == 0:
            for cc in range(2):
                dodma("gpsimd", dktm[cc], ktm[cc][:], reads=[ktm_k[cc]])
        ntl, ntl_k = sbufs(16, [128, 128], BF, f"h{hl}_nt")
        ptl, ptl_k = sbufs(16, [128, 128], F32, f"h{hl}_pt")
        hw.update(ntl=ntl, ptl=ptl)
        eut, eut_k = sbuf([128, 128], F32, f"h{hl}_eut")
        elt, elt_k = sbuf([128, 128], F32, f"h{hl}_elt")
        tq1, tq1_k = sbuf([128, 128], F32, f"h{hl}_tq1")
        tq2, tq2_k = sbuf([128, 128], F32, f"h{hl}_tq2")
        tq3, tq3_k = sbuf([128, 128], F32, f"h{hl}_tq3")
        ab_b, ab_k = sbufs(6, [128, 128], F32, f"h{hl}_ab")
        for cc in range(16):
            for fc in range(2):
                do("tensor", lambda e, cc=cc, fc=fc, st=(fc == 0), sp=(fc == 1):
                    e.matmul(ps["A"][:, 0:128],
                             kn[fc][:, cc * 128:(cc + 1) * 128],
                             qn[fc][:, cc * 128:(cc + 1) * 128],
                             start=st, stop=sp),
                    reads=[kn_k[fc], qn_k[fc]], writes=[pt["A"]])
                do("tensor", lambda e, cc=cc, fc=fc, st=(fc == 0), sp=(fc == 1):
                    e.matmul(ps["B"][:, 0:128],
                             kn[fc][:, cc * 128:(cc + 1) * 128],
                             kn[fc][:, cc * 128:(cc + 1) * 128],
                             start=st, stop=sp),
                    reads=[kn_k[fc]], writes=[pt["B"]])
            do("tensor", lambda e, cc=cc: e.matmul(
                ps["D"][:, 0:128], sm("o_r32")[0][:],
                cgr[0:1, cc * 128:(cc + 1) * 128], start=True, stop=True),
                reads=[cgr_k, sm("o_r32")[1]], writes=[pt["D"]])
            do("vector", lambda e, cc=cc: e.tensor_scalar(
                tq1[:], ps["D"][:, 0:128], cgc[:, cc:cc + 1], 0.0,
                ALU.subtract, ALU.min),
                reads=[pt["D"]], sreads=[cgc_k], writes=[tq1_k])
            do("scalar", lambda e: e.activation(eut[:], tq1[:], AF.Exp),
                reads=[tq1_k], writes=[eut_k])
            do("vector", lambda e, cc=cc: e.tensor_scalar(
                tq2[:], ps["D"][:, 0:128], cgc[:, cc:cc + 1], 0.0,
                ALU.subtract, ALU.max),
                reads=[pt["D"]], sreads=[cgc_k], writes=[tq2_k])
            do("scalar", lambda e: e.activation(elt[:], tq2[:], AF.Exp,
                                                scale=cp(3)),
                reads=[tq2_k, cp_k], writes=[elt_k])
            do("vector", lambda e: e.tensor_tensor(
                tq1[:], eut[:], sm("utm")[0][:], ALU.mult),
                reads=[eut_k, sm("utm")[1]], writes=[tq1_k])
            do("vector", lambda e, cc=cc: e.tensor_tensor(
                ntl[cc][:], tq1[:], ps["A"][:, 0:128], ALU.mult),
                reads=[tq1_k, pt["A"]], writes=[ntl_k[cc]])
            do("vector", lambda e: e.tensor_tensor(
                tq2[:], elt[:], sm("lts")[0][:], ALU.mult),
                reads=[elt_k, sm("lts")[1]], writes=[tq2_k])
            do("vector", lambda e, cc=cc: e.scalar_tensor_tensor(
                ab_b[0][:], tq2[:], nbc[:, cc:cc + 1], ps["B"][:, 0:128],
                ALU.mult, ALU.mult),
                reads=[tq2_k, pt["B"]], sreads=[nbc_k], writes=[ab_k[0]])
            do("vector", lambda e: e.tensor_tensor(
                tq3[:], eut[:], sm("uts")[0][:], ALU.mult),
                reads=[eut_k, sm("uts")[1]], writes=[tq3_k])
            do("vector", lambda e, cc=cc: e.tensor_tensor(
                tq1[:], tq3[:], ps["B"][:, 0:128], ALU.mult),
                reads=[tq3_k, pt["B"]], writes=[tq1_k])
            do("tensor", lambda e, cc=cc: e.matmul(
                ps["E"][:, 0:128], sm("no_r32")[0][:],
                btr[0:1, cc * 128:(cc + 1) * 128], start=True, stop=True),
                reads=[btr_k, sm("no_r32")[1]], writes=[pt["E"]])
            do("vector", lambda e: e.tensor_tensor(
                ab_b[2][:], tq1[:], ps["E"][:, 0:128], ALU.mult),
                reads=[tq1_k, pt["E"]], writes=[ab_k[2]])
            do("vector", lambda e: e.tensor_add(
                ab_b[4][:], ab_b[2][:], sm("idn32")[0][:]),
                reads=[ab_k[2], sm("idn32")[1]], writes=[ab_k[4]])
            cur_a, cur_b, cur_p = 0, 2, 4
            oth_a, oth_b, oth_p = 1, 3, 5
            for j in range(1, 7):
                do("tensor", lambda e, ca=cur_a, cb=cur_b: e.matmul(
                    ps["A"][:, 0:128], ab_b[cb][:], ab_b[ca][:],
                    start=True, stop=True),
                    reads=[ab_k[cur_a], ab_k[cur_b]], writes=[pt["A"]])
                do("vector", lambda e, oa=oth_a: e.tensor_copy(
                    ab_b[oa][:], ps["A"][:, 0:128]),
                    reads=[pt["A"]], writes=[ab_k[oth_a]])
                if j < 6:
                    do("tensor", lambda e, ca=cur_a, cb=cur_b: e.matmul(
                        ps["B"][:, 0:128], ab_b[ca][:], ab_b[cb][:],
                        start=True, stop=True),
                        reads=[ab_k[cur_a], ab_k[cur_b]], writes=[pt["B"]])
                    do("vector", lambda e, ob=oth_b: e.tensor_copy(
                        ab_b[ob][:], ps["B"][:, 0:128]),
                        reads=[pt["B"]], writes=[ab_k[oth_b]])
                do("tensor", lambda e, cp_=cur_p: e.matmul(
                    ps["C"][:, 0:128], sm("idn32")[0][:], ab_b[cp_][:],
                    start=True, stop=False),
                    reads=[ab_k[cur_p], sm("idn32")[1]], writes=[pt["C"]])
                do("tensor", lambda e, oa=oth_a, cp_=cur_p: e.matmul(
                    ps["C"][:, 0:128], ab_b[oa][:], ab_b[cp_][:],
                    start=False, stop=True),
                    reads=[ab_k[oth_a], ab_k[cur_p]], writes=[pt["C"]])
                do("vector", lambda e, op_=oth_p: e.tensor_copy(
                    ab_b[op_][:], ps["C"][:, 0:128]),
                    reads=[pt["C"]], writes=[ab_k[oth_p]])
                cur_a, oth_a = oth_a, cur_a
                cur_b, oth_b = oth_b, cur_b
                cur_p, oth_p = oth_p, cur_p
            do("vector", lambda e, cc=cc, cp_=cur_p: e.tensor_copy(
                ptl[cc][:], ab_b[cp_][:]),
                reads=[ab_k[cur_p]], writes=[ptl_k[cc]])
            if debug and hl == 0 and cc < 2:
                dodma("gpsimd", dnt[cc], ntl[cc][:], reads=[ntl_k[cc]])
                dodma("gpsimd", dpt[cc], ptl[cc][:], reads=[ptl_k[cc]])

        sf, sf_k = sbufs(2, [128, 256], F32, f"h{hl}_sf")
        sb_, sb_k = sbufs(2, [128, 256], BF, f"h{hl}_sb")
        snf, snf_k = sbufs(2, [128, 256], F32, f"h{hl}_snf")
        r2b, r2b_k = sbuf([128, 256], F32, f"h{hl}_r2")
        ubf, ubf_k = sbuf([128, 256], BF, f"h{hl}_ubf")
        oac, oac_k = sbuf([128, 256], F32, f"h{hl}_oac")
        oa2, oa2_k = sbuf([128, 256], F32, f"h{hl}_oa2")
        ssq2, ssq2_k = sbuf([128, 1], F32, f"h{hl}_ssq2")
        osq, osq_k = sbuf([128, 256], BF, f"h{hl}_osq")
        ssq, ssq_k = sbuf([128, 1], F32, f"h{hl}_ssq")
        gnw, gnw_k = sbuf([128, 256], BF, f"h{hl}_gnw")
        obuf, obuf_k = sbufs(2, [128, 256], BF, f"h{hl}_ob")
        for fc in range(2):
            do("vector", lambda e, fc=fc: e.memset(sf[fc][:], 0.0),
                writes=[sf_k[fc]])
            do("vector", lambda e, fc=fc: e.memset(sb_[fc][:], 0.0),
                writes=[sb_k[fc]])
        for cc in range(16):
            for fc in range(2):
                do("tensor", lambda e, cc=cc, fc=fc, st=(fc == 0), sp=(fc == 1):
                    e.matmul(ps["A"][:, 0:256],
                             kn[fc][:, cc * 128:(cc + 1) * 128], sb_[fc][:],
                             start=st, stop=sp),
                    reads=[kn_k[fc], sb_k[fc]], writes=[pt["A"]])
            do("vector", lambda e, cc=cc: e.scalar_tensor_tensor(
                r2b[:], ps["A"][:, 0:256], nbg[:, cc:cc + 1], bv[cc][:],
                ALU.mult, ALU.add),
                reads=[pt["A"], bv_k[cc]], sreads=[nbg_k], writes=[r2b_k])
            do("tensor", lambda e, cc=cc: e.matmul(
                ps["B"][:, 0:256], ptl[cc][:], r2b[:], start=True, stop=True),
                reads=[ptl_k[cc], r2b_k], writes=[pt["B"]])
            do("vector", lambda e: e.tensor_copy(ubf[:], ps["B"][:, 0:256]),
                reads=[pt["B"]], writes=[ubf_k])
            if debug and hl == 0 and cc < 2:
                dodma("gpsimd", dub[cc], ubf[:], reads=[ubf_k])
            for fc in range(2):
                do("tensor", lambda e, cc=cc, fc=fc, st=(fc == 0), sp=(fc == 1):
                    e.matmul(ps["C"][:, 0:256],
                             qn[fc][:, cc * 128:(cc + 1) * 128], sb_[fc][:],
                             start=st, stop=sp),
                    reads=[qn_k[fc], sb_k[fc]], writes=[pt["C"]])
            do("vector", lambda e, cc=cc: e.tensor_scalar_mul(
                oac[:], ps["C"][:, 0:256], gsc[:, cc:cc + 1]),
                reads=[pt["C"]], sreads=[gsc_k], writes=[oac_k])
            do("tensor", lambda e, cc=cc: e.matmul(
                ps["D"][:, 0:256], ntl[cc][:], ubf[:], start=True, stop=True),
                reads=[ntl_k[cc], ubf_k], writes=[pt["D"]])
            do("vector", lambda e: e.tensor_tensor(
                oa2[:], oac[:], ps["D"][:, 0:256], ALU.add),
                reads=[oac_k, pt["D"]], writes=[oa2_k])
            if debug and hl == 0 and cc < 2:
                dodma("gpsimd", doa[cc], oa2[:], reads=[oa2_k])
            do("scalar", lambda e: e.activation(
                osq[:], oa2[:], AF.Square, accum_out=ssq[:]),
                reads=[oa2_k], writes=[osq_k, ssq_k])
            do("scalar", lambda e: e.activation(
                ssq2[:], ssq[:], AF.Sqrt, scale=cp(6), bias=cp(0)),
                reads=[cp_k], sreads=[ssq_k], writes=[ssq2_k])
            do("vector", lambda e: e.reciprocal(ssq[:], ssq2[:]),
                reads=[ssq2_k], writes=[ssq_k])
            do("vector", lambda e, cc=cc: e.tensor_tensor(
                gnw[:], sg_h[cc][:], NW[:], ALU.mult),
                reads=[sg_k[cc], NW_k], writes=[gnw_k])
            rb = cc % 2
            do("vector", lambda e, rb=rb: e.scalar_tensor_tensor(
                obuf[rb][:], oa2[:], ssq[0:128, 0:1], gnw[:],
                ALU.mult, ALU.mult),
                reads=[oa2_k, gnw_k], sreads=[ssq_k], writes=[obuf_k[rb]])
            dodma("sync", oB[cc * 128:(cc + 1) * 128,
                               hl * 256:(hl + 1) * 256],
                  obuf[rb][:], reads=[obuf_k[rb]])
            do("tensor", lambda e, cc=cc: e.matmul(
                ps["A"][:, 0:256], kbar[cc][:, 0:128], ubf[:],
                start=True, stop=True),
                reads=[kbar_k[cc], ubf_k], writes=[pt["A"]])
            do("tensor", lambda e, cc=cc: e.matmul(
                ps["E"][:, 0:256], kbar[cc][:, 128:256], ubf[:],
                start=True, stop=True),
                reads=[kbar_k[cc], ubf_k], writes=[pt["E"]])
            for fc, bk in ((0, "A"), (1, "E")):
                do("vector", lambda e, cc=cc, bk=bk, dst=snf[fc], src=sf[fc]:
                    e.scalar_tensor_tensor(
                        dst[:], src[:], gCbc[:, cc:cc + 1],
                        ps[bk][:, 0:256], ALU.mult, ALU.add),
                    reads=[sf_k[fc], pt[bk]], sreads=[gCbc_k], writes=[snf_k[fc]])
                do("vector", lambda e, dst=sb_[fc], src=snf[fc]: e.tensor_copy(
                    dst[:], src[:]),
                    reads=[snf_k[fc]], writes=[sb_k[fc]])
            sf, snf = snf, sf
            sf_k, snf_k = snf_k, sf_k
        S.barrier()
        for nm in list(hw.keys()):
            A.free(hw[nm])
        A.free(sf, sb_, snf, r2b, ubf, oac, oa2, osq, ssq, ssq2, gnw, obuf,
               eut, elt, tq1, tq2, tq3, ab_b)

    for _hl in range(3):
        do_head(_hl)

    for _i in range(Sched.NVQ):
        S.wait("gpsimd", (f"gpsimd_d{_i}", S.cnt[f"gpsimd_d{_i}"]))
        S.wait("sync", (f"sync_d{_i}", S.cnt[f"sync_d{_i}"]))
    with nc.allow_low_precision("bf16 intermediates are deliberate"):
        with nc.Block() as block:
            S.build(block)
    ctx.close()
    return nc


def prep_B(d, x1fm):
    maps = []
    masks = dict(
        utm_sc=(np.triu(np.ones((128, 128))) * (256.0 ** -0.5)).astype(np.float32),
        uts=np.triu(np.ones((128, 128)), 1).astype(np.float32),
        lts=np.tril(np.ones((128, 128)), -1).astype(np.float32),
        idn32=np.eye(128, dtype=np.float32),
        idnbf=np.eye(128).astype(BF16),
        ones_r32=np.ones((1, 128), np.float32),
        negones_r32=np.full((1, 128), -1.0, np.float32),
        ones_cbf=np.ones((128, 1)).astype(BF16),
        onw=d["L1_attn_o_norm_w"].reshape(1, 256).astype(np.float32),
        cpool=CPOOL,
    )
    for c in range(8):
        b, p = c // 2, c % 2
        sl = slice(768 * p, 768 * (p + 1))
        hsl = slice(3 * p, 3 * (p + 1))
        wba = np.zeros((2048, 64), np.float32)
        wba[:, 0:3] = d["L1_attn_Wb"][hsl].T
        wba[:, 32:35] = d["L1_attn_Wa"][hsl].T
        m = dict(masks)
        m.update(
            x1=np.ascontiguousarray(x1fm[b]),
            wqT=np.ascontiguousarray(d["L1_attn_Wq"][sl].T),
            wkT=np.ascontiguousarray(d["L1_attn_Wk"][sl].T),
            wvT=np.ascontiguousarray(d["L1_attn_Wv"][sl].T),
            wgT=np.ascontiguousarray(d["L1_attn_Wg"][sl].T),
            wbaT=wba,
            cwq=np.ascontiguousarray(
                d["L1_attn_conv_q"][sl].reshape(6, 128, 4).transpose(1, 0, 2)
                .reshape(128, 24)),
            cwk=np.ascontiguousarray(
                d["L1_attn_conv_k"][sl].reshape(6, 128, 4).transpose(1, 0, 2)
                .reshape(128, 24)),
            cwv=np.ascontiguousarray(
                d["L1_attn_conv_v"][sl].reshape(6, 128, 4).transpose(1, 0, 2)
                .reshape(128, 24)),
            alog=np.ascontiguousarray(d["L1_attn_A_log"][hsl].reshape(3, 1)),
            dtb=np.ascontiguousarray(d["L1_attn_dt_bias"][hsl].reshape(3, 1)),
        )
        maps.append(m)
    return maps


def run_B(d, x1fm):
    nc = build_B()
    res = run_bass_kernel_spmd(nc, prep_B(d, x1fm), core_ids=list(range(8)))
    return [np.concatenate([res.results[2 * b]["oB"],
                            res.results[2 * b + 1]["oB"]], axis=1)
            for b in range(4)]



def build_C():
    nc = bass.Bass("TRN2", target_bir_lowering=False)
    ctx = ExitStack()

    def din(name, shape, dt):
        return nc.dram_tensor(name, shape, dt, kind="ExternalInput").ap()

    x1 = din("x1", [2048, 1024], F32)
    ofm = din("ofm", [1536, 1024], BF)
    woT = din("woT", [1536, 2048], F32)
    upWT = din("upWT", [2048, 8192], F32)
    dnWT = din("dnWT", [8192, 2048], F32)
    upsc = din("upsc", [128, 64], F32)
    dnsc = din("dnsc", [128, 16], F32)
    osc = din("osc", [1, 1], F32)
    ones_c16 = din("ones_c16", [128, 1], F16)
    ones_r32 = din("ones_r32", [1, 128], F32)
    cpool = din("cpool", [128, 8], F32)
    x2out = nc.dram_tensor("x2out", [2048, 1024], F32, kind="ExternalOutput").ap()

    A = Arena(nc, 202 * 1024)
    S = Sched(nc, ctx)
    ps = {k: ctx.enter_context(nc.psum_tensor("ps" + k, [128, 512], F32))
          for k in "ABCD"}
    pt = {k: Track() for k in ps}

    def do(eng, emit, reads=(), writes=(), inc=1, sreads=()):
        after = []
        fafter = []
        for r in reads:
            after += r.raw
        for r in sreads:
            fafter += r.raw
        for w in writes:
            after += w.war
        tok = S.op(eng, emit, after=after, inc=inc, fafter=fafter)
        for r in reads:
            r.read(tok)
        for r in sreads:
            r.read(tok)
        for w in writes:
            w.write(tok)
        return tok

    def dodma(eng, out_ap, in_ap, reads=(), writes=(), **kw):
        after = []
        for r in reads:
            after += r.raw
        for w in writes:
            after += w.war
        tok = S.dma(eng, out_ap, in_ap, after=after, **kw)
        for r in reads:
            r.read(tok)
        for w in writes:
            w.write(tok)
        return tok

    def sbuf(shape, dt, name=None):
        return A.buf(shape, dt, name), Track()

    def sbufs(n, shape, dt, name):
        hs = A.bufs(n, shape, dt, name)
        return hs, [Track() for _ in hs]

    cp_t, cp_k = sbuf([128, 8], F32, "cp_t")
    dodma("sync", cp_t[:], cpool, writes=[cp_k])

    def cp(j, P=128):
        return cp_t[0:P, j:j + 1]

    upsc_t, upsc_k = sbuf([128, 64], F32, "upsc_t")
    dodma("sync", upsc_t[:], upsc, writes=[upsc_k])
    dnsc_t, dnsc_k = sbuf([128, 16], F32, "dnsc_t")
    dodma("sync", dnsc_t[:], dnsc, writes=[dnsc_k])
    osc_t, osc_k = sbuf([1, 1], F32, "osc_t")
    dodma("sync", osc_t[:], osc, writes=[osc_k])
    oc16_t, oc16_k = sbuf([128, 1], F16, "oc16")
    do("vector", lambda e: e.memset(oc16_t[:], 1.0), writes=[oc16_k])
    or32_t, or32_k = sbuf([1, 128], F32, "or32")
    dodma("sync", or32_t[:], ones_r32, writes=[or32_k])
    oscb, oscb_k = sbuf([128, 1], F32, "oscb")
    do("tensor", lambda e: e.matmul(ps["D"][:, 0:1], or32_t[:], osc_t[:],
                                    start=True, stop=True),
        reads=[or32_k, osc_k], writes=[pt["D"]])
    do("vector", lambda e: e.tensor_copy(oscb[:], ps["D"][:, 0:1]),
        reads=[pt["D"]], writes=[oscb_k])
    dn2sc, dn2sc_k = sbuf([128, 16], F32, "dn2sc")
    do("vector", lambda e: e.tensor_scalar_mul(dn2sc[:], dnsc_t[:],
                                               oscb[:, 0:1]),
        reads=[dnsc_k], sreads=[oscb_k], writes=[dn2sc_k])

    x1f, x1f_k = sbufs(16, [128, 1024], F32, "x1f")
    for fc in range(16):
        dodma("sync", x1f[fc][:], x1[fc * 128:(fc + 1) * 128, :],
              writes=[x1f_k[fc]])
    sum_row, sum_k = sbuf([1, 1024], F32, "sum_row")
    ss_row, ss_k = sbuf([1, 1024], F32, "ss_row")
    m_row, m_k = sbuf([1, 1024], F32, "m_row")
    v_row, v_k = sbuf([1, 1024], F32, "v_row")
    r_row, r_k = sbuf([1, 1024], F32, "r_row")
    sq, sq_k = sbufs(2, [128, 512], F16, "sq")
    x1h, x1h_k = sbufs(2, [128, 512], F16, "x1h")
    for nt in range(2):
        for fc in range(16):
            r2 = fc % 2
            do("vector", lambda e, fc=fc, nt=nt, r2=r2: e.tensor_copy(
                x1h[r2][:], x1f[fc][:, nt * 512:(nt + 1) * 512]),
                reads=[x1f_k[fc]], writes=[x1h_k[r2]])
            do("scalar", lambda e, r2=r2: e.activation(
                sq[r2][:], x1h[r2][:], AF.Square),
                reads=[x1h_k[r2]], writes=[sq_k[r2]])
            do("tensor", lambda e, r2=r2, st=(fc == 0), sp=(fc == 15): e.matmul(
                ps["A"][0:1, :], oc16_t[:], x1h[r2][:], start=st, stop=sp),
                reads=[x1h_k[r2], oc16_k], writes=[pt["A"]])
            do("tensor", lambda e, r2=r2, st=(fc == 0), sp=(fc == 15): e.matmul(
                ps["B"][0:1, :], oc16_t[:], sq[r2][:], start=st, stop=sp),
                reads=[sq_k[r2], oc16_k], writes=[pt["B"]])
        do("vector", lambda e, nt=nt: e.tensor_copy(
            sum_row[0:1, nt * 512:(nt + 1) * 512], ps["A"][0:1, :]),
            reads=[pt["A"]], writes=[sum_k])
        do("vector", lambda e, nt=nt: e.tensor_copy(
            ss_row[0:1, nt * 512:(nt + 1) * 512], ps["B"][0:1, :]),
            reads=[pt["B"]], writes=[ss_k])
    do("vector", lambda e: e.tensor_scalar_mul(m_row[:], sum_row[:], 1.0 / 2048),
        reads=[sum_k], writes=[m_k])
    do("vector", lambda e: e.tensor_scalar_mul(v_row[:], ss_row[:], 1.0 / 2048),
        reads=[ss_k], writes=[v_k])
    do("vector", lambda e: e.scalar_tensor_tensor(
        r_row[:], m_row[:], -1.0, m_row[:], ALU.mult, ALU.mult),
        reads=[m_k], writes=[r_k])
    do("vector", lambda e: e.tensor_add(ss_row[:], v_row[:], r_row[:]),
        reads=[v_k, r_k], writes=[ss_k])
    do("scalar", lambda e: e.activation(v_row[:], ss_row[:], AF.Sqrt,
                                        bias=cp(0, 1)),
        reads=[ss_k, cp_k], writes=[v_k])
    do("vector", lambda e: e.reciprocal(r_row[:], v_row[:]),
        reads=[v_k], writes=[r_k])
    M_bc, M_k = sbuf([128, 1024], F32, "M_bc")
    R_bc, R_k = sbuf([128, 1024], F32, "R_bc")
    for nt in range(2):
        do("tensor", lambda e, nt=nt: e.matmul(
            ps["C"][:, :], or32_t[:], m_row[0:1, nt * 512:(nt + 1) * 512],
            start=True, stop=True),
            reads=[or32_k, m_k], writes=[pt["C"]])
        do("vector", lambda e, nt=nt: e.tensor_copy(
            M_bc[:, nt * 512:(nt + 1) * 512], ps["C"][:, :]),
            reads=[pt["C"]], writes=[M_k])
        do("tensor", lambda e, nt=nt: e.matmul(
            ps["D"][:, :], or32_t[:], r_row[0:1, nt * 512:(nt + 1) * 512],
            start=True, stop=True),
            reads=[or32_k, r_k], writes=[pt["D"]])
        do("vector", lambda e, nt=nt: e.tensor_copy(
            R_bc[:, nt * 512:(nt + 1) * 512], ps["D"][:, :]),
            reads=[pt["D"]], writes=[R_k])
    xn1, xn1_k = sbufs(16, [128, 1024], BF, "xn1")
    dtmp, dtmp_k = sbufs(2, [128, 1024], F32, "dtmp")
    for fc in range(16):
        r = fc % 2
        do("vector", lambda e, fc=fc, r=r: e.tensor_tensor(
            dtmp[r][:], x1f[fc][:], M_bc[:], ALU.subtract),
            reads=[x1f_k[fc], M_k], writes=[dtmp_k[r]])
        do("vector", lambda e, fc=fc, r=r: e.tensor_tensor(
            xn1[fc][:], dtmp[r][:], R_bc[:], ALU.mult),
            reads=[dtmp_k[r], R_k], writes=[xn1_k[fc]])
    S.barrier()
    A.free(x1f, sum_row, ss_row, m_row, v_row, r_row, sq, x1h, M_bc, R_bc, dtmp)

    hid, hid_k = sbufs(64, [128, 1024], BF, "hid")
    upw_t, upw_tk = sbufs(2, [128, 2048], BF, "upw_t")
    upw_s, upw_sk = sbufs(2, [128, 2048], BF, "upw_s")
    for oc in range(64):
        r = oc % 2
        dodma("gpsimd", upw_t[r][:].rearrange("p (c m) -> p c m", m=128),
              upWT[:, oc * 128:(oc + 1) * 128].rearrange(
                  "(c p) m -> p c m", p=128), writes=[upw_tk[r]])
        do("scalar", lambda e, r=r: e.activation(
            upw_s[r][:], upw_t[r][:], AF.Sign),
            reads=[upw_tk[r]], writes=[upw_sk[r]])
        for n2 in range(2):
            for ic in range(16):
                do("tensor", lambda e, r=r, ic=ic, n2=n2, st=(ic == 0),
                   sp=(ic == 15): e.matmul(
                       ps["A"][:, :], upw_s[r][:, ic * 128:(ic + 1) * 128],
                       xn1[ic][:, n2 * 512:(n2 + 1) * 512], start=st, stop=sp),
                    reads=[xn1_k[ic], upw_sk[r]], writes=[pt["A"]])
            do("scalar", lambda e, r=r, oc=oc, n2=n2: e.activation(
                hid[oc][:, n2 * 512:(n2 + 1) * 512], ps["A"][:, :], AF.Relu,
                scale=upsc_t[:, oc:oc + 1]),
                reads=[pt["A"]], sreads=[upsc_k], writes=[hid_k[oc]])
    S.barrier()
    A.free(xn1, upw_t, upw_s)

    ofm_t, ofm_k = sbufs(12, [128, 1024], BF, "ofm_t")
    for ic in range(12):
        dodma("sync", ofm_t[ic][:], ofm[ic * 128:(ic + 1) * 128, :],
              writes=[ofm_k[ic]])
    s1h, s1h_k = sbufs(16, [128, 1024], F16, "s1h")
    wot_t, wot_k = sbufs(2, [128, 1536], BF, "wot_t")
    for oc in range(16):
        r = oc % 2
        dodma("gpsimd", wot_t[r][:].rearrange("p (c m) -> p c m", m=128),
              woT[:, oc * 128:(oc + 1) * 128].rearrange(
                  "(c p) m -> p c m", p=128), writes=[wot_k[r]])
        for n2 in range(2):
            for ic in range(12):
                do("tensor", lambda e, r=r, ic=ic, n2=n2, st=(ic == 0),
                   sp=(ic == 11): e.matmul(
                       ps["B"][:, :], wot_t[r][:, ic * 128:(ic + 1) * 128],
                       ofm_t[ic][:, n2 * 512:(n2 + 1) * 512], start=st, stop=sp),
                    reads=[ofm_k[ic], wot_k[r]], writes=[pt["B"]])
            do("vector", lambda e, oc=oc, n2=n2: e.tensor_copy(
                s1h[oc][:, n2 * 512:(n2 + 1) * 512], ps["B"][:, :]),
                reads=[pt["B"]], writes=[s1h_k[oc]])
    S.barrier()
    A.free(ofm_t, wot_t)

    dnw_t, dnw_tk = sbuf([128, 2048], BF, "dnw_t")
    dnw_s, dnw_sk = sbuf([128, 8192], BF, "dnw_s")
    g1b, g1b_k = sbufs(2, [128, 512], F32, "g1b")
    x1s, x1s_k = sbufs(2, [128, 512], F32, "x1s")
    t_b, t_k = sbufs(2, [128, 512], F32, "t_b")
    u_b, u_k = sbufs(2, [128, 512], F32, "u_b")
    for oc in range(16):
        for q4 in range(4):
            dodma("gpsimd",
                  dnw_t[:].rearrange("p (c m) -> p c m", m=128),
                  dnWT[q4 * 2048:(q4 + 1) * 2048,
                       oc * 128:(oc + 1) * 128].rearrange(
                      "(c p) m -> p c m", p=128),
                  writes=[dnw_tk])
            do("scalar", lambda e, q4=q4: e.activation(
                dnw_s[:, q4 * 2048:(q4 + 1) * 2048], dnw_t[:], AF.Sign),
                reads=[dnw_tk], writes=[dnw_sk])
        for n2 in range(2):
            b = "A" if n2 == 0 else "B"
            for ic in range(64):
                do("tensor", lambda e, b=b, ic=ic, n2=n2, st=(ic == 0),
                   sp=(ic == 63): e.matmul(
                       ps[b][:, :], dnw_s[:, ic * 128:(ic + 1) * 128],
                       hid[ic][:, n2 * 512:(n2 + 1) * 512], start=st, stop=sp),
                    reads=[hid_k[ic], dnw_sk], writes=[pt[b]])
            do("scalar", lambda e, b=b, oc=oc, n2=n2: e.activation(
                g1b[n2][:], ps[b][:, :], AF.Silu, scale=dn2sc[:, oc:oc + 1]),
                reads=[pt[b]], sreads=[dn2sc_k], writes=[g1b_k[n2]])
            dodma("sync", x1s[n2][:],
                  x1[oc * 128:(oc + 1) * 128, n2 * 512:(n2 + 1) * 512],
                  writes=[x1s_k[n2]])
            do("vector", lambda e, oc=oc, n2=n2: e.tensor_tensor(
                t_b[n2][:], s1h[oc][:, n2 * 512:(n2 + 1) * 512], g1b[n2][:],
                ALU.mult),
                reads=[s1h_k[oc], g1b_k[n2]], writes=[t_k[n2]])
            do("vector", lambda e, n2=n2: e.tensor_tensor(
                u_b[n2][:], t_b[n2][:], x1s[n2][:], ALU.add),
                reads=[t_k[n2], x1s_k[n2]], writes=[u_k[n2]])
            dodma("sync", x2out[oc * 128:(oc + 1) * 128,
                                  n2 * 512:(n2 + 1) * 512],
                  u_b[n2][:], reads=[u_k[n2]])
    for _i in range(Sched.NVQ):
        S.wait("gpsimd", (f"gpsimd_d{_i}", S.cnt[f"gpsimd_d{_i}"]))
        S.wait("sync", (f"sync_d{_i}", S.cnt[f"sync_d{_i}"]))
    with nc.allow_low_precision("fp16/bf16 intermediates are deliberate"):
        with nc.Block() as block:
            S.build(block)
    ctx.close()
    return nc


def prep_C(d, x1fm, o_tm):
    shared = dict(
        woT=np.ascontiguousarray(d["L1_attn_Wo"].T),
        upWT=np.ascontiguousarray(d["L1_ffn_up_weight"].T),
        dnWT=np.ascontiguousarray(d["L1_ffn_down_weight"].T),
        upsc=np.ascontiguousarray(d["L1_ffn_up_scale"].reshape(64, 128).T),
        dnsc=np.ascontiguousarray(d["L1_ffn_down_scale"].reshape(16, 128).T),
        osc=d["L1_ffn_output_scale"].reshape(1, 1).astype(np.float32),
        ones_c16=np.ones((128, 1), np.float16),
        ones_r32=np.ones((1, 128), np.float32),
        cpool=CPOOL,
    )
    maps = []
    for c in range(8):
        b, half = c // 2, c % 2
        t0 = half * 1024
        ofm_full = np.ascontiguousarray(o_tm[b].T)
        m = dict(shared)
        m.update(
            x1=np.ascontiguousarray(x1fm[b][:, t0:t0 + 1024]),
            ofm=np.ascontiguousarray(ofm_full[:, t0:t0 + 1024]),
        )
        maps.append(m)
    return maps


def run_C(d, x1fm, o_tm):
    nc = build_C()
    res = run_bass_kernel_spmd(nc, prep_C(d, x1fm, o_tm), core_ids=list(range(8)))
    out = np.zeros((4, 2048, 2048), np.float32)
    for c in range(8):
        b, half = c // 2, c % 2
        out[b, half * 1024:(half + 1) * 1024, :] = res.results[c]["x2out"].T
    return out


def kernel(**inputs):
    x1fm = run_A(inputs)
    oBs = run_B(inputs, x1fm)
    return run_C(inputs, x1fm, oBs)


# version 04
# speedup vs baseline: 1.1811x; 1.1811x over previous
import numpy as np
import ml_dtypes
from contextlib import ExitStack

import concourse.bass as bass
import concourse.mybir as mybir
from concourse.bass_utils import run_bass_kernel_spmd

BF16 = ml_dtypes.bfloat16
F32 = mybir.dt.float32
F16 = mybir.dt.float16
BF = mybir.dt.bfloat16
AF = mybir.ActivationFunctionType
ALU = mybir.AluOpType

ENGINES = ("sync", "gpsimd", "scalar", "vector", "tensor")
ARENA_NOFREE = [False]
DMA_ENGINES = ("sync", "gpsimd", "scalar", "vector")


class Sched:

    MAXOUT = 1
    NVQ = 8

    def __init__(self, nc, ctx):
        self.nc = nc
        self.keys = list(ENGINES) + [f"{e}_d{i}" for e in DMA_ENGINES
                                     for i in range(self.NVQ)]
        self.sem = {k: ctx.enter_context(nc.semaphore("S_" + k)) for k in self.keys}
        self.cnt = {k: 0 for k in self.keys}
        self.vq = {e: 0 for e in DMA_ENGINES}
        self.ops = {e: [] for e in ENGINES}

    def op(self, eng, emit, after=(), inc=1, fafter=()):
        waits = []
        for t in fafter:
            if t is None:
                continue
            k, c = t
            if self.isdma(k):
                c = self.cnt[k]
            waits.append((k, c))
        for t in after:
            if t is None:
                continue
            k, c = t
            if k == eng == "tensor":
                continue
            if self.isdma(k):
                c = self.cnt[k]
            waits.append((k, c))
        self.cnt[eng] += inc
        self.ops[eng].append((emit, eng, inc, waits))
        return (eng, self.cnt[eng])

    def dma(self, eng, out, in_, after=(), **kw):
        key = f"{eng}_d{self.vq[eng]}"
        self.vq[eng] = (self.vq[eng] + 1) % self.NVQ
        waits = []
        for t in after:
            if t is None:
                continue
            k, c = t
            if self.isdma(k) and k != key:
                c = self.cnt[k]
            waits.append((k, c))
        waits = [(k, c) for (k, c) in waits if k != eng and k != key]
        n = self.cnt[key] // 16 + 1
        if n > self.MAXOUT:
            waits.append((key, 16 * (n - self.MAXOUT)))
        self.cnt[key] += 16

        def emit(e, out=out, in_=in_, kw=kw):
            return e.dma_start(out=out, in_=in_, **kw)

        self.ops[eng].append((emit, key, 16, waits))
        return (key, self.cnt[key])

    @staticmethod
    def isdma(k):
        return "_d" in k

    def wait(self, eng, tok):
        k, c = tok
        self.ops[eng].append((None, eng, 0, [(k, c)]))

    def barrier(self):
        snap = [(k, self.cnt[k]) for k in self.keys if self.cnt[k] > 0]
        for e in ENGINES:
            waits = [(k, c) for (k, c) in snap if k != e and not k.startswith(e + "_d")]
            self.ops[e].append((None, e, 0, waits))

    def build(self, block):
        for e in ENGINES:
            ops = self.ops[e]

            def body(eng, ops=ops):
                wm = {k: 0 for k in self.keys}
                for (emit, key, inc, waits) in ops:
                    for (wk, wc) in waits:
                        if wc > wm[wk]:
                            eng.wait_ge(self.sem[wk], wc)
                            wm[wk] = wc
                    if emit is not None:
                        inst = emit(eng)
                        if inc:
                            inst.then_inc(self.sem[key], inc)
            getattr(block, e)(body)


class Arena:

    def __init__(self, nc, nbytes):
        self.nc = nc
        nb = nbytes * (8 if ARENA_NOFREE[0] else 1)
        self.h = nc.alloc_sbuf_tensor("ARENA", [128, nb], mybir.dt.uint8)
        self.base = nc.lookup_mloc(self.h).addr
        self.nbytes = nb
        self.flist = [(0, nb)]
        self.sizes = {}
        self.n = 0
        self.hwm = 0

    def _take(self, per):
        for i, (off, sz) in enumerate(self.flist):
            if sz >= per:
                if sz == per:
                    self.flist.pop(i)
                else:
                    self.flist[i] = (off + per, sz - per)
                self.hwm = max(self.hwm, off + per)
                return off
        raise AssertionError(
            f"SBUF arena overflow: need {per}B, free={self.flist}")

    def buf(self, shape, dtype, name=None):
        per = int(np.prod(shape[1:])) * mybir.dt.size(dtype)
        per = (per + 31) // 32 * 32
        off = self._take(per)
        self.n += 1
        t = self.nc.alloc_sbuf_tensor_at(
            name or f"ab{self.n}", shape, dtype, offset=self.base + off)
        self.sizes[t.name] = (off, per)
        return t

    def bufs(self, n, shape, dtype, name=None):
        return [self.buf(shape, dtype, f"{name}{i}" if name else None)
                for i in range(n)]

    def free(self, *handles):
        if ARENA_NOFREE[0]:
            return
        for h in handles:
            for hh in (h if isinstance(h, (list, tuple)) else [h]):
                off, per = self.sizes.pop(hh.name)
                self.flist.append((off, per))
        self.flist.sort()
        merged = []
        for off, sz in self.flist:
            if merged and merged[-1][0] + merged[-1][1] == off:
                merged[-1] = (merged[-1][0], merged[-1][1] + sz)
            else:
                merged.append((off, sz))
        self.flist = [tuple(x) for x in merged]


class Track:

    def __init__(self):
        self.w = None
        self.r = []

    def write(self, tok):
        self.w = tok
        self.r = []

    def read(self, tok):
        self.r.append(tok)

    @property
    def war(self):
        return list(self.r) + ([self.w] if self.w else [])

    @property
    def raw(self):
        return [self.w] if self.w else []


def np2bir(a):
    d = str(a.dtype)
    return {"float32": F32, "float16": F16, "bfloat16": BF}[d]


def hadamard(n):
    H = np.array([[1.0]], dtype=np.float32)
    while H.shape[0] < n:
        H = np.block([[H, H], [H, -H]])
    return H.astype(np.float32)


def fwht_pairs(nc_, stage):
    s = 1 << stage
    return [(c, c + s) for c in range(nc_) if (c & s) == 0]



def build_A(debug=False):
    nc = bass.Bass("TRN2", target_bir_lowering=False)
    ctx = ExitStack()

    def din(name, shape, dt):
        return nc.dram_tensor(name, shape, dt, kind="ExternalInput").ap()

    x = din("x", [2048, 2048], F16)
    wq = din("wq", [2048, 2048], F16)
    wk = din("wk", [2048, 512], F16)
    wv = din("wv", [2048, 512], F16)
    wo = din("wo", [2048, 2048], F16)
    wupT = din("wupT", [2048, 8192], F32)
    wgT = din("wgT", [2048, 8192], F32)
    wdT = din("wdT", [8192, 2048], F32)
    su_ug = din("su_ug", [128, 16], F32)
    su_dp = din("su_dp", [128, 64], F32)
    wsc_u = din("wsc_u", [128, 64], F32)
    wsc_g = din("wsc_g", [128, 64], F32)
    wsc_dp = din("wsc_dp", [128, 16], F32)
    h128 = din("h128", [128, 128], BF)
    ones_c16 = din("ones_c16", [128, 1], F16)
    ones_r32 = din("ones_r32", [1, 128], F32)
    ones64 = din("ones64", [1, 64], F16)
    cpool = din("cpool", [128, 8], F32)
    x1out = nc.dram_tensor("x1out", [2048, 1024], F32, kind="ExternalOutput").ap()
    had = nc.dram_tensor("had", [64, 128, 1024], BF,
                         kind="ExternalOutput" if debug else "Internal").ap()
    if debug:
        s0out = nc.dram_tensor("s0out", [2048, 1024], F16, kind="ExternalOutput").ap()
        zout = nc.dram_tensor("zout", [16, 128, 1024], BF, kind="ExternalOutput").ap()
        xnout = nc.dram_tensor("xnout", [128, 2048], F16, kind="ExternalOutput").ap()
        h2out = nc.dram_tensor("h2out", [64, 128, 1024], BF, kind="ExternalOutput").ap()

    A = Arena(nc, 202 * 1024)
    S = Sched(nc, ctx)

    ps = {k: ctx.enter_context(nc.psum_tensor("ps" + k, [128, 512], F32))
          for k in "ABCDE"}
    pt = {k: Track() for k in ps}

    def do(eng, emit, reads=(), writes=(), inc=1, sreads=()):
        after = []
        fafter = []
        for r in reads:
            after += r.raw
        for r in sreads:
            fafter += r.raw
        for w in writes:
            after += w.war
        tok = S.op(eng, emit, after=after, inc=inc, fafter=fafter)
        for r in reads:
            r.read(tok)
        for r in sreads:
            r.read(tok)
        for w in writes:
            w.write(tok)
        return tok

    def dodma(eng, out_ap, in_ap, reads=(), writes=(), **kw):
        after = []
        for r in reads:
            after += r.raw
        for w in writes:
            after += w.war
        tok = S.dma(eng, out_ap, in_ap, after=after, **kw)
        for r in reads:
            r.read(tok)
        for w in writes:
            w.write(tok)
        return tok

    def sbuf(shape, dt, name=None):
        return A.buf(shape, dt, name), Track()

    def sbufs(n, shape, dt, name):
        hs = A.bufs(n, shape, dt, name)
        return hs, [Track() for _ in hs]

    cp_t, cp_k = sbuf([128, 8], F32, "cp_t")
    dodma("sync", cp_t[:], cpool, writes=[cp_k])

    def cp(j, P=128):
        return cp_t[0:P, j:j + 1]
    su_ug_t, su_ug_k = sbuf([128, 16], F32, "su_ug_t")
    su_dp_t, su_dp_k = sbuf([128, 64], F32, "su_dp_t")
    wsc_u_t, wsc_u_k = sbuf([128, 64], F32, "wsc_u_t")
    wsc_g_t, wsc_g_k = sbuf([128, 64], F32, "wsc_g_t")
    wsc_dp_t, wsc_dp_k = sbuf([128, 16], F32, "wsc_dp_t")
    colB_t, colB_k = sbuf([128, 64], F32, "colB")
    colG_t, colG_k = sbuf([128, 64], F32, "colG")
    colD_t, colD_k = sbuf([128, 16], F32, "colD")
    h128_t, h128_k = sbuf([128, 128], BF, "h128")
    oc16_t, oc16_k = sbuf([128, 1], F16, "oc16")
    or32_t, or32_k = sbuf([1, 128], F32, "or32")
    o64_t, o64_k = sbuf([1, 64], F16, "o64")
    for (dst, k_, src) in [(su_ug_t, su_ug_k, su_ug), (su_dp_t, su_dp_k, su_dp),
                           (wsc_u_t, wsc_u_k, wsc_u), (wsc_g_t, wsc_g_k, wsc_g),
                           (wsc_dp_t, wsc_dp_k, wsc_dp), (h128_t, h128_k, h128),
                           (oc16_t, oc16_k, ones_c16), (or32_t, or32_k, ones_r32),
                           (o64_t, o64_k, ones64)]:
        dodma("sync", dst[:], src, writes=[k_])
    K1 = float(2048.0 ** -0.5)
    K2 = float(8192.0 ** -0.5)
    do("vector", lambda e: e.scalar_tensor_tensor(
        colB_t[:], wsc_u_t[:], K1, su_dp_t[:], ALU.mult, ALU.mult),
        reads=[wsc_u_k, su_dp_k], writes=[colB_k])
    do("vector", lambda e: e.tensor_scalar_mul(colG_t[:], wsc_g_t[:], K1),
        reads=[wsc_g_k], writes=[colG_k])
    do("vector", lambda e: e.tensor_scalar_mul(colD_t[:], wsc_dp_t[:], K2),
        reads=[wsc_dp_k], writes=[colD_k])

    sum_row, sum_k = sbuf([1, 2048], F32, "sum_row")
    ss_row, ss_k = sbuf([1, 2048], F32, "ss_row")
    r_row, r_k = sbuf([1, 2048], F32, "r_row")
    M_bc, M_k = sbuf([128, 2048], F32, "M_bc")
    R_bc, R_k = sbuf([128, 2048], F32, "R_bc")
    xb, xb_k = sbufs(3, [128, 512], F16, "xb")
    sq, sq_k = sbufs(2, [128, 512], F16, "sq")
    for nt in range(4):
        for fc in range(16):
            r3 = fc % 3
            r2 = fc % 2
            dodma("sync", xb[r3][:],
                  x[nt * 512:(nt + 1) * 512, fc * 128:(fc + 1) * 128],
                  writes=[xb_k[r3]], transpose=True)
            do("scalar", lambda e, a=sq[r2], b=xb[r3]: e.activation(
                a[:], b[:], AF.Square),
                reads=[xb_k[r3]], writes=[sq_k[r2]])
            do("tensor", lambda e, b=xb[r3], st=(fc == 0), sp=(fc == 15): e.matmul(
                ps["A"][0:1, :], oc16_t[:], b[:], start=st, stop=sp),
                reads=[xb_k[r3], oc16_k], writes=[pt["A"]])
            do("tensor", lambda e, b=sq[r2], st=(fc == 0), sp=(fc == 15): e.matmul(
                ps["B"][0:1, :], oc16_t[:], b[:], start=st, stop=sp),
                reads=[sq_k[r2], oc16_k], writes=[pt["B"]])
        do("vector", lambda e, nt=nt: e.tensor_copy(
            sum_row[0:1, nt * 512:(nt + 1) * 512], ps["A"][0:1, :]),
            reads=[pt["A"]], writes=[sum_k])
        do("vector", lambda e, nt=nt: e.tensor_copy(
            ss_row[0:1, nt * 512:(nt + 1) * 512], ps["B"][0:1, :]),
            reads=[pt["B"]], writes=[ss_k])
    m_row, m_k = sbuf([1, 2048], F32, "m_row")
    v_row, v_k = sbuf([1, 2048], F32, "v_row")
    do("vector", lambda e: e.tensor_scalar_mul(m_row[:], sum_row[:], 1.0 / 2048),
        reads=[sum_k], writes=[m_k])
    do("vector", lambda e: e.tensor_scalar_mul(v_row[:], ss_row[:], 1.0 / 2048),
        reads=[ss_k], writes=[v_k])
    do("vector", lambda e: e.scalar_tensor_tensor(
        r_row[:], m_row[:], -1.0, m_row[:], ALU.mult, ALU.mult),
        reads=[m_k], writes=[r_k])
    do("vector", lambda e: e.tensor_add(ss_row[:], v_row[:], r_row[:]),
        reads=[v_k, r_k], writes=[ss_k])
    do("scalar", lambda e: e.activation(v_row[:], ss_row[:], AF.Sqrt, bias=cp(0, 1)),
        reads=[ss_k, cp_k], writes=[v_k])
    do("vector", lambda e: e.reciprocal(r_row[:], v_row[:]),
        reads=[v_k], writes=[r_k])
    for nt in range(4):
        do("tensor", lambda e, nt=nt: e.matmul(
            ps["C"][:, :], or32_t[:], m_row[0:1, nt * 512:(nt + 1) * 512],
            start=True, stop=True),
            reads=[or32_k, m_k], writes=[pt["C"]])
        do("vector", lambda e, nt=nt: e.tensor_copy(
            M_bc[:, nt * 512:(nt + 1) * 512], ps["C"][:, :]),
            reads=[pt["C"]], writes=[M_k])
        do("tensor", lambda e, nt=nt: e.matmul(
            ps["D"][:, :], or32_t[:], r_row[0:1, nt * 512:(nt + 1) * 512],
            start=True, stop=True),
            reads=[or32_k, r_k], writes=[pt["D"]])
        do("vector", lambda e, nt=nt: e.tensor_copy(
            R_bc[:, nt * 512:(nt + 1) * 512], ps["D"][:, :]),
            reads=[pt["D"]], writes=[R_k])
    S.barrier()
    A.free(xb, sq, sum_row, ss_row, r_row, m_row, v_row)

    xn, xn_k = sbufs(16, [128, 2048], F16, "xn")
    zp, zp_k = sbufs(18, [128, 1024], BF, "z")
    xt2, xt2_k = sbufs(2, [128, 2048], F16, "xt2")
    dtmp, dtmp_k = sbufs(2, [128, 2048], F16, "dtmp")
    zmap = list(range(16))
    spares = [16, 17]
    for fc in range(16):
        r = fc % 2
        dodma("sync", xt2[r][:], x[:, fc * 128:(fc + 1) * 128],
              writes=[xt2_k[r]], transpose=True)
        do("vector", lambda e, fc=fc, r=r: e.tensor_tensor(
            dtmp[r][:], xt2[r][:], M_bc[:], ALU.subtract),
            reads=[xt2_k[r], M_k], writes=[dtmp_k[r]])
        do("vector", lambda e, fc=fc, r=r: e.tensor_tensor(
            xn[fc][:], dtmp[r][:], R_bc[:], ALU.mult),
            reads=[dtmp_k[r], R_k], writes=[xn_k[fc]])
        do("vector", lambda e, fc=fc: e.tensor_scalar_mul(
            zp[fc][:], xn[fc][:, 0:1024], su_ug_t[:, fc:fc + 1]),
            reads=[xn_k[fc]], sreads=[su_ug_k], writes=[zp_k[fc]])
    for fc in range(16):
        src = zmap[fc]
        dst = spares[0]
        for n2 in range(2):
            b = "A" if n2 == 0 else "B"
            do("tensor", lambda e, s=src, n2=n2, b=b: e.matmul(
                ps[b][:, :], h128_t[:], zp[s][:, n2 * 512:(n2 + 1) * 512],
                start=True, stop=True),
                reads=[zp_k[src], h128_k], writes=[pt[b]])
            do("vector", lambda e, d=dst, n2=n2, b=b: e.tensor_copy(
                zp[d][:, n2 * 512:(n2 + 1) * 512], ps[b][:, :]),
                reads=[pt[b]], writes=[zp_k[dst]])
        zmap[fc] = dst
        spares[0] = src
    for st_ in range(4):
        for (a, b) in fwht_pairs(16, st_):
            pa, pb = zmap[a], zmap[b]
            na, nb = spares
            do("vector", lambda e, pa=pa, pb=pb, na=na: e.tensor_add(
                zp[na][:], zp[pa][:], zp[pb][:]),
                reads=[zp_k[pa], zp_k[pb]], writes=[zp_k[na]])
            do("vector", lambda e, pa=pa, pb=pb, nb=nb: e.tensor_tensor(
                zp[nb][:], zp[pa][:], zp[pb][:], ALU.subtract),
                reads=[zp_k[pa], zp_k[pb]], writes=[zp_k[nb]])
            zmap[a], zmap[b] = na, nb
            spares = [pa, pb]
    if debug:
        for fc in range(16):
            dodma("gpsimd", zout[fc], zp[zmap[fc]][:], reads=[zp_k[zmap[fc]]])
        dodma("gpsimd", xnout, xn[0][:], reads=[xn_k[0]])
    S.barrier()
    A.free(xt2, dtmp, M_bc, R_bc)

    wv_sb, wv_k = sbufs(16, [128, 512], F16, "wv_sb")
    for fc in range(16):
        dodma("sync", wv_sb[fc][:], wv[fc * 128:(fc + 1) * 128, :],
              writes=[wv_k[fc]])
    v_tm, v_k = sbufs(16, [128, 520], F16, "v_tm")
    for tt in range(16):
        for fc in range(16):
            do("tensor", lambda e, tt=tt, fc=fc: e.matmul(
                ps["A"][:, :], xn[fc][:, tt * 128:(tt + 1) * 128], wv_sb[fc][:],
                start=(fc == 0), stop=(fc == 15)),
                reads=[xn_k[fc], wv_k[fc]], writes=[pt["A"]])
        do("vector", lambda e, tt=tt: e.tensor_copy(
            v_tm[tt][:].rearrange("p (g c) -> p g c", c=65)[:, :, 0:64],
            ps["A"][:, :].rearrange("p (g c) -> p g c", c=64)),
            reads=[pt["A"]], writes=[v_k[tt]])
        do("vector", lambda e, tt=tt: e.memset(
            v_tm[tt][:].rearrange("p (g c) -> p g c", c=65)[:, :, 64:65], 1.0),
            reads=[], writes=[v_k[tt]])
    S.barrier()
    A.free(wv_sb)
    wk_sb, wk_k = sbufs(16, [128, 512], F16, "wk_sb")
    for fc in range(16):
        dodma("sync", wk_sb[fc][:], wk[fc * 128:(fc + 1) * 128, :],
              writes=[wk_k[fc]])
    k_fm, k_k = sbufs(4, [128, 2048], F16, "k_fm")
    for oc in range(4):
        for n4 in range(4):
            for fc in range(16):
                do("tensor", lambda e, oc=oc, n4=n4, fc=fc: e.matmul(
                    ps["B"][:, :], wk_sb[fc][:, oc * 128:(oc + 1) * 128],
                    xn[fc][:, n4 * 512:(n4 + 1) * 512],
                    start=(fc == 0), stop=(fc == 15)),
                    reads=[xn_k[fc], wk_k[fc]], writes=[pt["B"]])
            do("vector", lambda e, oc=oc, n4=n4: e.tensor_copy(
                k_fm[oc][:, n4 * 512:(n4 + 1) * 512], ps["B"][:, :]),
                reads=[pt["B"]], writes=[k_k[oc]])
    k_dup, kd_k = sbufs(8, [128, 2048], F16, "k_dup")
    for g in range(8):
        kc, kr = g // 2, (g % 2) * 64
        for half in range(2):
            do("vector", lambda e, g=g, kc=kc, kr=kr, half=half: e.tensor_copy(
                k_dup[g][half * 64:half * 64 + 64, :], k_fm[kc][kr:kr + 64, :]),
                reads=[k_k[kc]], writes=[kd_k[g]])
    S.barrier()
    A.free(wk_sb, k_fm)
    q_fm, q_k = sbufs(16, [128, 1024], F16, "q_fm")
    wq_t, wq_k = sbufs(2, [128, 2048], F16, "wq_t")
    for oc in range(16):
        r = oc % 2
        dodma("sync", wq_t[r][:].rearrange("p (c m) -> p c m", m=128),
              wq[:, oc * 128:(oc + 1) * 128].rearrange("(c p) m -> p c m", p=128),
              writes=[wq_k[r]])
        for n2 in range(2):
            for fc in range(16):
                do("tensor", lambda e, r=r, fc=fc, n2=n2: e.matmul(
                    ps["C"][:, :], wq_t[r][:, fc * 128:(fc + 1) * 128],
                    xn[fc][:, n2 * 512:(n2 + 1) * 512],
                    start=(fc == 0), stop=(fc == 15)),
                    reads=[xn_k[fc], wq_k[r]], writes=[pt["C"]])
            do("vector", lambda e, oc=oc, n2=n2: e.tensor_copy(
                q_fm[oc][:, n2 * 512:(n2 + 1) * 512], ps["C"][:, :]),
                reads=[pt["C"]], writes=[q_k[oc]])
    S.barrier()
    A.free(xn, wq_t)

    o_fm, o_k = sbufs(16, [128, 1024], F16, "o_fm")
    eb, eb_k = sbufs(3, [128, 512], F16, "eb")
    rd_t, rd_k = sbuf([1, 512], F16, "rd")
    rb_t, rb_k = sbuf([64, 512], F16, "rb")
    for h in range(32):
        g = h // 4
        qc, qr = h // 2, (h % 2) * 64
        for tq2 in range(2):
            for tk in range(16):
                b = "A" if tk % 2 == 0 else "B"
                r3 = tk % 3
                do("tensor", lambda e, b=b, g=g, qc=qc, qr=qr,
                   tk=tk, tq2=tq2: e.matmul(
                       ps[b][:, :],
                       k_dup[g][qr:qr + 64, tk * 128:(tk + 1) * 128],
                       q_fm[qc][qr:qr + 64, tq2 * 512:(tq2 + 1) * 512],
                       start=True, stop=True),
                    reads=[kd_k[g], q_k[qc]], writes=[pt[b]])
                do("scalar", lambda e, b=b, r3=r3: e.activation(
                    eb[r3][:], ps[b][:, :], AF.Exp, bias=cp(2), scale=cp(1)),
                    reads=[pt[b], cp_k], writes=[eb_k[r3]])
                do("tensor", lambda e, r3=r3, tk=tk, g=g: e.matmul(
                    ps["D"][0:65, :], v_tm[tk][:, g * 65:(g + 1) * 65],
                    eb[r3][:], start=(tk == 0), stop=(tk == 15)),
                    reads=[eb_k[r3], v_k[tk]], writes=[pt["D"]])
            do("vector", lambda e: e.reciprocal(rd_t[:], ps["D"][64:65, :]),
                reads=[pt["D"]], writes=[rd_k])
            do("tensor", lambda e: e.matmul(
                ps["E"][0:64, :], o64_t[:], rd_t[:], start=True, stop=True),
                reads=[rd_k, o64_k], writes=[pt["E"]])
            do("vector", lambda e: e.tensor_copy(rb_t[:], ps["E"][0:64, :]),
                reads=[pt["E"]], writes=[rb_k])
            do("vector", lambda e, qc=qc, qr=qr, tq2=tq2: e.tensor_tensor(
                o_fm[qc][qr:qr + 64, tq2 * 512:(tq2 + 1) * 512],
                ps["D"][0:64, :], rb_t[:], ALU.mult),
                reads=[pt["D"], rb_k], writes=[o_k[qc]])
    S.barrier()
    A.free(q_fm, k_dup, v_tm, eb, rd_t, rb_t)

    wo_sb, wo_k = sbufs(16, [128, 2048], F16, "wo_sb")
    for ic in range(16):
        dodma("sync", wo_sb[ic][:], wo[ic * 128:(ic + 1) * 128, :],
              writes=[wo_k[ic]])
    s0_fm, s0_k = sbufs(16, [128, 1024], F16, "s0_fm")
    for oc in range(16):
        for n2 in range(2):
            for ic in range(16):
                do("tensor", lambda e, oc=oc, n2=n2, ic=ic: e.matmul(
                    ps["A"][:, :], wo_sb[ic][:, oc * 128:(oc + 1) * 128],
                    o_fm[ic][:, n2 * 512:(n2 + 1) * 512],
                    start=(ic == 0), stop=(ic == 15)),
                    reads=[o_k[ic], wo_k[ic]], writes=[pt["A"]])
            do("vector", lambda e, oc=oc, n2=n2: e.tensor_copy(
                s0_fm[oc][:, n2 * 512:(n2 + 1) * 512], ps["A"][:, :]),
                reads=[pt["A"]], writes=[s0_k[oc]])
    if debug:
        for oc in range(16):
            dodma("gpsimd", s0out[oc * 128:(oc + 1) * 128, :], s0_fm[oc][:],
                  reads=[s0_k[oc]])
    S.barrier()
    A.free(wo_sb, o_fm)

    wu_t, wu_k = sbufs(2, [128, 2048], BF, "wu_t")
    wg_t, wg_k = sbufs(2, [128, 2048], BF, "wg_t")
    sg_b, sg_k = sbufs(2, [128, 1024], BF, "sg_b")
    hh_b, hh_k = sbufs(2, [128, 1024], BF, "hh_b")
    ha_b, ha_k = sbufs(2, [128, 1024], BF, "ha_b")
    for oc in range(64):
        r = oc % 2
        dodma("gpsimd", wu_t[r][:].rearrange("p (c m) -> p c m", m=128),
              wupT[:, oc * 128:(oc + 1) * 128].rearrange("(c p) m -> p c m", p=128),
              writes=[wu_k[r]])
        dodma("gpsimd", wg_t[r][:].rearrange("p (c m) -> p c m", m=128),
              wgT[:, oc * 128:(oc + 1) * 128].rearrange("(c p) m -> p c m", p=128),
              writes=[wg_k[r]])
        for n2 in range(2):
            for ic in range(16):
                zt = zmap[ic]
                do("tensor", lambda e, r=r, ic=ic, n2=n2, zt=zt: e.matmul(
                    ps["A"][:, :], wu_t[r][:, ic * 128:(ic + 1) * 128],
                    zp[zt][:, n2 * 512:(n2 + 1) * 512],
                    start=(ic == 0), stop=(ic == 15)),
                    reads=[zp_k[zt], wu_k[r]], writes=[pt["A"]])
                do("tensor", lambda e, r=r, ic=ic, n2=n2, zt=zt: e.matmul(
                    ps["B"][:, :], wg_t[r][:, ic * 128:(ic + 1) * 128],
                    zp[zt][:, n2 * 512:(n2 + 1) * 512],
                    start=(ic == 0), stop=(ic == 15)),
                    reads=[zp_k[zt], wg_k[r]], writes=[pt["B"]])
            do("scalar", lambda e, r=r, oc=oc, n2=n2: e.activation(
                sg_b[r][:, n2 * 512:(n2 + 1) * 512], ps["B"][:, :], AF.Silu,
                scale=colG_t[:, oc:oc + 1]),
                reads=[pt["B"], colG_k], writes=[sg_k[r]])
            do("vector", lambda e, r=r, oc=oc, n2=n2: e.scalar_tensor_tensor(
                hh_b[r][:, n2 * 512:(n2 + 1) * 512], ps["A"][:, :],
                colB_t[:, oc:oc + 1], sg_b[r][:, n2 * 512:(n2 + 1) * 512],
                ALU.mult, ALU.mult),
                reads=[pt["A"], sg_k[r]], sreads=[colB_k], writes=[hh_k[r]])
            do("tensor", lambda e, r=r, n2=n2: e.matmul(
                ps["C"][:, :], h128_t[:],
                hh_b[r][:, n2 * 512:(n2 + 1) * 512], start=True, stop=True),
                reads=[hh_k[r], h128_k], writes=[pt["C"]])
            do("vector", lambda e, r=r, n2=n2: e.tensor_copy(
                ha_b[r][:, n2 * 512:(n2 + 1) * 512], ps["C"][:, :]),
                reads=[pt["C"]], writes=[ha_k[r]])
        dodma("sync", had[oc], ha_b[r][:], reads=[ha_k[r]])
    S.barrier()
    A.free(zp, wu_t, wg_t, sg_b, hh_b, ha_b)

    h2, h2_k = sbufs(66, [128, 1024], BF, "h2")
    h2map = list(range(64))
    h2sp = [64, 65]
    for ic in range(64):
        dodma("sync", h2[ic][:], had[ic], writes=[h2_k[ic]])
    for st_ in range(6):
        for (a, b) in fwht_pairs(64, st_):
            pa, pb = h2map[a], h2map[b]
            na, nb = h2sp
            do("vector", lambda e, pa=pa, pb=pb, na=na: e.tensor_add(
                h2[na][:], h2[pa][:], h2[pb][:]),
                reads=[h2_k[pa], h2_k[pb]], writes=[h2_k[na]])
            do("vector", lambda e, pa=pa, pb=pb, nb=nb: e.tensor_tensor(
                h2[nb][:], h2[pa][:], h2[pb][:], ALU.subtract),
                reads=[h2_k[pa], h2_k[pb]], writes=[h2_k[nb]])
            h2map[a], h2map[b] = na, nb
            h2sp = [pa, pb]
    if debug:
        for c64 in range(64):
            dodma("gpsimd", h2out[c64], h2[h2map[c64]][:],
                  reads=[h2_k[h2map[c64]]])
    wd_t, wd_k = sbuf([128, 8192], BF, "wd_t")
    g0_b, g0_k = sbufs(2, [128, 512], F32, "g0_b")
    xs_b, xs_k = sbufs(2, [128, 512], F16, "xs_b")
    t_b, t_k = sbufs(2, [128, 512], F32, "t_b")
    u_b, u_k = sbufs(2, [128, 512], F32, "u_b")
    for oc in range(16):
        for q4 in range(4):
            dodma("gpsimd",
                  wd_t[:, q4 * 2048:(q4 + 1) * 2048].rearrange(
                      "p (c m) -> p c m", m=128),
                  wdT[q4 * 2048:(q4 + 1) * 2048,
                      oc * 128:(oc + 1) * 128].rearrange(
                      "(c p) m -> p c m", p=128),
                  writes=[wd_k])
        for n2 in range(2):
            b = "A" if n2 == 0 else "B"
            for ic in range(64):
                ht = h2map[ic]
                do("tensor", lambda e, b=b, ic=ic, n2=n2, ht=ht: e.matmul(
                    ps[b][:, :], wd_t[:, ic * 128:(ic + 1) * 128],
                    h2[ht][:, n2 * 512:(n2 + 1) * 512],
                    start=(ic == 0), stop=(ic == 63)),
                    reads=[h2_k[ht], wd_k], writes=[pt[b]])
            do("scalar", lambda e, b=b, oc=oc, n2=n2: e.activation(
                g0_b[n2][:], ps[b][:, :], AF.Silu, scale=colD_t[:, oc:oc + 1]),
                reads=[pt[b], colD_k], writes=[g0_k[n2]])
            dodma("sync", xs_b[n2][:],
                  x[n2 * 512:(n2 + 1) * 512, oc * 128:(oc + 1) * 128],
                  writes=[xs_k[n2]], transpose=True)
            do("vector", lambda e, oc=oc, n2=n2: e.tensor_tensor(
                t_b[n2][:], s0_fm[oc][:, n2 * 512:(n2 + 1) * 512], g0_b[n2][:],
                ALU.mult),
                reads=[s0_k[oc], g0_k[n2]], writes=[t_k[n2]])
            do("vector", lambda e, n2=n2: e.tensor_tensor(
                u_b[n2][:], t_b[n2][:], xs_b[n2][:], ALU.add),
                reads=[t_k[n2], xs_k[n2]], writes=[u_k[n2]])
            dodma("sync", x1out[oc * 128:(oc + 1) * 128,
                                  n2 * 512:(n2 + 1) * 512],
                  u_b[n2][:], reads=[u_k[n2]])
    for _i in range(Sched.NVQ):
        S.wait("gpsimd", (f"gpsimd_d{_i}", S.cnt[f"gpsimd_d{_i}"]))
        S.wait("sync", (f"sync_d{_i}", S.cnt[f"sync_d{_i}"]))
    with nc.allow_low_precision("fp16/bf16 intermediates are deliberate"):
        with nc.Block() as block:
            S.build(block)
    ctx.close()
    return nc


CPOOL = np.zeros((128, 8), np.float32)
CPOOL[:, 0] = 1e-5
CPOOL[:, 1] = 0.125
CPOOL[:, 2] = -4.0
CPOOL[:, 3] = -1.0
CPOOL[:, 4] = 1.0
CPOOL[:, 5] = 1e-6
CPOOL[:, 6] = 1.0 / 256.0
CPOOL[:, 7] = 0.0


def prep_A(d):
    x = d["L0_attn_x"]
    shared = dict(
        wq=np.ascontiguousarray(d["L0_attn_Wq"]),
        wk=np.ascontiguousarray(d["L0_attn_Wk"]),
        wv=np.ascontiguousarray(d["L0_attn_Wv"]),
        wo=np.ascontiguousarray(d["L0_attn_Wo"]),
        wupT=np.ascontiguousarray(d["L0_ffn_W_up"].T),
        wgT=np.ascontiguousarray(d["L0_ffn_W_gate"].T),
        wdT=np.ascontiguousarray(d["L0_ffn_W_down"].T),
        su_ug=np.ascontiguousarray(d["L0_ffn_SU_ug"].reshape(16, 128).T),
        su_dp=np.ascontiguousarray(d["L0_ffn_SU_dp"].reshape(64, 128).T),
        wsc_u=np.ascontiguousarray(
            d["L0_ffn_Wscale_ug"][:8192].reshape(64, 128).T),
        wsc_g=np.ascontiguousarray(
            d["L0_ffn_Wscale_ug"][8192:].reshape(64, 128).T),
        wsc_dp=np.ascontiguousarray(d["L0_ffn_Wscale_dp"].reshape(16, 128).T),
        h128=hadamard(128).astype(BF16),
        ones_c16=np.ones((128, 1), np.float16),
        ones_r32=np.ones((1, 128), np.float32),
        ones64=np.ones((1, 64), np.float16),
        cpool=CPOOL,
    )
    maps = []
    for c in range(8):
        b, half = c // 2, c % 2
        xr = np.ascontiguousarray(
            np.concatenate([x[b, half * 1024:], x[b, :half * 1024]], axis=0))
        m = dict(shared)
        m["x"] = xr
        maps.append(m)
    return maps


def run_A(d):
    nc = build_A()
    res = run_bass_kernel_spmd(nc, prep_A(d), core_ids=list(range(8)))
    x1fm = [np.concatenate([res.results[2 * b]["x1out"],
                            res.results[2 * b + 1]["x1out"]], axis=1)
            for b in range(4)]
    return x1fm



def build_B(debug=False):
    nc = bass.Bass("TRN2", target_bir_lowering=False)
    ctx = ExitStack()

    def din(name, shape, dt):
        return nc.dram_tensor(name, shape, dt, kind="ExternalInput").ap()

    x1 = din("x1", [2048, 2048], F32)
    wqT = din("wqT", [2048, 768], F32)
    wkT = din("wkT", [2048, 768], F32)
    wvT = din("wvT", [2048, 768], F32)
    wgT = din("wgT", [2048, 768], F32)
    wbaT = din("wbaT", [2048, 64], F32)
    cwq = din("cwq", [128, 24], F32)
    cwk = din("cwk", [128, 24], F32)
    cwv = din("cwv", [128, 24], F32)
    alog = din("alog", [3, 1], F32)
    dtb = din("dtb", [3, 1], F32)
    onw = din("onw", [1, 256], F32)
    utm_sc = din("utm_sc", [128, 128], F32)
    uts = din("uts", [128, 128], F32)
    lts = din("lts", [128, 128], F32)
    idn32 = din("idn32", [128, 128], F32)
    idnbf = din("idnbf", [128, 128], BF)
    ones_r32 = din("ones_r32", [1, 128], F32)
    negones_r32 = din("negones_r32", [1, 128], F32)
    ones_cbf = din("ones_cbf", [128, 1], BF)
    cpool = din("cpool", [128, 8], F32)
    oB = nc.dram_tensor("oB", [2048, 768], BF, kind="ExternalOutput").ap()
    rowd = nc.dram_tensor("rowd", [16, 2048], F32,
                          kind="ExternalOutput" if debug else "Internal").ap()
    if debug:
        dxn = nc.dram_tensor("dxn", [128, 2048], BF, kind="ExternalOutput").ap()
        dqn = nc.dram_tensor("dqn", [2, 128, 2048], BF, kind="ExternalOutput").ap()
        dkn = nc.dram_tensor("dkn", [2, 128, 2048], BF, kind="ExternalOutput").ap()
        dvc = nc.dram_tensor("dvc", [2, 128, 2048], BF, kind="ExternalOutput").ap()
        dnt = nc.dram_tensor("dnt", [2, 128, 128], BF, kind="ExternalOutput").ap()
        dpt = nc.dram_tensor("dpt", [2, 128, 128], F32, kind="ExternalOutput").ap()
        dub = nc.dram_tensor("dub", [2, 128, 256], BF, kind="ExternalOutput").ap()
        doa = nc.dram_tensor("doa", [2, 128, 256], F32, kind="ExternalOutput").ap()
        dcol = nc.dram_tensor("dcol", [6, 128, 16], F32, kind="ExternalOutput").ap()
        dktm = nc.dram_tensor("dktm", [2, 128, 256], BF, kind="ExternalOutput").ap()

    A = Arena(nc, 202 * 1024)
    S = Sched(nc, ctx)
    ps = {k: ctx.enter_context(nc.psum_tensor("ps" + k, [128, 512], F32))
          for k in "ABCDE"}
    ps["T1"] = ctx.enter_context(nc.psum_tensor("psT1", [128, 1024], BF))
    ps["T2"] = ctx.enter_context(nc.psum_tensor("psT2", [128, 1024], BF))
    pt = {k: Track() for k in ps}

    def do(eng, emit, reads=(), writes=(), inc=1, sreads=()):
        after = []
        fafter = []
        for r in reads:
            after += r.raw
        for r in sreads:
            fafter += r.raw
        for w in writes:
            after += w.war
        tok = S.op(eng, emit, after=after, inc=inc, fafter=fafter)
        for r in reads:
            r.read(tok)
        for r in sreads:
            r.read(tok)
        for w in writes:
            w.write(tok)
        return tok

    def dodma(eng, out_ap, in_ap, reads=(), writes=(), **kw):
        after = []
        for r in reads:
            after += r.raw
        for w in writes:
            after += w.war
        tok = S.dma(eng, out_ap, in_ap, after=after, **kw)
        for r in reads:
            r.read(tok)
        for w in writes:
            w.write(tok)
        return tok

    def sbuf(shape, dt, name=None):
        return A.buf(shape, dt, name), Track()

    def sbufs(n, shape, dt, name):
        hs = A.bufs(n, shape, dt, name)
        return hs, [Track() for _ in hs]

    cp_t, cp_k = sbuf([128, 8], F32, "cp_t")
    dodma("sync", cp_t[:], cpool, writes=[cp_k])

    def cp(j, P=128):
        return cp_t[0:P, j:j + 1]

    small = {}
    for nm, src, shp, dt in [
            ("cwq", cwq, [128, 24], F32), ("cwk", cwk, [128, 24], F32),
            ("cwv", cwv, [128, 24], F32), ("alog", alog, [3, 1], F32),
            ("dtb", dtb, [3, 1], F32), ("onw", onw, [1, 256], F32),
            ("utm", utm_sc, [128, 128], F32), ("uts", uts, [128, 128], F32),
            ("lts", lts, [128, 128], F32), ("idn32", idn32, [128, 128], F32),
            ("idnbf", idnbf, [128, 128], BF), ("o_r32", ones_r32, [1, 128], F32),
            ("no_r32", negones_r32, [1, 128], F32),
            ("o_cbf", ones_cbf, [128, 1], BF)]:
        t, k_ = sbuf(shp, dt, nm)
        dodma("sync", t[:], src, writes=[k_])
        small[nm] = (t, k_)

    def sm(nm):
        return small[nm][0], small[nm][1]

    x1h, x1h_k = sbufs(16, [128, 2048], F16, "x1h")
    for fc in range(16):
        dodma("gpsimd", x1h[fc][:], x1[fc * 128:(fc + 1) * 128, :],
              writes=[x1h_k[fc]])
    sum_row, sum_k = sbuf([1, 2048], F32, "sum_row")
    ss_row, ss_k = sbuf([1, 2048], F32, "ss_row")
    r_row, r_k = sbuf([1, 2048], F32, "r_row")
    oc16_t, oc16_k = sbuf([128, 1], F16, "oc16")
    do("vector", lambda e: e.memset(oc16_t[:], 1.0), writes=[oc16_k])
    sq, sq_k = sbufs(2, [128, 512], F16, "sq")
    for nt in range(4):
        for fc in range(16):
            r2 = fc % 2
            do("scalar", lambda e, fc=fc, nt=nt, r2=r2: e.activation(
                sq[r2][:], x1h[fc][:, nt * 512:(nt + 1) * 512], AF.Square),
                reads=[x1h_k[fc]], writes=[sq_k[r2]])
            do("tensor", lambda e, fc=fc, nt=nt, st=(fc == 0), sp=(fc == 15):
                e.matmul(ps["A"][0:1, :], oc16_t[:],
                         x1h[fc][:, nt * 512:(nt + 1) * 512], start=st, stop=sp),
                reads=[x1h_k[fc], oc16_k], writes=[pt["A"]])
            do("tensor", lambda e, r2=r2, st=(fc == 0), sp=(fc == 15): e.matmul(
                ps["B"][0:1, :], oc16_t[:], sq[r2][:], start=st, stop=sp),
                reads=[sq_k[r2], oc16_k], writes=[pt["B"]])
        do("vector", lambda e, nt=nt: e.tensor_copy(
            sum_row[0:1, nt * 512:(nt + 1) * 512], ps["A"][0:1, :]),
            reads=[pt["A"]], writes=[sum_k])
        do("vector", lambda e, nt=nt: e.tensor_copy(
            ss_row[0:1, nt * 512:(nt + 1) * 512], ps["B"][0:1, :]),
            reads=[pt["B"]], writes=[ss_k])
    m_row, m_k = sbuf([1, 2048], F32, "m_row")
    v_row, v_k = sbuf([1, 2048], F32, "v_row")
    do("vector", lambda e: e.tensor_scalar_mul(m_row[:], sum_row[:], 1.0 / 2048),
        reads=[sum_k], writes=[m_k])
    do("vector", lambda e: e.tensor_scalar_mul(v_row[:], ss_row[:], 1.0 / 2048),
        reads=[ss_k], writes=[v_k])
    do("vector", lambda e: e.scalar_tensor_tensor(
        r_row[:], m_row[:], -1.0, m_row[:], ALU.mult, ALU.mult),
        reads=[m_k], writes=[r_k])
    do("vector", lambda e: e.tensor_add(ss_row[:], v_row[:], r_row[:]),
        reads=[v_k, r_k], writes=[ss_k])
    do("scalar", lambda e: e.activation(v_row[:], ss_row[:], AF.Sqrt,
                                        bias=cp(0, 1)),
        reads=[ss_k, cp_k], writes=[v_k])
    do("vector", lambda e: e.reciprocal(r_row[:], v_row[:]),
        reads=[v_k], writes=[r_k])
    M_bc, M_k = sbuf([128, 2048], F32, "M_bc")
    R_bc, R_k = sbuf([128, 2048], F32, "R_bc")
    for nt in range(4):
        do("tensor", lambda e, nt=nt: e.matmul(
            ps["C"][:, :], sm("o_r32")[0][:], m_row[0:1, nt * 512:(nt + 1) * 512],
            start=True, stop=True),
            reads=[sm("o_r32")[1], m_k], writes=[pt["C"]])
        do("vector", lambda e, nt=nt: e.tensor_copy(
            M_bc[:, nt * 512:(nt + 1) * 512], ps["C"][:, :]),
            reads=[pt["C"]], writes=[M_k])
        do("tensor", lambda e, nt=nt: e.matmul(
            ps["D"][:, :], sm("o_r32")[0][:], r_row[0:1, nt * 512:(nt + 1) * 512],
            start=True, stop=True),
            reads=[sm("o_r32")[1], r_k], writes=[pt["D"]])
        do("vector", lambda e, nt=nt: e.tensor_copy(
            R_bc[:, nt * 512:(nt + 1) * 512], ps["D"][:, :]),
            reads=[pt["D"]], writes=[R_k])
    xn1, xn1_k = sbufs(16, [128, 2048], BF, "xn1")
    dtmp, dtmp_k = sbufs(2, [128, 2048], F16, "dtmpB")
    for fc in range(16):
        r = fc % 2
        do("vector", lambda e, fc=fc, r=r: e.tensor_tensor(
            dtmp[r][:], x1h[fc][:], M_bc[:], ALU.subtract),
            reads=[x1h_k[fc], M_k], writes=[dtmp_k[r]])
        do("vector", lambda e, fc=fc, r=r: e.tensor_tensor(
            xn1[fc][:], dtmp[r][:], R_bc[:], ALU.mult),
            reads=[dtmp_k[r], R_k], writes=[xn1_k[fc]])
    if debug:
        dodma("gpsimd", dxn, xn1[0][:], reads=[xn1_k[0]])
    S.barrier()
    A.free(x1h, M_bc, R_bc, sq, sum_row, ss_row, r_row, m_row, v_row, dtmp)

    wba_t, wba_k = sbuf([128, 1024], BF, "wba_t")
    dodma("gpsimd", wba_t[:].rearrange("p (c m) -> p c m", m=64),
          wbaT.rearrange("(c p) m -> p c m", p=128), writes=[wba_k])
    brow, brow_k = sbuf([32, 2048], F32, "brow")
    arow, arow_k = sbuf([32, 2048], F32, "arow")
    t3a, t3a_k = sbuf([32, 2048], F32, "t3a")
    t3b, t3b_k = sbuf([32, 2048], F32, "t3b")
    ea_t, ea_k = sbuf([32, 1], F32, "ea_t")
    dtb32, dtb32_k = sbuf([32, 1], F32, "dtb32")
    do("vector", lambda e: e.memset(dtb32[:], 0.0), writes=[dtb32_k])
    do("vector", lambda e: e.tensor_copy(dtb32[0:3, :], sm("dtb")[0][:]),
        reads=[sm("dtb")[1]], writes=[dtb32_k])
    alog32, alog32_k = sbuf([32, 1], F32, "alog32")
    do("vector", lambda e: e.memset(alog32[:], 0.0), writes=[alog32_k])
    do("vector", lambda e: e.tensor_copy(alog32[0:3, :], sm("alog")[0][:]),
        reads=[sm("alog")[1]], writes=[alog32_k])
    do("vector", lambda e: e.memset(arow[:], 0.0), writes=[arow_k])
    for nt in range(4):
        for ic in range(16):
            do("tensor", lambda e, ic=ic, nt=nt, st=(ic == 0), sp=(ic == 15):
                e.matmul(ps["A"][0:64, :], wba_t[:, ic * 64:(ic + 1) * 64],
                         xn1[ic][:, nt * 512:(nt + 1) * 512], start=st, stop=sp),
                reads=[xn1_k[ic], wba_k], writes=[pt["A"]])
        do("scalar", lambda e, nt=nt: e.activation(
            brow[0:3, nt * 512:(nt + 1) * 512], ps["A"][0:3, :], AF.Sigmoid),
            reads=[pt["A"]], writes=[brow_k])
        do("vector", lambda e, nt=nt: e.tensor_copy(
            arow[0:3, nt * 512:(nt + 1) * 512], ps["A"][32:35, :]),
            reads=[pt["A"]], writes=[arow_k])
    a2, a2_k = sbuf([32, 2048], F32, "a2")
    ea2, ea2_k = sbuf([32, 1], F32, "ea2")
    do("vector", lambda e: e.tensor_scalar_add(
        a2[:], arow[:], dtb32[:, 0:1]),
        reads=[arow_k], sreads=[dtb32_k], writes=[a2_k])
    do("scalar", lambda e: e.activation(t3a[:], a2[:], AF.Abs),
        reads=[a2_k], writes=[t3a_k])
    do("scalar", lambda e: e.activation(t3b[:], t3a[:], AF.Exp, scale=cp(3, 32)),
        reads=[t3a_k, cp_k], writes=[t3b_k])
    do("scalar", lambda e: e.activation(t3a[:], t3b[:], AF.Ln, bias=cp(4, 32)),
        reads=[t3b_k, cp_k], writes=[t3a_k])
    do("vector", lambda e: e.tensor_scalar_max(t3b[:], a2[:], 0.0),
        reads=[a2_k], writes=[t3b_k])
    do("vector", lambda e: e.tensor_add(arow[:], t3a[:], t3b[:]),
        reads=[t3a_k, t3b_k], writes=[arow_k])
    do("scalar", lambda e: e.activation(ea_t[:], alog32[:], AF.Exp),
        reads=[alog32_k], writes=[ea_k])
    do("vector", lambda e: e.tensor_scalar_mul(ea2[:], ea_t[:], -1.0),
        reads=[ea_k], writes=[ea2_k])
    do("vector", lambda e: e.tensor_scalar_mul(
        a2[:], arow[:], ea2[:, 0:1]),
        reads=[arow_k], sreads=[ea2_k], writes=[a2_k])
    do("vector", lambda e: e.tensor_tensor_scan(
        t3a[:], a2[:], a2[:], 0.0, ALU.add, ALU.bypass),
        reads=[a2_k], writes=[t3a_k])
    dodma("gpsimd", rowd[0:3, :], brow[0:3, :], reads=[brow_k])
    dodma("gpsimd", rowd[3:6, :], t3a[0:3, :], reads=[t3a_k])
    if debug:
        dodma("gpsimd", rowd[6:9, :], arow[0:3, :], reads=[arow_k])
        dodma("gpsimd", rowd[9:12, :], a2[0:3, :], reads=[a2_k])
        dodma("gpsimd", rowd[12:13, 0:32].rearrange("o p -> p o"), ea2[:, 0:1],
              reads=[ea2_k], allow_slow_non_contiguous=True)
        dodma("gpsimd", rowd[13:14, 0:32].rearrange("o p -> p o"), dtb32[:, 0:1],
              reads=[dtb32_k], allow_slow_non_contiguous=True)
    NW, NW_k = sbuf([128, 256], F32, "NW")
    do("tensor", lambda e: e.matmul(ps["B"][:, 0:256], sm("o_r32")[0][:],
                                    sm("onw")[0][:], start=True, stop=True),
        reads=[sm("onw")[1], sm("o_r32")[1]], writes=[pt["B"]])
    do("vector", lambda e: e.tensor_copy(NW[:], ps["B"][:, 0:256]),
        reads=[pt["B"]], writes=[NW_k])
    S.barrier()
    A.free(wba_t, brow, arow, t3b, ea_t, t3a, dtb32, alog32, a2, ea2)

    def do_head(hl):
        hw = {}

        def hbuf(nm, shape, dt):
            t, k_ = sbuf(shape, dt, f"h{hl}_{nm}")
            hw[nm] = t
            return t, k_

        cgr, cgr_k = hbuf("cgr", [1, 2048], F32)
        dodma("sync", cgr[:], rowd[3 + hl:4 + hl, :], writes=[cgr_k])
        btr, btr_k = hbuf("btr", [1, 2048], F32)
        dodma("sync", btr[:], rowd[hl:hl + 1, :], writes=[btr_k])
        c16, c16_k = hbuf("c16", [16, 128], F32)
        dodma("sync", c16[:], rowd[3 + hl:4 + hl, :].rearrange(
            "o (c p) -> (o c) p", p=128), writes=[c16_k])
        b16, b16_k = hbuf("b16", [16, 128], F32)
        dodma("sync", b16[:], rowd[hl:hl + 1, :].rearrange(
            "o (c p) -> (o c) p", p=128), writes=[b16_k])
        cgc, cgc_k = hbuf("cgc", [128, 16], F32)
        do("tensor", lambda e, c16=c16: e.transpose(
            ps["D"][:, 0:16], c16[:], sm("idn32")[0][0:16, 0:16]),
            reads=[c16_k, sm("idn32")[1]], writes=[pt["D"]])
        do("vector", lambda e: e.tensor_copy(cgc[:], ps["D"][:, 0:16]),
            reads=[pt["D"]], writes=[cgc_k])
        btc, btc_k = hbuf("btc", [128, 16], F32)
        do("tensor", lambda e, b16=b16: e.transpose(
            ps["D"][:, 0:16], b16[:], sm("idn32")[0][0:16, 0:16]),
            reads=[b16_k, sm("idn32")[1]], writes=[pt["D"]])
        do("vector", lambda e: e.tensor_copy(btc[:], ps["D"][:, 0:16]),
            reads=[pt["D"]], writes=[btc_k])
        nbc, nbc_k = hbuf("nbc", [128, 16], F32)
        do("vector", lambda e: e.tensor_scalar_mul(nbc[:], btc[:], -1.0),
            reads=[btc_k], writes=[nbc_k])
        endr, endr_k = hbuf("endr", [1, 16], F32)
        do("vector", lambda e: e.tensor_copy(
            endr[:], cgr[0:1, 127:2048:128]), reads=[cgr_k], writes=[endr_k])
        per, per_k = hbuf("per", [1, 16], F32)
        do("vector", lambda e: e.memset(per[:], 0.0), writes=[per_k])
        do("vector", lambda e: e.tensor_copy(
            per[0:1, 1:16], cgr[0:1, 127:1920:128]),
            reads=[cgr_k], writes=[per_k])
        gcr, gcr_k = hbuf("gcr", [1, 16], F32)
        gcx, gcx_k = hbuf("gcx", [1, 16], F32)
        do("vector", lambda e: e.tensor_tensor(
            gcx[:], endr[:], per[:], ALU.subtract),
            reads=[endr_k, per_k], writes=[gcx_k])
        do("scalar", lambda e: e.activation(gcr[:], gcx[:], AF.Exp),
            reads=[gcx_k], writes=[gcr_k])
        gCbc, gCbc_k = hbuf("gCbc", [128, 16], F32)
        do("tensor", lambda e: e.matmul(ps["D"][:, 0:16], sm("o_r32")[0][:],
                                        gcr[:], start=True, stop=True),
            reads=[gcr_k, sm("o_r32")[1]], writes=[pt["D"]])
        do("vector", lambda e: e.tensor_copy(gCbc[:], ps["D"][:, 0:16]),
            reads=[pt["D"]], writes=[gCbc_k])
        gcols, gcols_k = hbuf("gcols", [128, 16], F32)
        gtmp, gtmp_k = hbuf("gtmp", [128, 16], F32)
        do("tensor", lambda e: e.matmul(ps["D"][:, 0:16], sm("o_r32")[0][:],
                                        per[:], start=True, stop=True),
            reads=[per_k, sm("o_r32")[1]], writes=[pt["D"]])
        do("vector", lambda e: e.tensor_tensor(
            gtmp[:], cgc[:], ps["D"][:, 0:16], ALU.subtract),
            reads=[cgc_k, pt["D"]], writes=[gtmp_k])
        do("scalar", lambda e: e.activation(gcols[:], gtmp[:], AF.Exp),
            reads=[gtmp_k], writes=[gcols_k])
        gsc, gsc_k = hbuf("gsc", [128, 16], F32)
        do("vector", lambda e: e.tensor_scalar_mul(gsc[:], gcols[:], 1.0 / 16.0),
            reads=[gcols_k], writes=[gsc_k])
        nbg, nbg_k = hbuf("nbg", [128, 16], F32)
        do("vector", lambda e: e.tensor_tensor(
            nbg[:], nbc[:], gcols[:], ALU.mult),
            reads=[nbc_k, gcols_k], writes=[nbg_k])
        rend, rend_k = hbuf("rend", [128, 16], F32)
        do("tensor", lambda e: e.matmul(ps["D"][:, 0:16], sm("o_r32")[0][:],
                                        endr[:], start=True, stop=True),
            reads=[endr_k, sm("o_r32")[1]], writes=[pt["D"]])
        do("vector", lambda e: e.tensor_tensor(
            gtmp[:], ps["D"][:, 0:16], cgc[:], ALU.subtract),
            reads=[pt["D"], cgc_k], writes=[gtmp_k])
        do("scalar", lambda e: e.activation(rend[:], gtmp[:], AF.Exp),
            reads=[gtmp_k], writes=[rend_k])

        wts = {}
        for nm, src in [("wq", wqT), ("wk", wkT), ("wv", wvT), ("wg", wgT)]:
            t, k_ = hbuf(nm, [128, 4096], BF)
            dodma("gpsimd", t[:].rearrange("p (c m) -> p c m", m=256),
                  src[:, hl * 256:(hl + 1) * 256].rearrange(
                      "(c p) m -> p c m", p=128), writes=[k_])
            wts[nm] = (t, k_)

        proj = {}
        for nm in ("wq", "wk", "wv"):
            wt_, wk_ = wts[nm]
            pp, pp_k = sbufs(2, [128, 2048], BF, f"h{hl}_{nm}p")
            hw[nm + "p"] = pp
            for oc2 in range(2):
                for n4 in range(4):
                    for ic in range(16):
                        do("tensor", lambda e, wt_=wt_, ic=ic, oc2=oc2, n4=n4,
                           st=(ic == 0), sp=(ic == 15): e.matmul(
                               ps["A"][:, :],
                               wt_[:, ic * 256 + oc2 * 128:
                                   ic * 256 + oc2 * 128 + 128],
                               xn1[ic][:, n4 * 512:(n4 + 1) * 512],
                               start=st, stop=sp),
                            reads=[xn1_k[ic], wk_], writes=[pt["A"]])
                    do("vector", lambda e, pp=pp, oc2=oc2, n4=n4: e.tensor_copy(
                        pp[oc2][:, n4 * 512:(n4 + 1) * 512], ps["A"][:, :]),
                        reads=[pt["A"]], writes=[pp_k[oc2]])
            proj[nm] = (pp, pp_k)
        sg_h, sg_k = sbufs(16, [128, 256], BF, f"h{hl}_sg")
        hw["sg"] = sg_h
        wgt_, wgk_ = wts["wg"]
        for tt in range(16):
            for ic in range(16):
                do("tensor", lambda e, ic=ic, tt=tt, st=(ic == 0), sp=(ic == 15):
                    e.matmul(ps["B"][:, 0:256],
                             xn1[ic][:, tt * 128:(tt + 1) * 128],
                             wgt_[:, ic * 256:(ic + 1) * 256], start=st, stop=sp),
                    reads=[xn1_k[ic], wgk_], writes=[pt["B"]])
            do("scalar", lambda e, tt=tt: e.activation(
                sg_h[tt][:], ps["B"][:, 0:256], AF.Silu),
                reads=[pt["B"]], writes=[sg_k[tt]])
        S.barrier()
        for nm in ("wq", "wk", "wv", "wg"):
            A.free(wts[nm][0])
            hw.pop(nm)

        def conv_silu(nm, cw_nm, norm):
            pp, pp_k = proj[nm]
            cwt, cwt_k = sm(cw_nm)
            yb, yb_k = sbufs(2, [128, 2048], F32, f"h{hl}_{nm}y")
            out, out_k = sbufs(2, [128, 2048], BF, f"h{hl}_{nm}c")
            hw[nm + "c"] = out
            hw[nm + "y"] = yb
            for fc in range(2):
                cgl = hl * 2 + fc

                def cw(tap):
                    return cwt[:, cgl * 4 + tap:cgl * 4 + tap + 1]
                cur, oth = 0, 1
                do("vector", lambda e, fc=fc, cwap=cw(3): e.tensor_scalar_mul(
                    yb[0][:], pp[fc][:], cwap),
                    reads=[pp_k[fc], cwt_k], writes=[yb_k[0]])
                for i, tap in enumerate((2, 1, 0)):
                    off = i + 1
                    do("vector", lambda e, fc=fc, cwap=cw(tap), off=off, cur=cur,
                       oth=oth: e.scalar_tensor_tensor(
                           yb[oth][:, off:2048], pp[fc][:, 0:2048 - off],
                           cwap, yb[cur][:, off:2048], ALU.mult, ALU.add),
                        reads=[pp_k[fc], cwt_k, yb_k[cur]], writes=[yb_k[oth]])
                    do("vector", lambda e, off=off, cur=cur, oth=oth:
                        e.tensor_copy(yb[oth][:, 0:off], yb[cur][:, 0:off]),
                        reads=[yb_k[cur]], writes=[yb_k[oth]])
                    cur, oth = oth, cur
                do("scalar", lambda e, fc=fc, cur=cur: e.activation(
                    out[fc][:], yb[cur][:], AF.Silu),
                    reads=[yb_k[cur]], writes=[out_k[fc]])
            S.barrier()
            A.free(pp, yb)
            hw.pop(nm + "p")
            hw.pop(nm + "y")
            if norm:
                nr, nr_k = sbuf([1, 2048], F32, f"h{hl}_{nm}nr")
                nr2, nr2_k = sbuf([1, 2048], F32, f"h{hl}_{nm}nr2")
                sqb, sqb_k = sbufs(2, [128, 512], BF, f"h{hl}_{nm}sq")
                nout, nout_k = sbufs(2, [128, 2048], BF, f"h{hl}_{nm}n")
                hw[nm + "n"] = nout
                for nt in range(4):
                    for fc in range(2):
                        r2 = fc % 2
                        do("scalar", lambda e, fc=fc, nt=nt, r2=r2: e.activation(
                            sqb[r2][:], out[fc][:, nt * 512:(nt + 1) * 512],
                            AF.Square), reads=[out_k[fc]], writes=[sqb_k[r2]])
                        do("tensor", lambda e, r2=r2, st=(fc == 0), sp=(fc == 1):
                            e.matmul(ps["C"][0:1, :], sm("o_cbf")[0][:],
                                     sqb[r2][:], start=st, stop=sp),
                            reads=[sqb_k[r2], sm("o_cbf")[1]], writes=[pt["C"]])
                    do("vector", lambda e, nt=nt: e.tensor_copy(
                        nr[0:1, nt * 512:(nt + 1) * 512], ps["C"][0:1, :]),
                        reads=[pt["C"]], writes=[nr_k])
                do("scalar", lambda e: e.activation(nr2[:], nr[:], AF.Sqrt,
                                                    bias=cp(5, 1)),
                    reads=[nr_k, cp_k], writes=[nr2_k])
                do("vector", lambda e: e.reciprocal(nr[:], nr2[:]),
                    reads=[nr2_k], writes=[nr_k])
                for nt in range(4):
                    do("tensor", lambda e, nt=nt: e.matmul(
                        ps["D"][:, :], sm("o_r32")[0][:],
                        nr[0:1, nt * 512:(nt + 1) * 512], start=True, stop=True),
                        reads=[nr_k, sm("o_r32")[1]], writes=[pt["D"]])
                    for fc in range(2):
                        do("vector", lambda e, fc=fc, nt=nt: e.tensor_tensor(
                            nout[fc][:, nt * 512:(nt + 1) * 512],
                            out[fc][:, nt * 512:(nt + 1) * 512],
                            ps["D"][:, :], ALU.mult),
                            reads=[out_k[fc], pt["D"]], writes=[nout_k[fc]])
                A.free(nr, nr2, sqb, out)
                hw.pop(nm + "c")
                return nout, nout_k
            return out, out_k

        qn, qn_k = conv_silu("wq", "cwq", True)
        kn, kn_k = conv_silu("wk", "cwk", True)
        vc, vc_k = conv_silu("wv", "cwv", False)

        if debug and hl == 0:
            for fc in range(2):
                dodma("gpsimd", dqn[fc], qn[fc][:], reads=[qn_k[fc]])
                dodma("gpsimd", dkn[fc], kn[fc][:], reads=[kn_k[fc]])
                dodma("gpsimd", dvc[fc], vc[fc][:], reads=[vc_k[fc]])
            for i, (tt_, kk_) in enumerate([(cgc, cgc_k), (btc, btc_k),
                                            (gcols, gcols_k), (gCbc, gCbc_k),
                                            (rend, rend_k), (nbg, nbg_k)]):
                dodma("gpsimd", dcol[i], tt_[:], reads=[kk_])
        ktm, ktm_k = sbufs(16, [128, 256], BF, f"h{hl}_ktm")
        vtm, vtm_k = sbufs(16, [128, 256], BF, f"h{hl}_vtm")
        bv, bv_k = sbufs(16, [128, 256], BF, f"h{hl}_bv")
        kbar, kbar_k = sbufs(16, [128, 256], BF, f"h{hl}_kbar")
        hw.update(ktm=ktm, vtm=vtm, bv=bv, kbar=kbar)
        for cc in range(16):
            for fc in range(2):
                tb = "T1" if fc == 0 else "T2"
                do("tensor", lambda e, cc=cc, fc=fc, tb=tb: e.transpose(
                    ps[tb][:, 0:128], kn[fc][:, cc * 128:(cc + 1) * 128],
                    sm("idnbf")[0][:]),
                    reads=[kn_k[fc], sm("idnbf")[1]], writes=[pt[tb]])
                do("vector", lambda e, cc=cc, fc=fc, tb=tb: e.tensor_copy(
                    ktm[cc][:, fc * 128:(fc + 1) * 128], ps[tb][:, 0:128]),
                    reads=[pt[tb]], writes=[ktm_k[cc]])
                do("tensor", lambda e, cc=cc, fc=fc, tb=tb: e.transpose(
                    ps[tb][:, 128:256], vc[fc][:, cc * 128:(cc + 1) * 128],
                    sm("idnbf")[0][:]),
                    reads=[vc_k[fc], sm("idnbf")[1]], writes=[pt[tb]])
                do("vector", lambda e, cc=cc, fc=fc, tb=tb: e.tensor_copy(
                    vtm[cc][:, fc * 128:(fc + 1) * 128], ps[tb][:, 128:256]),
                    reads=[pt[tb]], writes=[vtm_k[cc]])
            do("vector", lambda e, cc=cc: e.tensor_scalar_mul(
                bv[cc][:], vtm[cc][:], btc[:, cc:cc + 1]),
                reads=[vtm_k[cc]], sreads=[btc_k], writes=[bv_k[cc]])
            do("vector", lambda e, cc=cc: e.tensor_scalar_mul(
                kbar[cc][:], ktm[cc][:], rend[:, cc:cc + 1]),
                reads=[ktm_k[cc]], sreads=[rend_k], writes=[kbar_k[cc]])

        if debug and hl == 0:
            for cc in range(2):
                dodma("gpsimd", dktm[cc], ktm[cc][:], reads=[ktm_k[cc]])
        ntl, ntl_k = sbufs(16, [128, 128], BF, f"h{hl}_nt")
        ptl, ptl_k = sbufs(16, [128, 128], F32, f"h{hl}_pt")
        hw.update(ntl=ntl, ptl=ptl)
        eut, eut_k = sbuf([128, 128], F32, f"h{hl}_eut")
        elt, elt_k = sbuf([128, 128], F32, f"h{hl}_elt")
        tq1, tq1_k = sbuf([128, 128], F32, f"h{hl}_tq1")
        tq2, tq2_k = sbuf([128, 128], F32, f"h{hl}_tq2")
        tq3, tq3_k = sbuf([128, 128], F32, f"h{hl}_tq3")
        ab_b, ab_k = sbufs(6, [128, 128], F32, f"h{hl}_ab")
        for cc in range(16):
            for fc in range(2):
                do("tensor", lambda e, cc=cc, fc=fc, st=(fc == 0), sp=(fc == 1):
                    e.matmul(ps["A"][:, 0:128],
                             kn[fc][:, cc * 128:(cc + 1) * 128],
                             qn[fc][:, cc * 128:(cc + 1) * 128],
                             start=st, stop=sp),
                    reads=[kn_k[fc], qn_k[fc]], writes=[pt["A"]])
                do("tensor", lambda e, cc=cc, fc=fc, st=(fc == 0), sp=(fc == 1):
                    e.matmul(ps["B"][:, 0:128],
                             kn[fc][:, cc * 128:(cc + 1) * 128],
                             kn[fc][:, cc * 128:(cc + 1) * 128],
                             start=st, stop=sp),
                    reads=[kn_k[fc]], writes=[pt["B"]])
            do("tensor", lambda e, cc=cc: e.matmul(
                ps["D"][:, 0:128], sm("o_r32")[0][:],
                cgr[0:1, cc * 128:(cc + 1) * 128], start=True, stop=True),
                reads=[cgr_k, sm("o_r32")[1]], writes=[pt["D"]])
            do("vector", lambda e, cc=cc: e.tensor_scalar(
                tq1[:], ps["D"][:, 0:128], cgc[:, cc:cc + 1], 0.0,
                ALU.subtract, ALU.min),
                reads=[pt["D"]], sreads=[cgc_k], writes=[tq1_k])
            do("scalar", lambda e: e.activation(eut[:], tq1[:], AF.Exp),
                reads=[tq1_k], writes=[eut_k])
            do("vector", lambda e, cc=cc: e.tensor_scalar(
                tq2[:], ps["D"][:, 0:128], cgc[:, cc:cc + 1], 0.0,
                ALU.subtract, ALU.max),
                reads=[pt["D"]], sreads=[cgc_k], writes=[tq2_k])
            do("scalar", lambda e: e.activation(elt[:], tq2[:], AF.Exp,
                                                scale=cp(3)),
                reads=[tq2_k, cp_k], writes=[elt_k])
            do("vector", lambda e: e.tensor_tensor(
                tq1[:], eut[:], sm("utm")[0][:], ALU.mult),
                reads=[eut_k, sm("utm")[1]], writes=[tq1_k])
            do("vector", lambda e, cc=cc: e.tensor_tensor(
                ntl[cc][:], tq1[:], ps["A"][:, 0:128], ALU.mult),
                reads=[tq1_k, pt["A"]], writes=[ntl_k[cc]])
            do("vector", lambda e: e.tensor_tensor(
                tq2[:], elt[:], sm("lts")[0][:], ALU.mult),
                reads=[elt_k, sm("lts")[1]], writes=[tq2_k])
            do("vector", lambda e, cc=cc: e.scalar_tensor_tensor(
                ab_b[0][:], tq2[:], nbc[:, cc:cc + 1], ps["B"][:, 0:128],
                ALU.mult, ALU.mult),
                reads=[tq2_k, pt["B"]], sreads=[nbc_k], writes=[ab_k[0]])
            do("vector", lambda e: e.tensor_tensor(
                tq3[:], eut[:], sm("uts")[0][:], ALU.mult),
                reads=[eut_k, sm("uts")[1]], writes=[tq3_k])
            do("vector", lambda e, cc=cc: e.tensor_tensor(
                tq1[:], tq3[:], ps["B"][:, 0:128], ALU.mult),
                reads=[tq3_k, pt["B"]], writes=[tq1_k])
            do("tensor", lambda e, cc=cc: e.matmul(
                ps["E"][:, 0:128], sm("no_r32")[0][:],
                btr[0:1, cc * 128:(cc + 1) * 128], start=True, stop=True),
                reads=[btr_k, sm("no_r32")[1]], writes=[pt["E"]])
            do("vector", lambda e: e.tensor_tensor(
                ab_b[2][:], tq1[:], ps["E"][:, 0:128], ALU.mult),
                reads=[tq1_k, pt["E"]], writes=[ab_k[2]])
            do("vector", lambda e: e.tensor_add(
                ab_b[4][:], ab_b[2][:], sm("idn32")[0][:]),
                reads=[ab_k[2], sm("idn32")[1]], writes=[ab_k[4]])
            cur_a, cur_b, cur_p = 0, 2, 4
            oth_a, oth_b, oth_p = 1, 3, 5
            for j in range(1, 7):
                do("tensor", lambda e, ca=cur_a, cb=cur_b: e.matmul(
                    ps["A"][:, 0:128], ab_b[cb][:], ab_b[ca][:],
                    start=True, stop=True),
                    reads=[ab_k[cur_a], ab_k[cur_b]], writes=[pt["A"]])
                do("vector", lambda e, oa=oth_a: e.tensor_copy(
                    ab_b[oa][:], ps["A"][:, 0:128]),
                    reads=[pt["A"]], writes=[ab_k[oth_a]])
                if j < 6:
                    do("tensor", lambda e, ca=cur_a, cb=cur_b: e.matmul(
                        ps["B"][:, 0:128], ab_b[ca][:], ab_b[cb][:],
                        start=True, stop=True),
                        reads=[ab_k[cur_a], ab_k[cur_b]], writes=[pt["B"]])
                    do("vector", lambda e, ob=oth_b: e.tensor_copy(
                        ab_b[ob][:], ps["B"][:, 0:128]),
                        reads=[pt["B"]], writes=[ab_k[oth_b]])
                do("tensor", lambda e, cp_=cur_p: e.matmul(
                    ps["C"][:, 0:128], sm("idn32")[0][:], ab_b[cp_][:],
                    start=True, stop=False),
                    reads=[ab_k[cur_p], sm("idn32")[1]], writes=[pt["C"]])
                do("tensor", lambda e, oa=oth_a, cp_=cur_p: e.matmul(
                    ps["C"][:, 0:128], ab_b[oa][:], ab_b[cp_][:],
                    start=False, stop=True),
                    reads=[ab_k[oth_a], ab_k[cur_p]], writes=[pt["C"]])
                do("vector", lambda e, op_=oth_p: e.tensor_copy(
                    ab_b[op_][:], ps["C"][:, 0:128]),
                    reads=[pt["C"]], writes=[ab_k[oth_p]])
                cur_a, oth_a = oth_a, cur_a
                cur_b, oth_b = oth_b, cur_b
                cur_p, oth_p = oth_p, cur_p
            do("vector", lambda e, cc=cc, cp_=cur_p: e.tensor_copy(
                ptl[cc][:], ab_b[cp_][:]),
                reads=[ab_k[cur_p]], writes=[ptl_k[cc]])
            if debug and hl == 0 and cc < 2:
                dodma("gpsimd", dnt[cc], ntl[cc][:], reads=[ntl_k[cc]])
                dodma("gpsimd", dpt[cc], ptl[cc][:], reads=[ptl_k[cc]])

        sf, sf_k = sbufs(2, [128, 256], F32, f"h{hl}_sf")
        sb_, sb_k = sbufs(2, [128, 256], BF, f"h{hl}_sb")
        snf, snf_k = sbufs(2, [128, 256], F32, f"h{hl}_snf")
        r2b, r2b_k = sbuf([128, 256], F32, f"h{hl}_r2")
        ubf, ubf_k = sbuf([128, 256], BF, f"h{hl}_ubf")
        oac, oac_k = sbuf([128, 256], F32, f"h{hl}_oac")
        oa2, oa2_k = sbuf([128, 256], F32, f"h{hl}_oa2")
        ssq2, ssq2_k = sbuf([128, 1], F32, f"h{hl}_ssq2")
        osq, osq_k = sbuf([128, 256], BF, f"h{hl}_osq")
        ssq, ssq_k = sbuf([128, 1], F32, f"h{hl}_ssq")
        gnw, gnw_k = sbuf([128, 256], BF, f"h{hl}_gnw")
        obuf, obuf_k = sbufs(2, [128, 256], BF, f"h{hl}_ob")
        for fc in range(2):
            do("vector", lambda e, fc=fc: e.memset(sf[fc][:], 0.0),
                writes=[sf_k[fc]])
            do("vector", lambda e, fc=fc: e.memset(sb_[fc][:], 0.0),
                writes=[sb_k[fc]])
        for cc in range(16):
            for fc in range(2):
                do("tensor", lambda e, cc=cc, fc=fc, st=(fc == 0), sp=(fc == 1):
                    e.matmul(ps["A"][:, 0:256],
                             kn[fc][:, cc * 128:(cc + 1) * 128], sb_[fc][:],
                             start=st, stop=sp),
                    reads=[kn_k[fc], sb_k[fc]], writes=[pt["A"]])
            do("vector", lambda e, cc=cc: e.scalar_tensor_tensor(
                r2b[:], ps["A"][:, 0:256], nbg[:, cc:cc + 1], bv[cc][:],
                ALU.mult, ALU.add),
                reads=[pt["A"], bv_k[cc]], sreads=[nbg_k], writes=[r2b_k])
            do("tensor", lambda e, cc=cc: e.matmul(
                ps["B"][:, 0:256], ptl[cc][:], r2b[:], start=True, stop=True),
                reads=[ptl_k[cc], r2b_k], writes=[pt["B"]])
            do("vector", lambda e: e.tensor_copy(ubf[:], ps["B"][:, 0:256]),
                reads=[pt["B"]], writes=[ubf_k])
            if debug and hl == 0 and cc < 2:
                dodma("gpsimd", dub[cc], ubf[:], reads=[ubf_k])
            for fc in range(2):
                do("tensor", lambda e, cc=cc, fc=fc, st=(fc == 0), sp=(fc == 1):
                    e.matmul(ps["C"][:, 0:256],
                             qn[fc][:, cc * 128:(cc + 1) * 128], sb_[fc][:],
                             start=st, stop=sp),
                    reads=[qn_k[fc], sb_k[fc]], writes=[pt["C"]])
            do("vector", lambda e, cc=cc: e.tensor_scalar_mul(
                oac[:], ps["C"][:, 0:256], gsc[:, cc:cc + 1]),
                reads=[pt["C"]], sreads=[gsc_k], writes=[oac_k])
            do("tensor", lambda e, cc=cc: e.matmul(
                ps["D"][:, 0:256], ntl[cc][:], ubf[:], start=True, stop=True),
                reads=[ntl_k[cc], ubf_k], writes=[pt["D"]])
            do("vector", lambda e: e.tensor_tensor(
                oa2[:], oac[:], ps["D"][:, 0:256], ALU.add),
                reads=[oac_k, pt["D"]], writes=[oa2_k])
            if debug and hl == 0 and cc < 2:
                dodma("gpsimd", doa[cc], oa2[:], reads=[oa2_k])
            do("scalar", lambda e: e.activation(
                osq[:], oa2[:], AF.Square, accum_out=ssq[:]),
                reads=[oa2_k], writes=[osq_k, ssq_k])
            do("scalar", lambda e: e.activation(
                ssq2[:], ssq[:], AF.Sqrt, scale=cp(6), bias=cp(0)),
                reads=[cp_k], sreads=[ssq_k], writes=[ssq2_k])
            do("vector", lambda e: e.reciprocal(ssq[:], ssq2[:]),
                reads=[ssq2_k], writes=[ssq_k])
            do("vector", lambda e, cc=cc: e.tensor_tensor(
                gnw[:], sg_h[cc][:], NW[:], ALU.mult),
                reads=[sg_k[cc], NW_k], writes=[gnw_k])
            rb = cc % 2
            do("vector", lambda e, rb=rb: e.scalar_tensor_tensor(
                obuf[rb][:], oa2[:], ssq[0:128, 0:1], gnw[:],
                ALU.mult, ALU.mult),
                reads=[oa2_k, gnw_k], sreads=[ssq_k], writes=[obuf_k[rb]])
            dodma("sync", oB[cc * 128:(cc + 1) * 128,
                               hl * 256:(hl + 1) * 256],
                  obuf[rb][:], reads=[obuf_k[rb]])
            do("tensor", lambda e, cc=cc: e.matmul(
                ps["A"][:, 0:256], kbar[cc][:, 0:128], ubf[:],
                start=True, stop=True),
                reads=[kbar_k[cc], ubf_k], writes=[pt["A"]])
            do("tensor", lambda e, cc=cc: e.matmul(
                ps["E"][:, 0:256], kbar[cc][:, 128:256], ubf[:],
                start=True, stop=True),
                reads=[kbar_k[cc], ubf_k], writes=[pt["E"]])
            for fc, bk in ((0, "A"), (1, "E")):
                do("vector", lambda e, cc=cc, bk=bk, dst=snf[fc], src=sf[fc]:
                    e.scalar_tensor_tensor(
                        dst[:], src[:], gCbc[:, cc:cc + 1],
                        ps[bk][:, 0:256], ALU.mult, ALU.add),
                    reads=[sf_k[fc], pt[bk]], sreads=[gCbc_k], writes=[snf_k[fc]])
                do("vector", lambda e, dst=sb_[fc], src=snf[fc]: e.tensor_copy(
                    dst[:], src[:]),
                    reads=[snf_k[fc]], writes=[sb_k[fc]])
            sf, snf = snf, sf
            sf_k, snf_k = snf_k, sf_k
        S.barrier()
        for nm in list(hw.keys()):
            A.free(hw[nm])
        A.free(sf, sb_, snf, r2b, ubf, oac, oa2, osq, ssq, ssq2, gnw, obuf,
               eut, elt, tq1, tq2, tq3, ab_b)

    for _hl in range(3):
        do_head(_hl)

    for _i in range(Sched.NVQ):
        S.wait("gpsimd", (f"gpsimd_d{_i}", S.cnt[f"gpsimd_d{_i}"]))
        S.wait("sync", (f"sync_d{_i}", S.cnt[f"sync_d{_i}"]))
    with nc.allow_low_precision("bf16 intermediates are deliberate"):
        with nc.Block() as block:
            S.build(block)
    ctx.close()
    return nc


def prep_B(d, x1fm):
    maps = []
    masks = dict(
        utm_sc=(np.triu(np.ones((128, 128))) * (256.0 ** -0.5)).astype(np.float32),
        uts=np.triu(np.ones((128, 128)), 1).astype(np.float32),
        lts=np.tril(np.ones((128, 128)), -1).astype(np.float32),
        idn32=np.eye(128, dtype=np.float32),
        idnbf=np.eye(128).astype(BF16),
        ones_r32=np.ones((1, 128), np.float32),
        negones_r32=np.full((1, 128), -1.0, np.float32),
        ones_cbf=np.ones((128, 1)).astype(BF16),
        onw=d["L1_attn_o_norm_w"].reshape(1, 256).astype(np.float32),
        cpool=CPOOL,
    )
    for c in range(8):
        b, p = c // 2, c % 2
        sl = slice(768 * p, 768 * (p + 1))
        hsl = slice(3 * p, 3 * (p + 1))
        wba = np.zeros((2048, 64), np.float32)
        wba[:, 0:3] = d["L1_attn_Wb"][hsl].T
        wba[:, 32:35] = d["L1_attn_Wa"][hsl].T
        m = dict(masks)
        m.update(
            x1=np.ascontiguousarray(x1fm[b]),
            wqT=np.ascontiguousarray(d["L1_attn_Wq"][sl].T),
            wkT=np.ascontiguousarray(d["L1_attn_Wk"][sl].T),
            wvT=np.ascontiguousarray(d["L1_attn_Wv"][sl].T),
            wgT=np.ascontiguousarray(d["L1_attn_Wg"][sl].T),
            wbaT=wba,
            cwq=np.ascontiguousarray(
                d["L1_attn_conv_q"][sl].reshape(6, 128, 4).transpose(1, 0, 2)
                .reshape(128, 24)),
            cwk=np.ascontiguousarray(
                d["L1_attn_conv_k"][sl].reshape(6, 128, 4).transpose(1, 0, 2)
                .reshape(128, 24)),
            cwv=np.ascontiguousarray(
                d["L1_attn_conv_v"][sl].reshape(6, 128, 4).transpose(1, 0, 2)
                .reshape(128, 24)),
            alog=np.ascontiguousarray(d["L1_attn_A_log"][hsl].reshape(3, 1)),
            dtb=np.ascontiguousarray(d["L1_attn_dt_bias"][hsl].reshape(3, 1)),
        )
        maps.append(m)
    return maps


def run_B(d, x1fm):
    nc = build_B()
    res = run_bass_kernel_spmd(nc, prep_B(d, x1fm), core_ids=list(range(8)))
    return [np.concatenate([res.results[2 * b]["oB"],
                            res.results[2 * b + 1]["oB"]], axis=1)
            for b in range(4)]



def build_C():
    nc = bass.Bass("TRN2", target_bir_lowering=False)
    ctx = ExitStack()

    def din(name, shape, dt):
        return nc.dram_tensor(name, shape, dt, kind="ExternalInput").ap()

    x1 = din("x1", [2048, 1024], F32)
    ofm = din("ofm", [1536, 1024], BF)
    woT = din("woT", [1536, 2048], F32)
    upWT = din("upWT", [2048, 8192], F32)
    dnWT = din("dnWT", [8192, 2048], F32)
    upsc = din("upsc", [128, 64], F32)
    dnsc = din("dnsc", [128, 16], F32)
    osc = din("osc", [1, 1], F32)
    ones_c16 = din("ones_c16", [128, 1], F16)
    ones_r32 = din("ones_r32", [1, 128], F32)
    cpool = din("cpool", [128, 8], F32)
    x2out = nc.dram_tensor("x2out", [2048, 1024], F32, kind="ExternalOutput").ap()

    A = Arena(nc, 202 * 1024)
    S = Sched(nc, ctx)
    ps = {k: ctx.enter_context(nc.psum_tensor("ps" + k, [128, 512], F32))
          for k in "ABCD"}
    pt = {k: Track() for k in ps}

    def do(eng, emit, reads=(), writes=(), inc=1, sreads=()):
        after = []
        fafter = []
        for r in reads:
            after += r.raw
        for r in sreads:
            fafter += r.raw
        for w in writes:
            after += w.war
        tok = S.op(eng, emit, after=after, inc=inc, fafter=fafter)
        for r in reads:
            r.read(tok)
        for r in sreads:
            r.read(tok)
        for w in writes:
            w.write(tok)
        return tok

    def dodma(eng, out_ap, in_ap, reads=(), writes=(), **kw):
        after = []
        for r in reads:
            after += r.raw
        for w in writes:
            after += w.war
        tok = S.dma(eng, out_ap, in_ap, after=after, **kw)
        for r in reads:
            r.read(tok)
        for w in writes:
            w.write(tok)
        return tok

    def sbuf(shape, dt, name=None):
        return A.buf(shape, dt, name), Track()

    def sbufs(n, shape, dt, name):
        hs = A.bufs(n, shape, dt, name)
        return hs, [Track() for _ in hs]

    cp_t, cp_k = sbuf([128, 8], F32, "cp_t")
    dodma("sync", cp_t[:], cpool, writes=[cp_k])

    def cp(j, P=128):
        return cp_t[0:P, j:j + 1]

    upsc_t, upsc_k = sbuf([128, 64], F32, "upsc_t")
    dodma("sync", upsc_t[:], upsc, writes=[upsc_k])
    dnsc_t, dnsc_k = sbuf([128, 16], F32, "dnsc_t")
    dodma("sync", dnsc_t[:], dnsc, writes=[dnsc_k])
    osc_t, osc_k = sbuf([1, 1], F32, "osc_t")
    dodma("sync", osc_t[:], osc, writes=[osc_k])
    oc16_t, oc16_k = sbuf([128, 1], F16, "oc16")
    do("vector", lambda e: e.memset(oc16_t[:], 1.0), writes=[oc16_k])
    or32_t, or32_k = sbuf([1, 128], F32, "or32")
    dodma("sync", or32_t[:], ones_r32, writes=[or32_k])
    oscb, oscb_k = sbuf([128, 1], F32, "oscb")
    do("tensor", lambda e: e.matmul(ps["D"][:, 0:1], or32_t[:], osc_t[:],
                                    start=True, stop=True),
        reads=[or32_k, osc_k], writes=[pt["D"]])
    do("vector", lambda e: e.tensor_copy(oscb[:], ps["D"][:, 0:1]),
        reads=[pt["D"]], writes=[oscb_k])
    dn2sc, dn2sc_k = sbuf([128, 16], F32, "dn2sc")
    do("vector", lambda e: e.tensor_scalar_mul(dn2sc[:], dnsc_t[:],
                                               oscb[:, 0:1]),
        reads=[dnsc_k], sreads=[oscb_k], writes=[dn2sc_k])

    x1f, x1f_k = sbufs(16, [128, 1024], F32, "x1f")
    for fc in range(16):
        dodma("sync", x1f[fc][:], x1[fc * 128:(fc + 1) * 128, :],
              writes=[x1f_k[fc]])
    sum_row, sum_k = sbuf([1, 1024], F32, "sum_row")
    ss_row, ss_k = sbuf([1, 1024], F32, "ss_row")
    m_row, m_k = sbuf([1, 1024], F32, "m_row")
    v_row, v_k = sbuf([1, 1024], F32, "v_row")
    r_row, r_k = sbuf([1, 1024], F32, "r_row")
    sq, sq_k = sbufs(2, [128, 512], F16, "sq")
    x1h, x1h_k = sbufs(2, [128, 512], F16, "x1h")
    for nt in range(2):
        for fc in range(16):
            r2 = fc % 2
            do("vector", lambda e, fc=fc, nt=nt, r2=r2: e.tensor_copy(
                x1h[r2][:], x1f[fc][:, nt * 512:(nt + 1) * 512]),
                reads=[x1f_k[fc]], writes=[x1h_k[r2]])
            do("scalar", lambda e, r2=r2: e.activation(
                sq[r2][:], x1h[r2][:], AF.Square),
                reads=[x1h_k[r2]], writes=[sq_k[r2]])
            do("tensor", lambda e, r2=r2, st=(fc == 0), sp=(fc == 15): e.matmul(
                ps["A"][0:1, :], oc16_t[:], x1h[r2][:], start=st, stop=sp),
                reads=[x1h_k[r2], oc16_k], writes=[pt["A"]])
            do("tensor", lambda e, r2=r2, st=(fc == 0), sp=(fc == 15): e.matmul(
                ps["B"][0:1, :], oc16_t[:], sq[r2][:], start=st, stop=sp),
                reads=[sq_k[r2], oc16_k], writes=[pt["B"]])
        do("vector", lambda e, nt=nt: e.tensor_copy(
            sum_row[0:1, nt * 512:(nt + 1) * 512], ps["A"][0:1, :]),
            reads=[pt["A"]], writes=[sum_k])
        do("vector", lambda e, nt=nt: e.tensor_copy(
            ss_row[0:1, nt * 512:(nt + 1) * 512], ps["B"][0:1, :]),
            reads=[pt["B"]], writes=[ss_k])
    do("vector", lambda e: e.tensor_scalar_mul(m_row[:], sum_row[:], 1.0 / 2048),
        reads=[sum_k], writes=[m_k])
    do("vector", lambda e: e.tensor_scalar_mul(v_row[:], ss_row[:], 1.0 / 2048),
        reads=[ss_k], writes=[v_k])
    do("vector", lambda e: e.scalar_tensor_tensor(
        r_row[:], m_row[:], -1.0, m_row[:], ALU.mult, ALU.mult),
        reads=[m_k], writes=[r_k])
    do("vector", lambda e: e.tensor_add(ss_row[:], v_row[:], r_row[:]),
        reads=[v_k, r_k], writes=[ss_k])
    do("scalar", lambda e: e.activation(v_row[:], ss_row[:], AF.Sqrt,
                                        bias=cp(0, 1)),
        reads=[ss_k, cp_k], writes=[v_k])
    do("vector", lambda e: e.reciprocal(r_row[:], v_row[:]),
        reads=[v_k], writes=[r_k])
    M_bc, M_k = sbuf([128, 1024], F32, "M_bc")
    R_bc, R_k = sbuf([128, 1024], F32, "R_bc")
    for nt in range(2):
        do("tensor", lambda e, nt=nt: e.matmul(
            ps["C"][:, :], or32_t[:], m_row[0:1, nt * 512:(nt + 1) * 512],
            start=True, stop=True),
            reads=[or32_k, m_k], writes=[pt["C"]])
        do("vector", lambda e, nt=nt: e.tensor_copy(
            M_bc[:, nt * 512:(nt + 1) * 512], ps["C"][:, :]),
            reads=[pt["C"]], writes=[M_k])
        do("tensor", lambda e, nt=nt: e.matmul(
            ps["D"][:, :], or32_t[:], r_row[0:1, nt * 512:(nt + 1) * 512],
            start=True, stop=True),
            reads=[or32_k, r_k], writes=[pt["D"]])
        do("vector", lambda e, nt=nt: e.tensor_copy(
            R_bc[:, nt * 512:(nt + 1) * 512], ps["D"][:, :]),
            reads=[pt["D"]], writes=[R_k])
    xn1, xn1_k = sbufs(16, [128, 1024], BF, "xn1")
    dtmp, dtmp_k = sbufs(2, [128, 1024], F32, "dtmp")
    for fc in range(16):
        r = fc % 2
        do("vector", lambda e, fc=fc, r=r: e.tensor_tensor(
            dtmp[r][:], x1f[fc][:], M_bc[:], ALU.subtract),
            reads=[x1f_k[fc], M_k], writes=[dtmp_k[r]])
        do("vector", lambda e, fc=fc, r=r: e.tensor_tensor(
            xn1[fc][:], dtmp[r][:], R_bc[:], ALU.mult),
            reads=[dtmp_k[r], R_k], writes=[xn1_k[fc]])
    S.barrier()
    A.free(x1f, sum_row, ss_row, m_row, v_row, r_row, sq, x1h, M_bc, R_bc, dtmp)

    hid, hid_k = sbufs(64, [128, 1024], BF, "hid")
    upw_t, upw_tk = sbufs(2, [128, 2048], BF, "upw_t")
    upw_s, upw_sk = sbufs(2, [128, 2048], BF, "upw_s")
    for oc in range(64):
        r = oc % 2
        dodma("gpsimd", upw_t[r][:].rearrange("p (c m) -> p c m", m=128),
              upWT[:, oc * 128:(oc + 1) * 128].rearrange(
                  "(c p) m -> p c m", p=128), writes=[upw_tk[r]])
        do("scalar", lambda e, r=r: e.activation(
            upw_s[r][:], upw_t[r][:], AF.Sign),
            reads=[upw_tk[r]], writes=[upw_sk[r]])
        for n2 in range(2):
            for ic in range(16):
                do("tensor", lambda e, r=r, ic=ic, n2=n2, st=(ic == 0),
                   sp=(ic == 15): e.matmul(
                       ps["A"][:, :], upw_s[r][:, ic * 128:(ic + 1) * 128],
                       xn1[ic][:, n2 * 512:(n2 + 1) * 512], start=st, stop=sp),
                    reads=[xn1_k[ic], upw_sk[r]], writes=[pt["A"]])
            do("scalar", lambda e, r=r, oc=oc, n2=n2: e.activation(
                hid[oc][:, n2 * 512:(n2 + 1) * 512], ps["A"][:, :], AF.Relu,
                scale=upsc_t[:, oc:oc + 1]),
                reads=[pt["A"]], sreads=[upsc_k], writes=[hid_k[oc]])
    S.barrier()
    A.free(xn1, upw_t, upw_s)

    ofm_t, ofm_k = sbufs(12, [128, 1024], BF, "ofm_t")
    for ic in range(12):
        dodma("sync", ofm_t[ic][:], ofm[ic * 128:(ic + 1) * 128, :],
              writes=[ofm_k[ic]])
    s1h, s1h_k = sbufs(16, [128, 1024], F16, "s1h")
    wot_t, wot_k = sbufs(2, [128, 1536], BF, "wot_t")
    for oc in range(16):
        r = oc % 2
        dodma("gpsimd", wot_t[r][:].rearrange("p (c m) -> p c m", m=128),
              woT[:, oc * 128:(oc + 1) * 128].rearrange(
                  "(c p) m -> p c m", p=128), writes=[wot_k[r]])
        for n2 in range(2):
            for ic in range(12):
                do("tensor", lambda e, r=r, ic=ic, n2=n2, st=(ic == 0),
                   sp=(ic == 11): e.matmul(
                       ps["B"][:, :], wot_t[r][:, ic * 128:(ic + 1) * 128],
                       ofm_t[ic][:, n2 * 512:(n2 + 1) * 512], start=st, stop=sp),
                    reads=[ofm_k[ic], wot_k[r]], writes=[pt["B"]])
            do("vector", lambda e, oc=oc, n2=n2: e.tensor_copy(
                s1h[oc][:, n2 * 512:(n2 + 1) * 512], ps["B"][:, :]),
                reads=[pt["B"]], writes=[s1h_k[oc]])
    S.barrier()
    A.free(ofm_t, wot_t)

    dnw_t, dnw_tk = sbuf([128, 2048], BF, "dnw_t")
    dnw_s, dnw_sk = sbuf([128, 8192], BF, "dnw_s")
    g1b, g1b_k = sbufs(2, [128, 512], F32, "g1b")
    x1s, x1s_k = sbufs(2, [128, 512], F32, "x1s")
    t_b, t_k = sbufs(2, [128, 512], F32, "t_b")
    u_b, u_k = sbufs(2, [128, 512], F32, "u_b")
    for oc in range(16):
        for q4 in range(4):
            dodma("gpsimd",
                  dnw_t[:].rearrange("p (c m) -> p c m", m=128),
                  dnWT[q4 * 2048:(q4 + 1) * 2048,
                       oc * 128:(oc + 1) * 128].rearrange(
                      "(c p) m -> p c m", p=128),
                  writes=[dnw_tk])
            do("scalar", lambda e, q4=q4: e.activation(
                dnw_s[:, q4 * 2048:(q4 + 1) * 2048], dnw_t[:], AF.Sign),
                reads=[dnw_tk], writes=[dnw_sk])
        for n2 in range(2):
            b = "A" if n2 == 0 else "B"
            for ic in range(64):
                do("tensor", lambda e, b=b, ic=ic, n2=n2, st=(ic == 0),
                   sp=(ic == 63): e.matmul(
                       ps[b][:, :], dnw_s[:, ic * 128:(ic + 1) * 128],
                       hid[ic][:, n2 * 512:(n2 + 1) * 512], start=st, stop=sp),
                    reads=[hid_k[ic], dnw_sk], writes=[pt[b]])
            do("scalar", lambda e, b=b, oc=oc, n2=n2: e.activation(
                g1b[n2][:], ps[b][:, :], AF.Silu, scale=dn2sc[:, oc:oc + 1]),
                reads=[pt[b]], sreads=[dn2sc_k], writes=[g1b_k[n2]])
            dodma("sync", x1s[n2][:],
                  x1[oc * 128:(oc + 1) * 128, n2 * 512:(n2 + 1) * 512],
                  writes=[x1s_k[n2]])
            do("vector", lambda e, oc=oc, n2=n2: e.tensor_tensor(
                t_b[n2][:], s1h[oc][:, n2 * 512:(n2 + 1) * 512], g1b[n2][:],
                ALU.mult),
                reads=[s1h_k[oc], g1b_k[n2]], writes=[t_k[n2]])
            do("vector", lambda e, n2=n2: e.tensor_tensor(
                u_b[n2][:], t_b[n2][:], x1s[n2][:], ALU.add),
                reads=[t_k[n2], x1s_k[n2]], writes=[u_k[n2]])
            dodma("sync", x2out[oc * 128:(oc + 1) * 128,
                                  n2 * 512:(n2 + 1) * 512],
                  u_b[n2][:], reads=[u_k[n2]])
    for _i in range(Sched.NVQ):
        S.wait("gpsimd", (f"gpsimd_d{_i}", S.cnt[f"gpsimd_d{_i}"]))
        S.wait("sync", (f"sync_d{_i}", S.cnt[f"sync_d{_i}"]))
    with nc.allow_low_precision("fp16/bf16 intermediates are deliberate"):
        with nc.Block() as block:
            S.build(block)
    ctx.close()
    return nc


def prep_C(d, x1fm, o_tm):
    shared = dict(
        woT=np.ascontiguousarray(d["L1_attn_Wo"].T),
        upWT=np.ascontiguousarray(d["L1_ffn_up_weight"].T),
        dnWT=np.ascontiguousarray(d["L1_ffn_down_weight"].T),
        upsc=np.ascontiguousarray(d["L1_ffn_up_scale"].reshape(64, 128).T),
        dnsc=np.ascontiguousarray(d["L1_ffn_down_scale"].reshape(16, 128).T),
        osc=d["L1_ffn_output_scale"].reshape(1, 1).astype(np.float32),
        ones_c16=np.ones((128, 1), np.float16),
        ones_r32=np.ones((1, 128), np.float32),
        cpool=CPOOL,
    )
    maps = []
    for c in range(8):
        b, half = c // 2, c % 2
        t0 = half * 1024
        ofm_full = np.ascontiguousarray(o_tm[b].T)
        m = dict(shared)
        m.update(
            x1=np.ascontiguousarray(x1fm[b][:, t0:t0 + 1024]),
            ofm=np.ascontiguousarray(ofm_full[:, t0:t0 + 1024]),
        )
        maps.append(m)
    return maps


def run_C(d, x1fm, o_tm):
    nc = build_C()
    res = run_bass_kernel_spmd(nc, prep_C(d, x1fm, o_tm), core_ids=list(range(8)))
    out = np.zeros((4, 2048, 2048), np.float32)
    for c in range(8):
        b, half = c // 2, c % 2
        out[b, half * 1024:(half + 1) * 1024, :] = res.results[c]["x2out"].T
    return out


def kernel(**inputs):
    x1fm = run_A(inputs)
    oBs = run_B(inputs, x1fm)
    return run_C(inputs, x1fm, oBs)
